# version 14
# baseline (speedup 1.0000x reference)
import contextlib

import numpy as np
import ml_dtypes
import concourse.bass as bass
import concourse.mybir as mybir
import concourse.bass_utils as bass_utils
from concourse.bass_core_rust import IndirectOffsetOnAxis

F32 = mybir.dt.float32
BF16 = mybir.dt.bfloat16
U32 = mybir.dt.uint32
AF = mybir.ActivationFunctionType
ALU = mybir.AluOpType
AX = mybir.AxisListType

B, S, D = 16, 2048, 1024
NCORES = 8
NT = B * S
NTILE = NT // 128
OWN_TILES = 2 * S // 128
H, DK = 16, 64
E, CAP = 8, 2048
C_LOC = 512
DI = 4096
EPS = 1e-5
ROPE_THETA = 10000.0
DEBUG_OUTS = False
SEQ1_ONLY = False
SERIAL_ATTN = False

ENGINES = ("sync", "tensor", "vector", "scalar", "gpsimd")


class Prog:
    def __init__(self):
        self.items = {e: [] for e in ENGINES}
        self.cnt = {}

    def now(self, sem):
        return (sem, self.cnt.get(sem, 0))

    def _bump(self, sem, k):
        self.cnt[sem] = self.cnt.get(sem, 0) + k
        return (sem, self.cnt[sem])

    def on(self, eng, fn, after=(), inc="auto", label=""):
        waits = [w for w in after if w is not None]
        if eng in ("vector", "scalar", "gpsimd"):
            waits.append(self.now(eng))
            if inc == "auto":
                inc = (eng, 1)
        if inc == "auto":
            inc = None
        ev, incs = None, []
        if inc is not None:
            ev = self._bump(*inc)
            incs = [inc]
        self.items[eng].append((waits, fn, incs, label))
        return ev

    def fence(self):
        evs = [self.now(e) for e in ("tensor", "vector", "scalar", "gpsimd")]
        evs += [self.now(s) for s in list(self.cnt) if s.startswith("d_")]
        return evs

    def chain(self, eng, fns, after=(), label=""):
        ev = None
        for ix, fn in enumerate(fns):
            ev = self.on(eng, fn, after=after if ix == 0 else (),
                         label=f"{label}#{ix}")
        return ev

    def dma(self, eng, out, in_, after=(), label="", slot=None, **kw):
        assert slot is not None, label
        def fn(g):
            return g.dma_start(out=out, in_=in_, **kw)
        return self.on(eng, fn, after=after, inc=("d_" + slot, 16), label=label)

    def emit(self, nc, es):
        sems = {}
        for name in sorted(self.cnt.keys()):
            assert self.cnt[name] < 2 ** 31
            sems[name] = es.enter_context(nc.semaphore(name))
        lastwait = {e: {} for e in ENGINES}

        def run(engobj, eng):
            lw = lastwait[eng]
            for waits, fn, incs, label in self.items[eng]:
                for sem, val in waits:
                    if val <= 0 or lw.get(sem, -1) >= val:
                        continue
                    engobj.wait_ge(sems[sem], val)
                    lw[sem] = val
                inst = fn(engobj)
                for sem, k in incs:
                    assert inst is not None, label
                    inst.then_inc(sems[sem], k)

        with nc.Block() as block:
            @block.sync
            def _(e):
                run(e, "sync")

            @block.tensor
            def _(e):
                run(e, "tensor")

            @block.vector
            def _(e):
                run(e, "vector")

            @block.scalar
            def _(e):
                run(e, "scalar")

            @block.gpsimd
            def _(e):
                run(e, "gpsimd")


def build_nc():
    nc = bass.Bass("TRN2", target_bir_lowering=False, debug=False)
    es = contextlib.ExitStack()
    nc._keepalive_es = es
    p = Prog()
    SC = contextlib.ExitStack

    def din(name, shape, dt=F32):
        return nc.dram_tensor(name, shape, dt, kind="ExternalInput").ap()

    xf = din("xf", [NT, D])
    xT = din("xT", [64, 128, 8, 512])
    wqkvt = din("wqkvt", [3, 8, 128, 8, 128])
    watot = din("watot", [128, 8, D])
    gT = din("gT", [D, E])
    gamma_bc = din("gamma_bc", [128, D]); beta_bc = din("beta_bc", [128, D])
    gammacol = din("gammacol", [128, 8]); betacol = din("betacol", [128, 8])
    wit = din("wit", [E, 8, 128, 8, 512])
    wot = din("wot", [E, 8, 128, 32, 128])
    cw1T = din("cw1T", [D, 512]); cb1 = din("cb1", [128, 4])
    cw2T = din("cw2T", [128, 4]); cb2r = din("cb2r", [128, 1])
    compwT = din("compwT", [128, 8]); compbr = din("compbr", [128, 1])
    ident = din("ident", [128, 128]); identb = din("identb", [128, 128], BF16)
    lstrictT = din("lstrictT", [128, 128])
    onesrow = din("onesrow", [1, 128]); onescol = din("onescol", [128, 1])
    iota8 = din("iota8", [128, 8])
    trimask = din("trimask", [128, 128], BF16)
    ropecos = din("ropecos", [128, S]); ropesin = din("ropesin", [128, S])
    ropePT = din("ropePT", [128, 128], BF16)
    signcol = din("signcol", [128, 1])
    bigiota = din("bigiota", [128, OWN_TILES])
    ownrow_u = nc.dram_tensor("ownrow_u", [128, OWN_TILES], U32,
                              kind="ExternalInput").ap()

    y = nc.dram_tensor("y", [2 * S, D], F32, kind="ExternalOutput").ap()
    _dbg = "ExternalOutput" if DEBUG_OUTS else "Internal"
    dbg_cat = (nc.dram_tensor("dbg_cat", [2, 128, 8 * S], BF16, kind=_dbg).ap()
               if DEBUG_OUTS else None)
    dbg_xnT = (nc.dram_tensor("dbg_xnT", [2, 128, 8 * S], BF16, kind=_dbg).ap()
               if DEBUG_OUTS else None)
    dbg_qk = (nc.dram_tensor("dbg_qk", [2, 128, S], BF16, kind=_dbg).ap()
              if DEBUG_OUTS else None)
    dbg_PT = (nc.dram_tensor("dbg_PT", [128, 16 * 512], BF16, kind=_dbg).ap()
              if DEBUG_OUTS else None)
    dbg_xn2 = (nc.dram_tensor("dbg_xn2", [2, 128, OWN_TILES * D], BF16,
                              kind=_dbg).ap() if DEBUG_OUTS else None)
    dbg_idx = (nc.dram_tensor("dbg_idx", [128, 2 * OWN_TILES], U32,
                              kind=_dbg).ap() if DEBUG_OUTS else None)
    dbg_w = (nc.dram_tensor("dbg_w", [128, OWN_TILES], F32, kind=_dbg).ap()
             if DEBUG_OUTS else None)
    disp = nc.dram_tensor("disp", [E * C_LOC, D], BF16, kind=_dbg).ap()
    eo = nc.dram_tensor("eo", [E * C_LOC, D], F32, kind=_dbg).ap()
    meta_d = nc.dram_tensor("meta_d", [NT, 8], F32, kind=_dbg).ap()
    wqkv_bf = nc.dram_tensor("wqkv_bf", [3, 8, 128, 8, 128], BF16,
                             kind="Internal").ap()
    wato_bf = nc.dram_tensor("wato_bf", [128, 8, D], BF16,
                             kind="Internal").ap()

    sb = lambda n, sh, dt=F32: es.enter_context(nc.sbuf_tensor(n, sh, dt))

    c23 = SC()
    pre3 = SC()
    _sp_reqs = []

    def sp(n, sh, dt=F32):
        _sp_reqs.append((n, sh, dt))
        return ("SPREQ", len(_sp_reqs) - 1)
    s_id = sb("c_id", [128, 128]); s_idb = sb("c_idb", [128, 128], BF16)
    s_L = sp("c_L", [128, 128])
    s_or = sb("c_or", [1, 128]); s_oc = sb("c_oc", [128, 1])
    s_io8 = sp("c_io8", [128, 8]); s_tri = sb("c_tri", [128, 128], BF16)
    s_cos = sb("c_cos", [128, S]); s_sin = sb("c_sin", [128, S])
    s_rpt = sb("c_rpt", [128, 128], BF16)
    s_sgn = sb("c_sgn", [128, 1])
    s_gbc = sp("c_gbc", [128, D]); s_bbc = sp("c_bbc", [128, D])
    s_gTw = sp("c_gTw", [128, 8, 8])
    s_gcol = sp("c_gcol", [128, 8]); s_bcol = sp("c_bcol", [128, 8])
    s_g9 = sp("c_g9", [128, 8, 9])
    s_Cbc = sp("c_Cbc", [128, 8]); s_Bgbc = sp("c_Bgbc", [128, 8])
    s_cb2 = sp("c_cb2", [128, 1]); s_compb = sp("c_compb", [128, 1])
    s_cw2b = sp("c_cw2b", [128, 4], BF16)
    s_compwb = sp("c_compwb", [128, 8], BF16)
    s_cb1 = sp("c_cb1", [128, 4])
    s_bigio = sp("c_bigio", [128, OWN_TILES])
    s_ownrow = es.enter_context(nc.sbuf_tensor("ownrow", [128, OWN_TILES], U32))
    s_gidx = es.enter_context(nc.sbuf_tensor("gidx", [128, OWN_TILES], U32))
    s_sidx = es.enter_context(nc.sbuf_tensor("sidx", [128, OWN_TILES], U32))
    s_wown = sb("wown", [128, OWN_TILES])
    s_r1own = sb("r1own", [128, OWN_TILES])
    s_nm1r1 = sb("nm1r1own", [128, OWN_TILES])
    s_acc = sp("accs", [1, 8])
    s_iok = sp("c_iok", [128, 8])
    s_m1000 = sp("c_m1000", [128, 1])
    s_acc2 = sp("accs2", [1, 8])
    s_confacc = sp("confacc", [128, 1]); s_compacc = sp("compacc", [128, 1])
    s_zero = sp("zerot", [128, D], BF16)
    s_eps = sp("epst", [128, 1])

    s_xnown = c23.enter_context(nc.sbuf_tensor("xnown", [128, OWN_TILES, D], BF16))
    _sp_t = [pre3.enter_context(nc.sbuf_tensor(n_, sh_, dt_))
             for (n_, sh_, dt_) in _sp_reqs]

    def _resolve(h):
        return _sp_t[h[1]] if isinstance(h, tuple) and len(h) == 2 and h[0] == "SPREQ" else h
    s_L = _resolve(s_L); s_io8 = _resolve(s_io8)
    s_gbc = _resolve(s_gbc); s_bbc = _resolve(s_bbc)
    s_gTw = _resolve(s_gTw); s_gcol = _resolve(s_gcol); s_bcol = _resolve(s_bcol)
    s_g9 = _resolve(s_g9); s_Cbc = _resolve(s_Cbc); s_Bgbc = _resolve(s_Bgbc)
    s_cb2 = _resolve(s_cb2); s_compb = _resolve(s_compb)
    s_cw2b = _resolve(s_cw2b); s_compwb = _resolve(s_compwb)
    s_cb1 = _resolve(s_cb1); s_bigio = _resolve(s_bigio)
    s_acc = _resolve(s_acc); s_acc2 = _resolve(s_acc2)
    s_iok = _resolve(s_iok); s_m1000 = _resolve(s_m1000)
    s_confacc = _resolve(s_confacc); s_compacc = _resolve(s_compacc)
    s_zero = _resolve(s_zero)
    s_eps = _resolve(s_eps)
    evc = []
    for dst, src in [(s_id, ident), (s_idb, identb), (s_L, lstrictT),
                     (s_or, onesrow), (s_oc, onescol), (s_io8, iota8),
                     (s_tri, trimask), (s_cos, ropecos), (s_sin, ropesin),
                     (s_rpt, ropePT), (s_sgn, signcol),
                     (s_gbc, gamma_bc), (s_bbc, beta_bc),
                     (s_cb2, cb2r), (s_compb, compbr), (s_bigio, bigiota),
                     (s_ownrow, ownrow_u)]:
        evc.append(p.dma("sync", dst[:], src[:, :], slot="const"))
    evc.append(p.dma("sync", s_gTw[:], gT.rearrange("(c q) n -> q c n", q=128),
                     slot="const"))
    for dst, srcap in [(s_gcol, gammacol), (s_bcol, betacol), (s_cb1, cb1)]:
        evc.append(p.dma("sync", dst[:], srcap[:, :], slot="const"))
    for dst, srcap in [(s_cw2b, cw2T), (s_compwb, compwT)]:
        evc.append(p.dma("gpsimd", dst[:], srcap[:, :], slot="constg"))
    evc = [evc[-3], evc[-1]]

    def mkg9(v):
        for ch in range(8):
            v.tensor_scalar(s_g9[:, ch, 0:8], s_gTw[:, ch], s_gcol[:, ch:ch + 1],
                            None, ALU.mult)
        return v.memset(s_g9[:, :, 8:9], 1.0)

    ev_g9 = p.on("vector", mkg9, after=evc)
    ev_zt = p.on("vector", lambda v: v.memset(s_zero[:, :], 0.0))
    ev_eps = p.on("vector", lambda v: v.memset(s_eps[:, :], EPS))
    ev_iok = p.on("vector",
                  lambda v: v.tensor_scalar(s_iok[:, :], s_io8[:, :], 1000.0,
                                            None, ALU.add), after=evc)
    ev_m1000 = p.on("vector", lambda v: v.memset(s_m1000[:, :], -1000.0))
    ev_acc0 = p.on("vector", lambda v: v.memset(s_acc[:, :], 0.0))
    ev_acc20 = p.on("vector", lambda v: v.memset(s_acc2[:, :], 0.0))
    ev_cf0 = p.on("vector", lambda v: v.memset(s_confacc[:, :], 0.0))
    ev_cp0 = p.on("vector", lambda v: v.memset(s_compacc[:, :], 0.0))

    with SC() as c0:
        p_r8 = c0.enter_context(nc.psum_tensor("p_r8", [1, 8], F32))
        p_b8 = c0.enter_context(nc.psum_tensor("p_b8", [128, 8], F32))

        def mmC(t):
            for ch in range(8):
                mm = t.matmul(p_r8[:, :], s_oc[:, :], s_g9[:, ch, 0:8],
                              start=(ch == 0), stop=(ch == 7))
            return mm
        e1 = p.on("tensor", mmC, after=[ev_g9] + evc, inc=("tensor", 1))
        e2 = p.on("vector", lambda v: v.tensor_copy(s_Cbc[0:1, :], p_r8[:, :]),
                  after=[e1])
        e3 = p.on("tensor",
                  lambda t: t.matmul(p_b8[:, :], s_or[0:1, :], s_Cbc[0:1, :],
                                     start=True, stop=True),
                  after=[e2], inc=("tensor", 1))
        e4 = p.on("vector", lambda v: v.tensor_copy(s_Cbc[:, :], p_b8[:, :]),
                  after=[e3])

        def mmBg(t):
            for ch in range(8):
                mm = t.matmul(p_r8[:, :], s_bcol[:, ch:ch + 1], s_gTw[:, ch],
                              start=(ch == 0), stop=(ch == 7))
            return mm
        e5 = p.on("tensor", mmBg, after=[e4] + evc, inc=("tensor", 1))
        e6 = p.on("vector", lambda v: v.tensor_copy(s_Bgbc[0:1, :], p_r8[:, :]),
                  after=[e5])
        e7 = p.on("tensor",
                  lambda t: t.matmul(p_b8[:, :], s_or[0:1, :], s_Bgbc[0:1, :],
                                     start=True, stop=True),
                  after=[e6], inc=("tensor", 1))
        ev_gate = p.on("vector", lambda v: v.tensor_copy(s_Bgbc[:, :], p_b8[:, :]),
                       after=[e7])

    ev_dz = []
    for r0 in range(0, E * C_LOC, 128):
        ev_dz.append(p.dma("gpsimd", disp[r0:r0 + 128, :], s_zero[:, :],
                           after=[ev_zt], label="dispz", slot="dz"))
    ev_dz_last = ev_dz[-1]

    ev_pc_qkv = []
    for wi_ in range(3):
        ev_pc_qkv.append(p.dma("gpsimd", wqkv_bf[wi_], wqkvt[wi_],
                               label=f"pcq{wi_}", slot=f"pq{wi_}"))
    ev_pc_qkv.append(p.dma("gpsimd", wato_bf, watot,
                           label="pcq3", slot="pq3"))

    NQ = NTILE // 4
    with SC() as c1:
        sb1 = lambda n, sh, dt=F32: c1.enter_context(nc.sbuf_tensor(n, sh, dt))
        s_xq = sb1("sc_x", [128, 2, 4, D])
        s_xTq = sb1("sc_xT", [128, 2, 8, 512])
        s_jk = c1.enter_context(nc.sbuf_tensor("sc_jk", [128, 2, D], BF16))
        s_gTq = sb1("sc_gT", [9, 2, 512])
        s_Gq = sb1("sc_G", [128, 4, 40])
        s_m8 = sb1("sc_m8", [128, 4, 32])
        s_q = sb1("sc_q", [128, 4, 24])
        s_lgq = sb1("sc_lg", [128, 4, 32])
        s_ohq = sb1("sc_oh", [128, 4, 32])
        s_r8q = sb1("sc_r8", [128, 4, 32])
        s_mp = sb1("sc_mp", [128, 2, 4, 8])
        s_tot = sb1("sc_tot", [1, 2, 32])
        s_off = sb1("sc_off", [1, 2, 4, 8])
        s_acc2d = sb1("sc_acc2d", [1, 2, 8])
        p_gTq = c1.enter_context(nc.psum_tensor("p_gT", [9, 2, 512], F32))
        p_Gq = c1.enter_context(nc.psum_tensor("p_G", [128, 2, 40], F32))
        p_rank = c1.enter_context(nc.psum_tensor("p_rank", [128, 4, 8], F32))
        p_tot = c1.enter_context(nc.psum_tensor("p_tot", [1, 32], F32))

        ev_z1 = p.on("vector", lambda v: v.memset(s_m8[:, :, 24:32], 0.0))
        ev_acc0q = p.on("vector", lambda v: v.memset(s_acc2d[:, :], 0.0))
        ev_gpp = p.chain("vector", [
            lambda v: v.tensor_scalar(s_Cbc[:, :], s_Cbc[:, :], 1.0 / D, None,
                                      ALU.mult)] + [
            (lambda v, ch=ch: v.tensor_tensor(s_g9[:, ch, 0:8], s_g9[:, ch, 0:8],
                                              s_Cbc[:, :], ALU.subtract))
            for ch in range(8)],
            after=[ev_g9, ev_gate], label="gpp")

        ev_sq = {}
        ev_xld = {}
        ev_xTld = {}
        ev_gmm = {}
        ev_gevac = {}
        ev_gtr = {}
        ev_Gev = {}
        ev_snap = {-1: ev_acc0}
        ev_meta = {}
        ev_st1 = {}
        ev_r1 = {}
        ev_st2 = {}
        ev_r2 = {}
        ev_lgam = {}
        ev_e8 = {}
        ev_pack = {}

        def S0(q):
            b2 = q % 2
            ev_xld[q] = p.dma("sync", s_xq[:, b2],
                              xf[q * 512:(q + 1) * 512, :]
                              .rearrange("(a p) n -> p a n", p=128),
                              after=[ev_sq.get(q - 2)], label=f"xld{q}",
                              slot=f"x{b2}")
            ev_xTld[q] = p.dma(
                "sync", s_xTq[:, b2], xT[q],
                after=[ev_gmm.get(q - 2)],
                label=f"xTld{q}", slot=f"xT{b2}")

            def gmm(t, b2=b2):
                for ch in range(8):
                    mm = t.matmul(p_gTq[:, b2], s_g9[:, ch], s_xTq[:, b2, ch],
                                  start=(ch == 0), stop=(ch == 7))
                return mm
            ev_gmm[q] = p.on("tensor", gmm,
                             after=[ev_xTld[q], ev_gpp, ev_gevac.get(q - 2)],
                             inc=("tensor", 1), label=f"gmm{q}")
            ev_gevac[q] = p.on(
                "vector",
                lambda v, b2=b2: v.tensor_copy(s_gTq[:, b2], p_gTq[:, b2]),
                after=[ev_gmm[q], ev_gtr.get(q - 2)], label=f"gev{q}")

            def gtr(t, b2=b2):
                for k in range(4):
                    mm = t.transpose(p_Gq[:, b2, 10 * k:10 * k + 9],
                                     s_gTq[:, b2, 128 * k:128 * k + 128],
                                     s_id[0:9, 0:9])
                return mm
            ev_gtr[q] = p.on("tensor", gtr,
                             after=[ev_gevac[q], ev_Gev.get(q - 2)],
                             inc=("tensor", 1), label=f"gtr{q}")

            def _gev(s, q=q, b2=b2):
                for k in range(4):
                    cp = s.activation(out=s_Gq[:, q % 4, 10 * k:10 * k + 9],
                                      in_=p_Gq[:, b2, 10 * k:10 * k + 9],
                                      func=AF.Copy, bias=0.0)
                return cp
            ev_Gev[q] = p.on("scalar", _gev,
                             after=[ev_gtr[q], ev_lgam.get(q - 4)],
                             label=f"Gev{q}")

        def S1(q):
            b2 = q % 2
            ev = None
            for k in range(4):
                ev = p.on("scalar",
                          lambda s, q=q, b2=b2, k=k: s.activation(
                              out=s_jk[:, k % 2], in_=s_xq[:, b2, k],
                              func=AF.Square,
                              accum_out=s_q[:, q % 4, k:k + 1]),
                          after=[ev_xld[q], ev_meta.get(q - 4)],
                          label=f"sq{q}_{k}")
            ev_sq[q] = ev

        def S2(q):
            b4 = q % 4
            qq, m8 = s_q[:, b4], s_m8[:, b4]
            G4 = s_Gq[:, b4]
            Gs = G4.rearrange("p (a b) -> p a b", b=10)[:, :, 8:9]
            m13 = m8[:, 0:4].rearrange("p (a b) -> p a b", b=1)
            ev_st1[q] = p.chain("vector", [
                lambda v: v.tensor_scalar(m13, Gs, 1.0 / D, None, ALU.mult),
                lambda v: v.tensor_tensor(qq[:, 4:8], m8[:, 0:4], m8[:, 0:4],
                                          ALU.mult),
                lambda v: v.tensor_scalar(qq[:, 8:12], qq[:, 0:4], 1.0 / D,
                                          None, ALU.mult),
                lambda v: v.tensor_tensor(qq[:, 8:12], qq[:, 8:12], qq[:, 4:8],
                                          ALU.subtract),
            ], after=[ev_sq[q], ev_Gev[q], ev_meta.get(q - 4), ev_z1],
                label=f"st1_{q}")

        def S3(q):
            b4 = q % 4
            p.on("scalar",
                 lambda s, b4=b4: s.activation(
                     out=s_q[:, b4, 4:8], in_=s_q[:, b4, 8:12],
                     func=AF.Ln, bias=s_eps[:, 0:1]),
                 after=[ev_st1[q], ev_eps], label=f"ln1_{q}")
            ev_r1[q] = p.on("scalar",
                            lambda s, b4=b4: s.activation(
                                out=s_m8[:, b4, 4:8], in_=s_q[:, b4, 4:8],
                                func=AF.Exp, scale=-0.5),
                            after=[], label=f"r1_{q}")

        def S4(q):
            b4 = q % 4
            qq, m8 = s_q[:, b4], s_m8[:, b4]
            ev_st2[q] = p.chain("vector", [
                lambda v: v.tensor_tensor(qq[:, 12:16], m8[:, 4:8], m8[:, 4:8],
                                          ALU.mult),
                lambda v: v.tensor_tensor(qq[:, 12:16], qq[:, 8:12],
                                          qq[:, 12:16], ALU.mult),
            ], after=[ev_r1[q]], label=f"st2_{q}")

        def S5(q):
            b4 = q % 4
            p.on("scalar",
                 lambda s, b4=b4: s.activation(
                     out=s_q[:, b4, 16:20], in_=s_q[:, b4, 12:16],
                     func=AF.Ln, bias=s_eps[:, 0:1]),
                 after=[ev_st2[q]], label=f"ln2_{q}")
            ev_r2[q] = p.on("scalar",
                            lambda s, b4=b4: s.activation(
                                out=s_q[:, b4, 16:20], in_=s_q[:, b4, 16:20],
                                func=AF.Exp, scale=-0.5),
                            after=[], label=f"r2_{q}")

        def S6(q):
            b4 = q % 4
            qq, m8 = s_q[:, b4], s_m8[:, b4]
            lg, oh, r8 = s_lgq[:, b4], s_ohq[:, b4], s_r8q[:, b4]
            lg3 = s_lgq[:, b4].rearrange("p (a b) -> p a b", b=8)
            oh3 = s_ohq[:, b4].rearrange("p (a b) -> p a b", b=8)
            r83 = s_r8q[:, b4].rearrange("p (a b) -> p a b", b=8)
            G3 = s_Gq[:, b4].rearrange("p (a b) -> p a b", b=10)[:, :, 0:8]
            al_b = m8[:, 8:12].rearrange("p (a b) -> p a b", b=1)\
                .broadcast_to([128, 4, 8])
            mx_b = qq[:, 20:24].rearrange("p (a b) -> p a b", b=1)\
                .broadcast_to([128, 4, 8])
            e_b = m8[:, 16:20].rearrange("p (a b) -> p a b", b=1)\
                .broadcast_to([128, 4, 8])
            Bg_b = s_Bgbc[:, :].rearrange("p (a b) -> p a b", a=1)\
                .broadcast_to([128, 4, 8])
            iok_b = s_iok[:, :].rearrange("p (a b) -> p a b", a=1)\
                .broadcast_to([128, 4, 8])
            ev_lgam[q] = p.chain("vector", [
                lambda v: v.tensor_tensor(m8[:, 8:12], m8[:, 4:8], qq[:, 16:20],
                                          ALU.mult),
                lambda v: v.tensor_tensor(lg3, G3, al_b, ALU.mult),
                lambda v: v.tensor_tensor(lg3, lg3, Bg_b, ALU.add),
                lambda v: v.tensor_reduce(qq[:, 20:24], lg3, AX.X, ALU.max),
                lambda v: v.tensor_tensor(oh3, lg3, mx_b, ALU.is_equal),
                lambda v: v.scalar_tensor_tensor(
                    oh3, oh3, s_m1000[:, 0:1], iok_b, ALU.mult, ALU.add),
                lambda v: v.tensor_reduce(m8[:, 16:20], oh3, AX.X, ALU.min),
                lambda v: v.tensor_tensor(oh3, oh3, e_b, ALU.is_equal),
                lambda v: v.tensor_tensor(r83, lg3, mx_b, ALU.subtract),
            ], after=[ev_r2[q], ev_gpp], label=f"lgam{q}")

        def S7(q):
            b4 = q % 4
            ev_e8[q] = p.on("scalar",
                            lambda s, b4=b4: s.activation(
                                out=s_r8q[:, b4], in_=s_r8q[:, b4],
                                func=AF.Exp),
                            after=[ev_lgam[q]], label=f"exp8{q}")

        def S8(q):
            b4 = q % 4
            b2 = q % 2
            qa = q % 2
            qq, m8 = s_q[:, b4], s_m8[:, b4]
            r83 = s_r8q[:, b4].rearrange("p (a b) -> p a b", b=8)
            ev8 = p.chain("vector", [
                lambda v: v.tensor_reduce(qq[:, 0:4], r83, AX.X, ALU.add),
                lambda v: v.reciprocal(qq[:, 4:8], qq[:, 0:4]),
            ], after=[ev_e8[q]], label=f"wr{q}")

            def totmm(t, b4=b4):
                for k in range(4):
                    mm = t.matmul(p_tot[:, 8 * k:8 * k + 8], s_oc[:, :],
                                  s_ohq[:, b4, 8 * k:8 * k + 8],
                                  start=True, stop=True)
                return mm
            evtm = p.on("tensor", totmm, after=[ev_lgam[q], ev_snap[q - 1]],
                        inc=("tensor", 1), label=f"tot{q}")

            def rmm0(t, b4=b4, qa=qa):
                t.matmul(p_rank[:, 0], s_or[0:1, :], s_acc2d[0:1, qa],
                         start=True, stop=False)
                return t.matmul(p_rank[:, 0], s_L[:, :],
                                s_ohq[:, b4, 0:8],
                                start=False, stop=True)
            evr0 = p.on("tensor", rmm0,
                        after=[ev_lgam[q], ev_snap[q - 1], ev_pack.get(q - 1)],
                        inc=("tensor", 1), label=f"rmm0{q}")

            evte = p.on("scalar",
                        lambda s, b2=b2: s.activation(
                            out=s_tot[:, b2], in_=p_tot[:, :],
                            func=AF.Copy, bias=0.0),
                        after=[evtm], label=f"tote{q}")
            evoff = p.chain("vector", [
                lambda v, b2=b2, qa=qa: v.tensor_tensor(
                    s_off[:, b2, 0], s_acc2d[0:1, qa], s_tot[:, b2, 0:8],
                    ALU.add),
                lambda v, b2=b2: v.tensor_tensor(
                    s_off[:, b2, 1], s_off[:, b2, 0], s_tot[:, b2, 8:16],
                    ALU.add),
                lambda v, b2=b2: v.tensor_tensor(
                    s_off[:, b2, 2], s_off[:, b2, 1], s_tot[:, b2, 16:24],
                    ALU.add),
                lambda v, b2=b2, qa=qa: v.tensor_tensor(
                    s_acc2d[0:1, 1 - qa], s_off[0:1, b2, 2],
                    s_tot[:, b2, 24:32], ALU.add),
            ], after=[evte, ev_acc0q], label=f"off{q}")
            ev_snap[q] = evoff

            def rmm(t, b4=b4, b2=b2):
                for k in range(1, 4):
                    t.matmul(p_rank[:, k], s_or[0:1, :],
                             s_off[0:1, b2, k - 1],
                             start=True, stop=False)
                    mm = t.matmul(p_rank[:, k], s_L[:, :],
                                  s_ohq[:, b4, 8 * k:8 * k + 8],
                                  start=False, stop=True)
                return mm
            evr = p.on("tensor", rmm, after=[evoff], inc=("tensor", 1),
                       label=f"rmm{q}")
            oh3 = s_ohq[:, b4].rearrange("p (a b) -> p a b", b=8)
            keep_b = qq[:, 8:12]
            evx = p.chain("vector", [
                lambda v: v.tensor_tensor(
                    s_r8q[:, b4].rearrange("p (a b) -> p a b", b=8),
                    p_rank[:, :, :], oh3, ALU.mult),
                lambda v: v.tensor_reduce(
                    m8[:, 20:24],
                    s_r8q[:, b4].rearrange("p (a b) -> p a b", b=8),
                    AX.X, ALU.add),
                lambda v: v.tensor_scalar(keep_b, m8[:, 20:24], float(CAP),
                                          None, ALU.is_lt),
                lambda v: v.tensor_tensor(m8[:, 12:16], qq[:, 4:8], keep_b,
                                          ALU.mult),
            ], after=[evr, evr0, ev8], label=f"rx{q}")
            evpk = p.on("scalar",
                        lambda s, b2=b2, b4=b4: s.activation(
                            out=s_mp[:, b2],
                            in_=s_m8[:, b4].rearrange("p (c k) -> p k c", k=4),
                            func=AF.Copy, bias=0.0),
                        after=[evx, ev_meta.get(q - 2)], label=f"pk{q}")
            ev_pack[q] = evpk
            ev_meta[q] = p.dma("gpsimd",
                               meta_d[q * 512:(q + 1) * 512, :]
                               .rearrange("(k p) b -> p k b", p=128),
                               s_mp[:, b2],
                               after=[evpk], label=f"meta{q}",
                               slot=f"meta{q % 2}")

        stages = [S0, S1, S2, S3, S4, S5, S6, S7, S8]
        SKEW = [0, 1, 1, 2, 2, 3, 3, 4, 4]
        for step in range(NQ + 5):
            for st_fn, off in zip(stages, SKEW):
                qi = step - off
                if 0 <= qi < NQ:
                    st_fn(qi)

        ev_meta_all = [ev_meta[NQ - 2], ev_meta[NQ - 1]]

    with SC() as c2:
        s_cw1b = c2.enter_context(nc.sbuf_tensor("c_cw1b", [128, 8, 512], BF16))
        ev_cw1 = p.dma("gpsimd", s_cw1b[:], cw1T.rearrange("(c q) n -> q c n", q=128), slot="cw1")
        sb2 = lambda n, sh, dt=F32: c2.enter_context(nc.sbuf_tensor(n, sh, dt))
        s_ox = sb2("ow_x", [128, 2, D])
        s_om = sb2("ow_m", [128, 2, 8])
        s_u = sb2("ow_u", [128, D])
        s_t2b = c2.enter_context(nc.sbuf_tensor("ow_t2", [128, 2, D], BF16))
        s_t2T = c2.enter_context(nc.sbuf_tensor("ow_t2T", [128, 8, 128], BF16))
        s_hdnT = c2.enter_context(nc.sbuf_tensor("ow_hdn", [128, 4, 128], BF16))
        s_sc2 = sb2("ow_sc", [128, 16])
        s_oh2 = sb2("ow_oh", [128, 8])
        s_r82 = sb2("ow_r8", [128, 8])
        p_t2T = c2.enter_context(nc.psum_tensor("p_t2T", [128, 128], BF16))
        p_cf = c2.enter_context(nc.psum_tensor("p_cf", [128, 128], F32))
        p_c2 = c2.enter_context(nc.psum_tensor("p_c2", [128, 1], F32))
        p_cp = c2.enter_context(nc.psum_tensor("p_cp", [128, 1], F32))
        p_rank2 = c2.enter_context(nc.psum_tensor("p_rank2", [128, 8], F32))
        p_acc2p = c2.enter_context(nc.psum_tensor("p_acc2p", [1, 8], F32))

        ev_snap2 = {-1: ev_acc20}
        ev_xn_prev = {}
        fence2 = p.fence()
        ev_scat = {}
        ev_u_by_j = {}
        ev_si_by_j = {}
        ev_t2_by_j = {}
        prev_t2T_evac = None
        ev_conf_prev = None
        ev_comp_prev = None
        ev_sg_prev = None
        ev_sp_prev = None
        eva2_prev = None
        for j in range(OWN_TILES):
            b2 = j % 2
            evx_ = p.on("gpsimd",
                        lambda g, b2=b2, j=j: g.indirect_dma_start(
                            out=s_ox[:, b2], out_offset=None, in_=xf[:, :],
                            in_offset=IndirectOffsetOnAxis(s_ownrow[:, j:j + 1], 0)),
                        after=[ev_u_by_j.get(j - 2), ev_xn_prev.get(j - 2)]
                        + evc + fence2,
                        inc=("d_ox%d" % b2, 16), label=f"oxg{j}")
            evm_ = p.on("gpsimd",
                        lambda g, b2=b2, j=j: g.indirect_dma_start(
                            out=s_om[:, b2], out_offset=None, in_=meta_d[:, :],
                            in_offset=IndirectOffsetOnAxis(s_ownrow[:, j:j + 1], 0)),
                        after=ev_meta_all + [ev_si_by_j.get(j - 2)] + fence2,
                        inc=("d_om%d" % b2, 16), label=f"omg{j}")

            om_ = s_om[:, b2]
            evp = p.chain("vector", [
                lambda v, j=j, om=om_: v.tensor_scalar(
                    s_nm1r1[:, j:j + 1], om[:, 0:1], -1.0, None, ALU.mult),
                lambda v, j=j, om=om_: v.tensor_copy(
                    s_r1own[:, j:j + 1], om[:, 1:2]),
            ], after=[evm_, ev_u_by_j.get(j - 1)], label=f"prep{j}")

            ev_xn = p.on("vector",
                         lambda v, j=j, b2=b2: v.tensor_scalar(
                             s_xnown[:, j], s_ox[:, b2],
                             s_nm1r1[:, j:j + 1], s_om[:, b2, 1:2],
                             ALU.add, ALU.mult),
                         after=[evx_, evp], label=f"xn{j}")
            ev_xn_prev[j] = ev_xn
            ev_u = p.on("vector",
                        lambda v, j=j, b2=b2: v.tensor_scalar(
                            s_u[:, :], s_ox[:, b2],
                            s_nm1r1[:, j:j + 1], s_om[:, b2, 2:3],
                            ALU.add, ALU.mult),
                        after=[evx_, evp, ev_t2_by_j.get(j - 1)], label=f"u{j}")
            ev_u_by_j[j] = ev_u

            ev_t2 = p.chain("vector", [
                lambda v: v.tensor_tensor(s_u[:, :], s_u[:, :], s_gbc[:, :],
                                          ALU.mult),
                lambda v, b2=b2: v.tensor_tensor(s_t2b[:, b2], s_u[:, :],
                                                 s_bbc[:, :], ALU.add),
            ], after=[ev_u, ev_scat.get(j - 2)], label=f"t2{j}")
            ev_t2_by_j[j] = ev_t2

            tr_evs = []
            for ch in range(8):
                evtr = p.on("tensor",
                            lambda t, b2=b2, ch=ch: t.transpose(
                                p_t2T[:, :], s_t2b[:, b2, ch * 128:(ch + 1) * 128],
                                s_idb[:, :]),
                            after=[ev_t2, tr_evs[-1] if tr_evs else prev_t2T_evac],
                            inc=("tensor", 1), label=f"t2T{j}_{ch}")
                evev = p.on("vector",
                            lambda v, ch=ch: v.tensor_copy(s_t2T[:, ch],
                                                           p_t2T[:, :]),
                            after=[evtr], label=f"t2Te{j}_{ch}")
                tr_evs.append(evev)
            prev_t2T_evac = tr_evs[-1]
            t2T_done = tr_evs[-1]

            cf_evs = []
            for hc in range(4):
                def cfm(t, hc=hc):
                    for ch in range(8):
                        mm = t.matmul(p_cf[:, :],
                                      s_cw1b[:, ch, hc * 128:(hc + 1) * 128],
                                      s_t2T[:, ch], start=(ch == 0), stop=(ch == 7))
                    return mm
                evcf = p.on("tensor", cfm,
                            after=[t2T_done, ev_cw1,
                                   cf_evs[-1] if cf_evs else ev_conf_prev],
                            inc=("tensor", 1), label=f"cf{j}_{hc}")
                evrl = p.on("scalar",
                            lambda s, hc=hc: s.activation(
                                out=s_hdnT[:, hc], in_=p_cf[:, :], func=AF.Relu,
                                bias=s_cb1[:, hc:hc + 1]),
                            after=[evcf], label=f"rl{j}_{hc}")
                cf_evs.append(evrl)

            def cf2(t):
                for hc in range(4):
                    mm = t.matmul(p_c2[:, :], s_hdnT[:, hc], s_cw2b[:, hc:hc + 1],
                                  start=(hc == 0), stop=(hc == 3))
                return mm
            evc2 = p.on("tensor", cf2, after=cf_evs + [ev_sg_prev],
                        inc=("tensor", 1), label=f"cf2{j}")
            evsg = p.on("scalar",
                        lambda s: s.activation(out=s_sc2[:, 2:3], in_=p_c2[:, :],
                                               func=AF.Sigmoid,
                                               bias=s_cb2[:, 0:1]),
                        after=[evc2, ev_conf_prev], label=f"sg{j}")
            ev_sg_prev = evsg
            ev_conf_prev = p.on(
                "vector",
                lambda v: v.tensor_tensor(s_confacc[:, :], s_confacc[:, :],
                                          s_sc2[:, 2:3], ALU.add),
                after=[evsg, ev_cf0], label=f"cfa{j}")

            def cpm(t):
                for ch in range(8):
                    mm = t.matmul(p_cp[:, :], s_t2T[:, ch], s_compwb[:, ch:ch + 1],
                                  start=(ch == 0), stop=(ch == 7))
                return mm
            evcp = p.on("tensor", cpm, after=[t2T_done, ev_sp_prev],
                        inc=("tensor", 1), label=f"cp{j}")
            evsp = p.on("scalar",
                        lambda s: s.activation(out=s_sc2[:, 3:4], in_=p_cp[:, :],
                                               func=AF.Sigmoid,
                                               bias=s_compb[:, 0:1]),
                        after=[evcp, ev_comp_prev], label=f"sgp{j}")
            ev_sp_prev = evsp
            ev_comp_prev = p.on(
                "vector",
                lambda v: v.tensor_tensor(s_compacc[:, :], s_compacc[:, :],
                                          s_sc2[:, 3:4], ALU.add),
                after=[evsp, ev_cp0], label=f"cpa{j}")

            evrt = p.chain("vector", [
                lambda v, om=om_: v.tensor_scalar(
                    s_sc2[:, 4:5], om[:, 5:6], float(CAP), None, ALU.is_lt),
                lambda v, om=om_: v.tensor_scalar(
                    s_oh2[:, :], s_io8[:, :], om[:, 4:5], None, ALU.is_equal),
                lambda v: v.tensor_scalar(
                    s_oh2[:, :], s_oh2[:, :], s_sc2[:, 4:5], None, ALU.mult),
            ], after=[evm_, eva2_prev], label=f"rt{j}")

            def rmm2(t):
                t.matmul(p_rank2[:, :], s_or[0:1, :], s_acc2[0:1, :],
                         start=True, stop=False)
                return t.matmul(p_rank2[:, :], s_L[:, :], s_oh2[:, :],
                                start=False, stop=True)
            evr2 = p.on("tensor", rmm2, after=[evrt, ev_snap2[j - 1]],
                        inc=("tensor", 1), label=f"rmm2{j}")

            def accmm2(t):
                return t.matmul(p_acc2p[:, :], s_oc[:, :], s_oh2[:, :],
                                start=True, stop=True)
            eva2 = p.on("tensor", accmm2, after=[evr2], inc=("tensor", 1),
                        label=f"acc2{j}")
            eva2_prev = eva2

            evsi = p.chain("vector", [
                lambda v: v.tensor_tensor(s_r82[:, :], p_rank2[:, :],
                                          s_oh2[:, :], ALU.mult),
                lambda v: v.reduce_sum(s_sc2[:, 5:6], s_r82[:, :], AX.X),
                lambda v: v.tensor_scalar(s_sc2[:, 6:7], s_sc2[:, 5:6],
                                          float(C_LOC), None, ALU.is_lt),
                lambda v: v.tensor_tensor(s_sc2[:, 7:8], s_sc2[:, 4:5],
                                          s_sc2[:, 6:7], ALU.mult),
                lambda v, j=j, om=om_: v.tensor_tensor(
                    s_wown[:, j:j + 1], om[:, 3:4], s_sc2[:, 6:7], ALU.mult),
                lambda v, om=om_: v.tensor_scalar(
                    s_sc2[:, 8:9], om[:, 4:5], float(C_LOC), None, ALU.mult),
                lambda v: v.tensor_tensor(s_sc2[:, 8:9], s_sc2[:, 8:9],
                                          s_sc2[:, 5:6], ALU.add),
                lambda v: v.tensor_scalar(s_sc2[:, 9:10], s_sc2[:, 7:8],
                                          -1.0, -1.0, ALU.add, ALU.mult),
                lambda v, j=j: v.tensor_tensor(
                    s_sc2[:, 10:11], s_sc2[:, 9:10], s_bigio[:, j:j + 1],
                    ALU.mult),
                lambda v: v.tensor_tensor(s_sc2[:, 11:12], s_sc2[:, 8:9],
                                          s_sc2[:, 10:11], ALU.add),
                lambda v, j=j: v.tensor_copy(s_sidx[:, j:j + 1],
                                             s_sc2[:, 11:12]),
                lambda v: v.tensor_tensor(s_sc2[:, 12:13], s_sc2[:, 5:6],
                                          s_sc2[:, 7:8], ALU.mult),
                lambda v, om=om_: v.tensor_scalar(
                    s_sc2[:, 13:14], om[:, 4:5], float(C_LOC), None, ALU.mult),
                lambda v: v.tensor_tensor(s_sc2[:, 13:14], s_sc2[:, 13:14],
                                          s_sc2[:, 12:13], ALU.add),
                lambda v, j=j: v.tensor_copy(s_gidx[:, j:j + 1],
                                             s_sc2[:, 13:14]),
            ], after=[evr2, evrt], label=f"si{j}")
            ev_si_by_j[j] = evsi
            ev_snap2[j] = p.on(
                "vector", lambda v: v.tensor_tensor(s_acc2[:, :], s_acc2[:, :],
                                                    p_acc2p[:, :], ALU.add),
                after=[eva2], label=f"snap2{j}")

            ev_scat[j] = p.on(
                "gpsimd",
                lambda g, j=j, b2=b2: g.indirect_dma_start(
                    out=disp[:, :],
                    out_offset=IndirectOffsetOnAxis(s_sidx[:, j:j + 1], 0),
                    in_=s_t2b[:, b2], in_offset=None,
                    bounds_check=E * C_LOC - 1, oob_is_err=False),
                after=[evsi, ev_t2, ev_dz_last],
                inc=("d_scat%d" % b2, 16), label=f"scat{j}")

        ev_scat_all = [ev_scat[OWN_TILES - 2], ev_scat[OWN_TILES - 1]]
        ev_xn_all = p.now("vector")
        if DEBUG_OUTS:
            p.dma("gpsimd", dbg_xn2[0], s_xnown.rearrange("p a b -> p (a b)"),
                  after=[ev_xn_all], label="dbgxn0", slot="dbgx")
            p.dma("gpsimd", dbg_idx[:, 0:OWN_TILES], s_sidx[:, :],
                  after=[ev_xn_all] + ev_scat_all, label="dbgsi", slot="dbgx")
            p.dma("gpsimd", dbg_idx[:, OWN_TILES:], s_gidx[:, :],
                  after=[ev_xn_all] + ev_scat_all, label="dbggi", slot="dbgx")
            p.dma("gpsimd", dbg_w[:, :], s_wown[:, :],
                  after=[ev_xn_all] + ev_scat_all, label="dbgw", slot="dbgx")
    pre3.close()

    with SC() as c4:
        s_dT = c4.enter_context(nc.sbuf_tensor("ff_dT", [128, 8, C_LOC], BF16))
        s_dtok = c4.enter_context(nc.sbuf_tensor("ff_dtok", [128, 4, D], BF16))
        s_wib = c4.enter_context(nc.sbuf_tensor("ff_wib", [128, 2, 8, 512], BF16))
        s_hT = c4.enter_context(nc.sbuf_tensor("ff_hT", [128, 32, C_LOC], BF16))
        s_wob = c4.enter_context(nc.sbuf_tensor("ff_wob", [128, 2, 32, 128], BF16))
        s_eoT = c4.enter_context(nc.sbuf_tensor("ff_eoT", [128, 8, 512], F32))
        s_eot = c4.enter_context(nc.sbuf_tensor("ff_eot", [128, 2, D], F32))
        p_h = c4.enter_context(nc.psum_tensor("p_h", [128, 2, 512], F32))
        p_oT = c4.enter_context(nc.psum_tensor("p_oT", [128, 2, 512], F32))
        p_tr = c4.enter_context(nc.psum_tensor("p_tr", [128, 128], F32))
        p_trb = c4.enter_context(nc.psum_tensor("p_trb", [128, 128], BF16))

        fence4 = p.fence()
        eo_last = None
        dtok_guard = None
        dT_guard = None
        eoT_guard = None
        wi_guard = {}
        wo_guard = {}
        gel_ring = []
        oev_ring = []
        trb_evac = None
        tr_evac = None
        eot_dma = {}
        for e in range(E):
            dt_evs = []
            for st4 in range(4):
                evd = p.dma("sync", s_dtok[:, st4],
                            disp[e * C_LOC + st4 * 128:
                                 e * C_LOC + (st4 + 1) * 128, :],
                            after=ev_scat_all + [dtok_guard] + fence4,
                            label=f"dld{e}{st4}", slot=f"dld{st4}")
                for ch in range(8):
                    evt = p.on("tensor",
                               lambda t, st4=st4, ch=ch: t.transpose(
                                   p_trb[:, :],
                                   s_dtok[:, st4, ch * 128:(ch + 1) * 128],
                                   s_idb[:, :]),
                               after=[evd, trb_evac, dT_guard],
                               inc=("tensor", 1), label=f"dtr{e}{st4}{ch}")
                    eve = p.on("vector",
                               lambda v, st4=st4, ch=ch: v.tensor_copy(
                                   s_dT[:, ch, st4 * 128:(st4 + 1) * 128],
                                   p_trb[:, :]),
                               after=[evt], label=f"dtre{e}{st4}{ch}")
                    trb_evac = eve
                    dt_evs.append(eve)
                    dtok_guard = evt
            dT_done = dt_evs[-1]

            gel_evs = []
            for hg in range(8):
                bw = hg % 2
                evw = p.dma("gpsimd", s_wib[:, bw], wit[e, hg],
                            after=[wi_guard.get(bw)] + fence4,
                            label=f"wi{e}{hg}", slot=f"wi{bw}")
                for sub in range(4):
                    hc = hg * 4 + sub
                    bh = hc % 2

                    def hmm(t, bw=bw, sub=sub, bh=bh):
                        for ch in range(8):
                            mm = t.matmul(
                                p_h[:, bh],
                                s_wib[:, bw, ch, sub * 128:(sub + 1) * 128],
                                s_dT[:, ch, :],
                                start=(ch == 0), stop=(ch == 7))
                        return mm
                    evh = p.on("tensor", hmm,
                               after=[evw, dT_done,
                                      gel_ring[-2] if len(gel_ring) >= 2 else None],
                               inc=("tensor", 1), label=f"h{e}{hc}")
                    wi_guard[bw] = evh
                    dT_guard = evh
                    evg = p.on("scalar",
                               lambda s, hc=hc, bh=bh: s.activation(
                                   out=s_hT[:, hc], in_=p_h[:, bh],
                                   func=AF.Gelu),
                               after=[evh], label=f"g{e}{hc}")
                    gel_evs.append(evg)
                    gel_ring.append(evg)
            h_done = gel_evs[-1]

            oc_evs = []
            for oc in range(8):
                bw = oc % 2
                bh = oc % 2
                evw = p.dma("gpsimd", s_wob[:, bw], wot[e, oc],
                            after=[wo_guard.get(bw)] + fence4,
                            label=f"wo{e}{oc}", slot=f"wo{bw}")

                def omm(t, bw=bw, bh=bh):
                    for hc in range(32):
                        mm = t.matmul(p_oT[:, bh], s_wob[:, bw, hc],
                                      s_hT[:, hc, :],
                                      start=(hc == 0), stop=(hc == 31))
                    return mm
                evo = p.on("tensor", omm,
                           after=[evw, h_done,
                                  oev_ring[-2] if len(oev_ring) >= 2 else None],
                           inc=("tensor", 1), label=f"o{e}{oc}")
                wo_guard[bw] = evo
                evoe = p.on("vector",
                            lambda v, oc=oc, bh=bh: v.tensor_copy(
                                s_eoT[:, oc], p_oT[:, bh]),
                            after=[evo, eoT_guard], label=f"oev{e}{oc}")
                oc_evs.append(evoe)
                oev_ring.append(evoe)
            oT_done = oc_evs[-1]

            for st4 in range(4):
                b2 = st4 % 2
                tr_last = None
                for ch in range(8):
                    evt = p.on("tensor",
                               lambda t, st4=st4, ch=ch: t.transpose(
                                   p_tr[:, :],
                                   s_eoT[:, ch, st4 * 128:(st4 + 1) * 128],
                                   s_id[:, :]),
                               after=[oT_done, tr_evac],
                               inc=("tensor", 1), label=f"etr{e}{st4}{ch}")
                    eoT_guard = evt
                    eve = p.on("vector",
                               lambda v, b2=b2, ch=ch: v.tensor_copy(
                                   s_eot[:, b2, ch * 128:(ch + 1) * 128],
                                   p_tr[:, :]),
                               after=[evt, eot_dma.get(b2)],
                               label=f"etre{e}{st4}{ch}")
                    tr_evac = eve
                    tr_last = eve
                evdma = p.dma("gpsimd",
                              eo[e * C_LOC + st4 * 128:
                                 e * C_LOC + (st4 + 1) * 128, :],
                              s_eot[:, b2], after=[tr_last],
                              label=f"eody{e}{st4}", slot=f"eo{b2}")
                eot_dma[b2] = evdma
        ev_eo_all = [eot_dma[0], eot_dma[1]]

    st = dict(last_tb_evac=None, last_pj_evac=None, last_rot_evac=None,
              last_pair_pe=None, last_exp=None, prev_cb=None, ev_y_last=None,
              o_evac=None)
    s_wo2g = c23.enter_context(nc.sbuf_tensor("f_wo2g", [128, 8, D], BF16))
    ev_wo2g = p.dma("sync", s_wo2g[:], wato_bf,
                    after=[ev_pc_qkv[3]] + p.fence(), label="wo2g",
                    slot="awo2")
    prev_cbs = {}
    SHUF = []
    for _i in range(0, 32, 2):
        SHUF += [_i + 1, _i]
    prev_y = {}
    prev_xn2 = {}

    def emit_seq(sq):
        fenceA = p.fence()
        with SC() as cs:
            s_catT = cs.enter_context(nc.sbuf_tensor(f"catT{sq}", [128, 8, S],
                                                     BF16))
            with SC() as c3:
                s_xnT = c3.enter_context(nc.sbuf_tensor(f"at_xnT{sq}", [128, 8, S],
                                                        BF16))
                s_wqk = c3.enter_context(nc.sbuf_tensor(f"at_wqk{sq}",
                                                        [128, 2, 8, 128], BF16))
                s_wvp = c3.enter_context(nc.sbuf_tensor(f"at_wv{sq}", [128, 8, 128],
                                                        BF16))
                s_qT = c3.enter_context(nc.sbuf_tensor(f"at_qT{sq}", [128, S], BF16))
                s_kT = c3.enter_context(nc.sbuf_tensor(f"at_kT{sq}", [128, S], BF16))
                s_vh = c3.enter_context(nc.sbuf_tensor(f"at_vh{sq}", [128, 16, 130],
                                                       BF16))
                s_qe = c3.enter_context(nc.sbuf_tensor(f"at_qe{sq}", [128, 2, 512], BF16))
                s_t1 = c3.enter_context(nc.sbuf_tensor(f"at_t1{sq}", [128, 2, 512], BF16))
                s_PT = c3.enter_context(nc.sbuf_tensor(f"at_PT{sq}", [128, 16, 512],
                                                       BF16))
                p_pj = c3.enter_context(nc.psum_tensor(f"p_pj{sq}",
                                                        [128, 2, 512], F32))
                p_ST = c3.enter_context(nc.psum_tensor(f"p_ST{sq}",
                                                       [128, 2, 2, 512], F32))
                p_O = c3.enter_context(nc.psum_tensor(f"p_O{sq}",
                                                      [128, 4, 65], F32))
                p_tb = c3.enter_context(nc.psum_tensor(f"p_tb{sq}",
                                                       [128, 8, 128], BF16))
                s_Osl = c3.enter_context(nc.sbuf_tensor(f"at_Osl{sq}",
                                                        [128, 16, 65], F32))
                s_Ons = c3.enter_context(nc.sbuf_tensor(f"at_Ons{sq}",
                                                        [128, 16, 64], BF16))
                s_rcl16 = c3.enter_context(nc.sbuf_tensor(f"at_rcl{sq}",
                                                          [128, 16], F32))

                prev_evac = st['last_tb_evac']
                for t16 in range(16):
                    jj = sq * 16 + t16

                    def xtr(t, jj=jj):
                        for ch in range(8):
                            mm = t.transpose(p_tb[:, ch, :],
                                             s_xnown[:, jj, ch * 128:(ch + 1) * 128],
                                             s_idb[:, :])
                        return mm
                    evt = p.on("tensor", xtr,
                               after=[ev_xn_all, prev_evac] + fenceA,
                               inc=("tensor", 1), label=f"xnT{sq}_{t16}")
                    prev_evac = p.on(
                        "vector",
                        lambda v, t16=t16: v.tensor_copy(
                            s_xnT[:, 0:8, t16 * 128:(t16 + 1) * 128],
                            p_tb[:, :, :]),
                        after=[evt], label=f"xnTe{sq}_{t16}")
                xnT_ready = prev_evac
                st['last_tb_evac'] = prev_evac

                for pr in range(8):
                    evwq = p.dma("sync", s_wqk[:, 0], wqkv_bf[0, pr],
                                 after=[st['last_pair_pe'], ev_pc_qkv[0]]
                                 + fenceA,
                                 label=f"wq{sq}_{pr}", slot="awq")
                    evwk = p.dma("sync", s_wqk[:, 1], wqkv_bf[1, pr],
                                 after=[st['last_pair_pe'], ev_pc_qkv[1]]
                                 + fenceA,
                                 label=f"wk{sq}_{pr}", slot="awk")
                    evwv = p.dma("sync", s_wvp[:], wqkv_bf[2, pr],
                                 after=[st['last_pair_pe'], ev_pc_qkv[2]]
                                 + fenceA,
                                 label=f"wv{sq}_{pr}", slot="awv")

                    rope_done = None
                    unit_dve = {}
                    uidx = 0
                    for qk in range(2):
                        dstT = s_qT if qk == 0 else s_kT
                        for gq in range(4):
                            ub = uidx % 2

                            def pj(t, qk=qk, gq=gq, ub=ub):
                                for ch in range(8):
                                    mm = t.matmul(
                                        p_pj[:, ub], s_wqk[:, qk, ch],
                                        s_xnT[:, ch, gq * 512:(gq + 1) * 512],
                                        start=(ch == 0), stop=(ch == 7))
                                return mm
                            vr = st.get('v_ring')
                            evpj = p.on("tensor", pj,
                                        after=[xnT_ready, evwq, evwk,
                                               unit_dve.get(uidx - 2),
                                               vr[uidx] if (vr and uidx < 2)
                                               else None],
                                        inc=("tensor", 1),
                                        label=f"pj{sq}{pr}{qk}{gq}")
                            evac = p.on("vector",
                                        lambda v, ub=ub: v.tensor_copy(
                                            s_qe[:, ub], p_pj[:, ub]),
                                        after=[evpj],
                                        label=f"pjev{sq}{pr}{qk}{gq}")
                            evch = p.chain("vector", [
                                lambda v, ub=ub: v.stream_shuffle(
                                    s_t1[:, ub], s_qe[:, ub], SHUF),
                                lambda v, ub=ub, gq=gq: v.scalar_tensor_tensor(
                                    s_t1[:, ub], s_t1[:, ub], s_sgn[:, 0:1],
                                    s_sin[:, gq * 512:(gq + 1) * 512],
                                    ALU.mult, ALU.mult),
                                lambda v, ub=ub, gq=gq, dstT=dstT:
                                v.tensor_tensor(
                                    dstT[:, gq * 512:(gq + 1) * 512],
                                    s_qe[:, ub],
                                    s_cos[:, gq * 512:(gq + 1) * 512], ALU.mult),
                                lambda v, ub=ub, gq=gq, dstT=dstT:
                                v.tensor_tensor(
                                    dstT[:, gq * 512:(gq + 1) * 512],
                                    s_t1[:, ub],
                                    dstT[:, gq * 512:(gq + 1) * 512], ALU.add),
                            ], after=[st['last_pair_pe']],
                                label=f"rope{sq}{pr}{qk}{gq}")
                            unit_dve[uidx] = evac
                            rope_done = evch
                            uidx += 1
                    qk_done = rope_done

                    evvz = p.on("vector",
                                lambda v: v.memset(s_vh[:, :, 64:65], 1.0),
                                after=[st['last_pair_pe']], label="vones")
                    evvz2 = p.on("vector",
                                 lambda v: v.memset(s_vh[:, :, 129:130], 1.0),
                                 after=[st['last_pair_pe']], label="vones2")
                    v_last = None
                    v_prev = {}
                    for kc in range(16):
                        vb = kc % 2

                        def vpj(t, kc=kc, vb=vb):
                            for ch in range(8):
                                mm = t.matmul(
                                    p_pj[:, vb, 0:128],
                                    s_xnT[:, ch, kc * 128:(kc + 1) * 128],
                                    s_wvp[:, ch],
                                    start=(ch == 0), stop=(ch == 7))
                            return mm
                        evv = p.on("tensor", vpj,
                                   after=[xnT_ready, evwv, v_prev.get(kc - 2),
                                          unit_dve.get(6 + kc) if kc < 2
                                          else None],
                                   inc=("tensor", 1), label=f"v{sq}{pr}{kc}")
                        evve = p.chain("vector", [
                            lambda v, kc=kc, vb=vb: v.tensor_copy(
                                s_vh[:, kc, 0:64], p_pj[:, vb, 0:64]),
                            lambda v, kc=kc, vb=vb: v.tensor_copy(
                                s_vh[:, kc, 65:129], p_pj[:, vb, 64:128]),
                        ], after=[evv, evvz, evvz2], label="vev")
                        v_prev[kc] = evve
                        v_last = evve
                    v_done = v_last
                    st['v_ring'] = (v_prev[14], v_prev[15])

                    pair_pe_events = []
                    for hh in range(2):
                        pb = hh * 64
                        vsl = (0, 65) if hh == 0 else (65, 130)
                        for gq in range(4):
                            nch = 4 * gq + 4
                            exp_evs = {}
                            bidx = 0
                            for cb in range(0, nch, 2):
                                nb = min(2, nch - cb)
                                sb_ = bidx % 2

                                def stm(t, cb=cb, nb=nb, gq=gq, pb=pb, sb_=sb_):
                                    for kk in range(nb):
                                        kc = cb + kk
                                        mm = t.matmul(
                                            p_ST[:, sb_, kk],
                                            s_kT[pb:pb + 64,
                                                 kc * 128:(kc + 1) * 128],
                                            s_qT[pb:pb + 64,
                                                 gq * 512:(gq + 1) * 512],
                                            start=True, stop=True)
                                    return mm
                                evst = p.on("tensor", stm,
                                            after=[qk_done, v_done,
                                                   exp_evs.get(bidx - 2,
                                                               st['last_exp'])],
                                            inc=("tensor", 1),
                                            label=f"stm{sq}{pr}{hh}{gq}{cb}")
                                evex = p.on("scalar",
                                            lambda s, cb=cb, nb=nb, sb_=sb_:
                                            s.activation(
                                                out=s_PT[:, cb:cb + nb]
                                                .rearrange("p a b -> p (a b)"),
                                                in_=p_ST[:, sb_, 0:nb]
                                                .rearrange("p a b -> p (a b)"),
                                                func=AF.Exp, scale=0.125),
                                            after=[evst], label=f"ex{cb}")
                                exp_evs[bidx] = evex
                                bidx += 1
                            st['last_exp'] = exp_evs[bidx - 1]
                            mask_evs = []
                            for jd in range(4):
                                kc = 4 * gq + jd
                                evmk = p.on("vector",
                                            lambda v, kc=kc, jd=jd:
                                            v.tensor_tensor(
                                                s_PT[:, kc,
                                                     jd * 128:(jd + 1) * 128],
                                                s_PT[:, kc,
                                                     jd * 128:(jd + 1) * 128],
                                                s_tri[:, :], ALU.mult),
                                            after=[exp_evs[bidx - 1]],
                                            label=f"mk{kc}")
                                mask_evs.append(evmk)
                            for qt in range(4):
                                nk = 4 * gq + qt + 1

                                def pv(t, nk=nk, qt=qt, vsl=vsl):
                                    for kc in range(nk):
                                        mm = t.matmul(
                                            p_O[:, qt],
                                            s_PT[:, kc,
                                                 qt * 128:(qt + 1) * 128],
                                            s_vh[:, kc, vsl[0]:vsl[1]],
                                            start=(kc == 0),
                                            stop=(kc == nk - 1))
                                    return mm
                                evpv = p.on("tensor", pv,
                                            after=mask_evs + [st['o_evac']],
                                            inc=("tensor", 1),
                                            label=f"pv{sq}{pr}{hh}{gq}{qt}")
                                pair_pe_events.append(evpv)
                            st['o_evac'] = p.on(
                                "vector",
                                lambda v, gq=gq: v.tensor_copy(
                                    s_Osl[:, gq * 4:(gq + 1) * 4, :],
                                    p_O[:, :, :]),
                                after=[evpv], label=f"oev{sq}{pr}{hh}{gq}")
                        evn = p.chain("vector", [
                            lambda v: v.reciprocal(s_rcl16[:, :],
                                                   s_Osl[:, :, 64:65]
                                                   .rearrange("p a b -> p (a b)")),
                            lambda v: v.tensor_tensor(
                                s_Ons[:, :, :], s_Osl[:, :, 0:64],
                                s_rcl16[:, :].rearrange("p (a b) -> p a b", b=1)
                                .broadcast_to([128, 16, 64]), ALU.mult),
                        ], after=[st['o_evac']], label=f"onrm{sq}{pr}{hh}")
                        for grp in range(2):
                            def otr(t, grp=grp):
                                for k8 in range(8):
                                    mm = t.transpose(
                                        p_tb[0:64, k8, :],
                                        s_Ons[:, grp * 8 + k8, :],
                                        s_idb[:, :])
                                return mm
                            evot = p.on("tensor", otr,
                                        after=[evn, st['last_tb_evac']],
                                        inc=("tensor", 1),
                                        label=f"otr{sq}{pr}{hh}{grp}")
                            st['last_tb_evac'] = p.on(
                                "vector",
                                lambda v, pb=pb, pr=pr, grp=grp: v.tensor_copy(
                                    s_catT[pb:pb + 64, pr,
                                           grp * 1024:(grp + 1) * 1024],
                                    p_tb[0:64, :, :]
                                    .rearrange("p a b -> p (a b)")),
                                after=[evot], label=f"oevac{sq}{pr}{hh}{grp}")
                    st['last_pair_pe'] = pair_pe_events[-1]
                ev_attn_done = st['last_tb_evac']
                if DEBUG_OUTS:
                    p.dma("gpsimd", dbg_cat[sq],
                          s_catT.rearrange("p a b -> p (a b)"),
                          after=[ev_attn_done], label=f"dbgcat{sq}",
                          slot="dbgcat")

            fence5 = p.fence()
            with SC() as c5:
                s_fx = c5.enter_context(nc.sbuf_tensor(f"f_x{sq}", [128, 2, D], F32))
                s_eog = c5.enter_context(nc.sbuf_tensor(f"f_eog{sq}", [128, 2, D],
                                                        F32))
                s_xn2 = c5.enter_context(nc.sbuf_tensor(f"f_xn{sq}", [128, D], F32))
                s_yt = c5.enter_context(nc.sbuf_tensor(f"f_y{sq}", [128, 2, D], F32))
                p_w1 = c5.enter_context(nc.psum_tensor(f"p_w1{sq}",
                                                       [128, 2, 2, 512], F32))
                for t16 in range(16):
                    j = sq * 16 + t16
                    b2 = j % 2
                    evfx = p.on("gpsimd",
                                lambda g, j=j, b2=b2: g.indirect_dma_start(
                                    out=s_fx[:, b2], out_offset=None,
                                    in_=xf[:, :],
                                    in_offset=IndirectOffsetOnAxis(
                                        s_ownrow[:, j:j + 1], 0)),
                                after=[prev_y.get(j - 2),
                                       prev_xn2.get(j - 1)] + fence5,
                                inc=("d_fx%d" % b2, 16), label=f"fx{j}")
                    evge = p.on("gpsimd",
                                lambda g, j=j, b2=b2: g.indirect_dma_start(
                                    out=s_eog[:, b2], out_offset=None,
                                    in_=eo[:, :],
                                    in_offset=IndirectOffsetOnAxis(
                                        s_gidx[:, j:j + 1], 0)),
                                after=ev_eo_all + [prev_y.get(j - 2)] + fence5,
                                inc=("d_ge%d" % b2, 16), label=f"ge{j}")
                    wo_evs = []
                    for half in range(2):
                        def wmm(t, t16=t16, half=half, b2=b2):
                            for ch in range(8):
                                mm = t.matmul(
                                    p_w1[:, b2, half],
                                    s_catT[:, ch, t16 * 128:(t16 + 1) * 128],
                                    s_wo2g[:, ch, half * 512:(half + 1) * 512],
                                    start=(ch == 0), stop=(ch == 7))
                            return mm
                        evw = p.on("tensor", wmm,
                                   after=[ev_attn_done, ev_wo2g,
                                          prev_cbs.get(j - 2)],
                                   inc=("tensor", 1), label=f"wom{j}_{half}")
                        wo_evs.append(evw)
                    evxn = p.on("vector",
                                lambda v, j=j, b2=b2: v.tensor_scalar(
                                    s_xn2[:, :], s_fx[:, b2],
                                    s_nm1r1[:, j:j + 1],
                                    s_r1own[:, j:j + 1],
                                    ALU.add, ALU.mult),
                                after=[evfx, st['prev_cb']], label=f"fxn{j}")
                    prev_xn2[j] = evxn

                    evcb = p.chain("vector", [
                        lambda v, j=j, b2=b2: v.tensor_scalar(
                            s_eog[:, b2], s_eog[:, b2], s_wown[:, j:j + 1],
                            None, ALU.mult),
                        lambda v, b2=b2: v.tensor_tensor(
                            s_yt[:, b2], s_fx[:, b2], s_xn2[:, :], ALU.add),
                        lambda v, b2=b2: v.tensor_tensor(
                            s_yt[:, b2], s_yt[:, b2], s_eog[:, b2], ALU.add),
                        lambda v, b2=b2: v.tensor_tensor(
                            s_yt[:, b2, 0:512], s_yt[:, b2, 0:512],
                            p_w1[:, b2, 0], ALU.add),
                        lambda v, b2=b2: v.tensor_tensor(
                            s_yt[:, b2, 512:1024], s_yt[:, b2, 512:1024],
                            p_w1[:, b2, 1], ALU.add),
                    ], after=[evxn, evge] + wo_evs, label=f"cb{j}")
                    st['prev_cb'] = evcb
                    prev_cbs[j] = evcb
                    evy = p.dma("gpsimd", y[j * 128:(j + 1) * 128, :],
                                s_yt[:, b2], after=[evcb], label=f"y{j}",
                                slot=f"y{b2}")
                    prev_y[j] = evy
                    st['ev_y_last'] = evy

    for sq in ([1] if SEQ1_ONLY else range(2)):
        emit_seq(sq)
    c23.close()

    p.on("gpsimd", lambda g: g.engine_nop(),
         after=[prev_y[OWN_TILES - 2], prev_y[OWN_TILES - 1]], inc=None,
         label="fin")

    p.emit(nc, es)
    return nc


_NC_CACHE = {}


def _get_nc():
    if "nc" not in _NC_CACHE:
        _NC_CACHE["nc"] = build_nc()
    return _NC_CACHE["nc"]


def kernel(L0_attn_x, L0_attn_Wq, L0_attn_Wk, L0_attn_Wv, L0_attn_Wo,
           L0_ffn_gate_weight, L0_ffn_ln_gamma, L0_ffn_ln_beta,
           L0_ffn_wi, L0_ffn_wo, L0_ffn_conf_w1, L0_ffn_conf_b1,
           L0_ffn_conf_w2, L0_ffn_conf_b2, L0_ffn_comp_w, L0_ffn_comp_b):
    x = np.ascontiguousarray(np.asarray(L0_attn_x, dtype=np.float32))
    xf = x.reshape(NT, D)
    xT = np.ascontiguousarray(
        xf.T.reshape(8, 128, 64, 512).transpose(2, 1, 0, 3))

    c = {}
    c["ident"] = np.eye(128, dtype=np.float32)
    c["identb"] = np.eye(128, dtype=np.float32).astype(ml_dtypes.bfloat16)
    c["lstrictT"] = np.triu(np.ones((128, 128), np.float32), 1)
    c["onesrow"] = np.ones((1, 128), np.float32)
    c["onescol"] = np.ones((128, 1), np.float32)
    c["iota8"] = np.tile(np.arange(8, dtype=np.float32), (128, 1))
    c["trimask"] = np.triu(np.ones((128, 128), np.float32), 0).astype(
        ml_dtypes.bfloat16)
    inv = 1.0 / ROPE_THETA ** (np.arange(0, DK, 2, dtype=np.float64) / DK)
    ang = np.outer(np.arange(S, dtype=np.float64), inv)
    cos = np.repeat(np.cos(ang).astype(np.float32), 2, axis=1).T
    sin = np.repeat(np.sin(ang).astype(np.float32), 2, axis=1).T
    c["ropecos"] = np.ascontiguousarray(np.vstack([cos, cos]), dtype=np.float32)
    c["ropesin"] = np.ascontiguousarray(np.vstack([sin, sin]), dtype=np.float32)
    R = np.zeros((64, 64), np.float32)
    for i in range(32):
        R[2 * i, 2 * i + 1] = -1.0
        R[2 * i + 1, 2 * i] = 1.0
    blk = np.zeros((128, 128), np.float32)
    blk[0:64, 0:64] = R.T
    blk[64:128, 64:128] = R.T
    c["ropePT"] = blk.astype(ml_dtypes.bfloat16)
    sg = np.empty((128, 1), np.float32)
    sg[0::2, 0] = -1.0
    sg[1::2, 0] = 1.0
    c["signcol"] = sg
    big = np.empty((128, OWN_TILES), np.float32)
    for j in range(OWN_TILES):
        big[:, j] = 2.0e8 + j * 128 + np.arange(128)
    c["bigiota"] = big

    def til(w):
        return np.asarray(w, np.float32).reshape(8, 128, 8, 128)\
            .transpose(2, 1, 0, 3)
    base = {
        "wqkvt": np.ascontiguousarray(np.stack(
            [til(L0_attn_Wq), til(L0_attn_Wk), til(L0_attn_Wv)])),
        "watot": np.ascontiguousarray(
            np.asarray(L0_attn_Wo, np.float32).reshape(8, 128, D)
            .transpose(1, 0, 2)),
        "gT": np.ascontiguousarray(np.asarray(L0_ffn_gate_weight, np.float32).T),
        "gamma_bc": np.ascontiguousarray(
            np.tile(np.asarray(L0_ffn_ln_gamma, np.float32)[None, :], (128, 1))),
        "beta_bc": np.ascontiguousarray(
            np.tile(np.asarray(L0_ffn_ln_beta, np.float32)[None, :], (128, 1))),
        "gammacol": np.ascontiguousarray(
            np.asarray(L0_ffn_ln_gamma, np.float32).reshape(8, 128).T),
        "betacol": np.ascontiguousarray(
            np.asarray(L0_ffn_ln_beta, np.float32).reshape(8, 128).T),
        "wit": np.ascontiguousarray(
            np.asarray(L0_ffn_wi, np.float32).reshape(E, 8, 128, 8, 512)
            .transpose(0, 3, 2, 1, 4)),
        "wot": np.ascontiguousarray(
            np.asarray(L0_ffn_wo, np.float32).reshape(E, 32, 128, 8, 128)
            .transpose(0, 3, 2, 1, 4)),
        "cw1T": np.ascontiguousarray(np.asarray(L0_ffn_conf_w1, np.float32).T),
        "cb1": np.ascontiguousarray(
            np.asarray(L0_ffn_conf_b1, np.float32).reshape(4, 128).T),
        "cw2T": np.ascontiguousarray(
            np.asarray(L0_ffn_conf_w2, np.float32).reshape(4, 128).T),
        "cb2r": np.full((128, 1),
                        float(np.asarray(L0_ffn_conf_b2).reshape(-1)[0]),
                        np.float32),
        "compwT": np.ascontiguousarray(
            np.asarray(L0_ffn_comp_w, np.float32).reshape(8, 128).T),
        "compbr": np.full((128, 1),
                          float(np.asarray(L0_ffn_comp_b).reshape(-1)[0]),
                          np.float32),
        "xf": xf, "xT": xT,
    }
    base.update(c)

    in_maps = []
    for core in range(NCORES):
        m = dict(base)
        rows = np.empty((128, OWN_TILES), np.uint32)
        for j in range(OWN_TILES):
            sq, t16 = divmod(j, 16)
            bb = core + 8 * sq
            rows[:, j] = bb * S + t16 * 128 + np.arange(128, dtype=np.uint32)
        m["ownrow_u"] = rows
        in_maps.append(m)

    nc = _get_nc()
    res = bass_utils.run_bass_kernel_spmd(nc, in_maps,
                                          core_ids=list(range(NCORES)))
    out = np.empty((B, S, D), np.float32)
    for core in range(NCORES):
        ycore = res.results[core]["y"]
        out[core] = ycore[0:S]
        out[core + 8] = ycore[S:2 * S]
    return out


# version 15
# speedup vs baseline: 1.0751x; 1.0751x over previous
import contextlib

import numpy as np
import ml_dtypes
import concourse.bass as bass
import concourse.mybir as mybir
import concourse.bass_utils as bass_utils
from concourse.bass_core_rust import IndirectOffsetOnAxis

F32 = mybir.dt.float32
BF16 = mybir.dt.bfloat16
U32 = mybir.dt.uint32
AF = mybir.ActivationFunctionType
ALU = mybir.AluOpType
AX = mybir.AxisListType

B, S, D = 16, 2048, 1024
NCORES = 8
NT = B * S
NTILE = NT // 128
OWN_TILES = 2 * S // 128
H, DK = 16, 64
E, CAP = 8, 2048
C_LOC = 512
DI = 4096
EPS = 1e-5
ROPE_THETA = 10000.0
DEBUG_OUTS = False
SEQ1_ONLY = False
SERIAL_ATTN = False

ENGINES = ("sync", "tensor", "vector", "scalar", "gpsimd")


class Prog:
    def __init__(self):
        self.items = {e: [] for e in ENGINES}
        self.cnt = {}

    def now(self, sem):
        return (sem, self.cnt.get(sem, 0))

    def _bump(self, sem, k):
        self.cnt[sem] = self.cnt.get(sem, 0) + k
        return (sem, self.cnt[sem])

    def on(self, eng, fn, after=(), inc="auto", label=""):
        waits = [w for w in after if w is not None]
        if eng in ("vector", "scalar", "gpsimd"):
            waits.append(self.now(eng))
            if inc == "auto":
                inc = (eng, 1)
        if inc == "auto":
            inc = None
        ev, incs = None, []
        if inc is not None:
            ev = self._bump(*inc)
            incs = [inc]
        self.items[eng].append((waits, fn, incs, label))
        return ev

    def fence(self):
        evs = [self.now(e) for e in ("tensor", "vector", "scalar", "gpsimd")]
        evs += [self.now(s) for s in list(self.cnt) if s.startswith("d_")]
        return evs

    def chain(self, eng, fns, after=(), label=""):
        ev = None
        for ix, fn in enumerate(fns):
            ev = self.on(eng, fn, after=after if ix == 0 else (),
                         label=f"{label}#{ix}")
        return ev

    def dma(self, eng, out, in_, after=(), label="", slot=None, **kw):
        assert slot is not None, label
        def fn(g):
            return g.dma_start(out=out, in_=in_, **kw)
        return self.on(eng, fn, after=after, inc=("d_" + slot, 16), label=label)

    def emit(self, nc, es):
        sems = {}
        for name in sorted(self.cnt.keys()):
            assert self.cnt[name] < 2 ** 31
            sems[name] = es.enter_context(nc.semaphore(name))
        lastwait = {e: {} for e in ENGINES}

        def run(engobj, eng):
            lw = lastwait[eng]
            for waits, fn, incs, label in self.items[eng]:
                for sem, val in waits:
                    if val <= 0 or lw.get(sem, -1) >= val:
                        continue
                    engobj.wait_ge(sems[sem], val)
                    lw[sem] = val
                inst = fn(engobj)
                for sem, k in incs:
                    assert inst is not None, label
                    inst.then_inc(sems[sem], k)

        with nc.Block() as block:
            @block.sync
            def _(e):
                run(e, "sync")

            @block.tensor
            def _(e):
                run(e, "tensor")

            @block.vector
            def _(e):
                run(e, "vector")

            @block.scalar
            def _(e):
                run(e, "scalar")

            @block.gpsimd
            def _(e):
                run(e, "gpsimd")


def build_nc():
    nc = bass.Bass("TRN2", target_bir_lowering=False, debug=False)
    es = contextlib.ExitStack()
    nc._keepalive_es = es
    p = Prog()
    SC = contextlib.ExitStack

    def din(name, shape, dt=F32):
        return nc.dram_tensor(name, shape, dt, kind="ExternalInput").ap()

    xf = din("xf", [NT, D])
    xT = din("xT", [64, 128, 8, 512])
    wqkvt = din("wqkvt", [3, 8, 128, 8, 128])
    watot = din("watot", [128, 8, D])
    gT = din("gT", [D, E])
    gamma_bc = din("gamma_bc", [128, D]); beta_bc = din("beta_bc", [128, D])
    gammacol = din("gammacol", [128, 8]); betacol = din("betacol", [128, 8])
    wit = din("wit", [E, 8, 128, 8, 512])
    wot = din("wot", [E, 8, 128, 32, 128])
    cw1T = din("cw1T", [D, 512]); cb1 = din("cb1", [128, 4])
    cw2T = din("cw2T", [128, 4]); cb2r = din("cb2r", [128, 1])
    compwT = din("compwT", [128, 8]); compbr = din("compbr", [128, 1])
    ident = din("ident", [128, 128]); identb = din("identb", [128, 128], BF16)
    lstrictT = din("lstrictT", [128, 128])
    onesrow = din("onesrow", [1, 128]); onescol = din("onescol", [128, 1])
    iota8 = din("iota8", [128, 8])
    trimask = din("trimask", [128, 128], BF16)
    ropecos = din("ropecos", [128, S]); ropesin = din("ropesin", [128, S])
    ropePT = din("ropePT", [128, 128], BF16)
    signcol = din("signcol", [128, 1])
    bigiota = din("bigiota", [128, OWN_TILES])
    ownrow_u = nc.dram_tensor("ownrow_u", [128, OWN_TILES], U32,
                              kind="ExternalInput").ap()

    y = nc.dram_tensor("y", [2 * S, D], F32, kind="ExternalOutput").ap()
    _dbg = "ExternalOutput" if DEBUG_OUTS else "Internal"
    dbg_cat = (nc.dram_tensor("dbg_cat", [2, 128, 8 * S], BF16, kind=_dbg).ap()
               if DEBUG_OUTS else None)
    dbg_xnT = (nc.dram_tensor("dbg_xnT", [2, 128, 8 * S], BF16, kind=_dbg).ap()
               if DEBUG_OUTS else None)
    dbg_qk = (nc.dram_tensor("dbg_qk", [2, 128, S], BF16, kind=_dbg).ap()
              if DEBUG_OUTS else None)
    dbg_PT = (nc.dram_tensor("dbg_PT", [128, 16 * 512], BF16, kind=_dbg).ap()
              if DEBUG_OUTS else None)
    dbg_xn2 = (nc.dram_tensor("dbg_xn2", [2, 128, OWN_TILES * D], BF16,
                              kind=_dbg).ap() if DEBUG_OUTS else None)
    dbg_idx = (nc.dram_tensor("dbg_idx", [128, 2 * OWN_TILES], U32,
                              kind=_dbg).ap() if DEBUG_OUTS else None)
    dbg_w = (nc.dram_tensor("dbg_w", [128, OWN_TILES], F32, kind=_dbg).ap()
             if DEBUG_OUTS else None)
    disp = nc.dram_tensor("disp", [E * C_LOC, D], BF16, kind=_dbg).ap()
    eo = nc.dram_tensor("eo", [E * C_LOC, D], F32, kind=_dbg).ap()
    meta_d = nc.dram_tensor("meta_d", [NT, 8], F32, kind=_dbg).ap()
    wqkv_bf = nc.dram_tensor("wqkv_bf", [3, 8, 128, 8, 128], BF16,
                             kind="Internal").ap()
    wato_bf = nc.dram_tensor("wato_bf", [128, 8, D], BF16,
                             kind="Internal").ap()

    sb = lambda n, sh, dt=F32: es.enter_context(nc.sbuf_tensor(n, sh, dt))

    c23 = SC()
    pre3 = SC()
    _sp_reqs = []

    def sp(n, sh, dt=F32):
        _sp_reqs.append((n, sh, dt))
        return ("SPREQ", len(_sp_reqs) - 1)
    s_id = sb("c_id", [128, 128]); s_idb = sb("c_idb", [128, 128], BF16)
    s_L = sp("c_L", [128, 128])
    s_or = sb("c_or", [1, 128]); s_oc = sb("c_oc", [128, 1])
    s_io8 = sp("c_io8", [128, 8]); s_tri = sb("c_tri", [128, 128], BF16)
    s_cos = sb("c_cos", [128, S]); s_sin = sb("c_sin", [128, S])
    s_rpt = sb("c_rpt", [128, 128], BF16)
    s_sgn = sb("c_sgn", [128, 1])
    s_gbc = sp("c_gbc", [128, D]); s_bbc = sp("c_bbc", [128, D])
    s_gTw = sp("c_gTw", [128, 8, 8])
    s_gcol = sp("c_gcol", [128, 8]); s_bcol = sp("c_bcol", [128, 8])
    s_g9 = sp("c_g9", [128, 8, 9])
    s_Cbc = sp("c_Cbc", [128, 8]); s_Bgbc = sp("c_Bgbc", [128, 8])
    s_cb2 = sp("c_cb2", [128, 1]); s_compb = sp("c_compb", [128, 1])
    s_cw2b = sp("c_cw2b", [128, 4], BF16)
    s_compwb = sp("c_compwb", [128, 8], BF16)
    s_cb1 = sp("c_cb1", [128, 4])
    s_bigio = sp("c_bigio", [128, OWN_TILES])
    s_ownrow = es.enter_context(nc.sbuf_tensor("ownrow", [128, OWN_TILES], U32))
    s_gidx = es.enter_context(nc.sbuf_tensor("gidx", [128, OWN_TILES], U32))
    s_sidx = es.enter_context(nc.sbuf_tensor("sidx", [128, OWN_TILES], U32))
    s_wown = sb("wown", [128, OWN_TILES])
    s_r1own = sb("r1own", [128, OWN_TILES])
    s_nm1r1 = sb("nm1r1own", [128, OWN_TILES])
    s_acc = sp("accs", [1, 8])
    s_iok = sp("c_iok", [128, 8])
    s_m1000 = sp("c_m1000", [128, 1])
    s_acc2 = sp("accs2", [1, 8])
    s_confacc = sp("confacc", [128, 1]); s_compacc = sp("compacc", [128, 1])
    s_zero = sp("zerot", [128, D], BF16)
    s_eps = sp("epst", [128, 1])

    s_xnown = c23.enter_context(nc.sbuf_tensor("xnown", [128, OWN_TILES, D], BF16))
    _sp_t = [pre3.enter_context(nc.sbuf_tensor(n_, sh_, dt_))
             for (n_, sh_, dt_) in _sp_reqs]

    def _resolve(h):
        return _sp_t[h[1]] if isinstance(h, tuple) and len(h) == 2 and h[0] == "SPREQ" else h
    s_L = _resolve(s_L); s_io8 = _resolve(s_io8)
    s_gbc = _resolve(s_gbc); s_bbc = _resolve(s_bbc)
    s_gTw = _resolve(s_gTw); s_gcol = _resolve(s_gcol); s_bcol = _resolve(s_bcol)
    s_g9 = _resolve(s_g9); s_Cbc = _resolve(s_Cbc); s_Bgbc = _resolve(s_Bgbc)
    s_cb2 = _resolve(s_cb2); s_compb = _resolve(s_compb)
    s_cw2b = _resolve(s_cw2b); s_compwb = _resolve(s_compwb)
    s_cb1 = _resolve(s_cb1); s_bigio = _resolve(s_bigio)
    s_acc = _resolve(s_acc); s_acc2 = _resolve(s_acc2)
    s_iok = _resolve(s_iok); s_m1000 = _resolve(s_m1000)
    s_confacc = _resolve(s_confacc); s_compacc = _resolve(s_compacc)
    s_zero = _resolve(s_zero)
    s_eps = _resolve(s_eps)
    evc = []
    for dst, src in [(s_id, ident), (s_idb, identb), (s_L, lstrictT),
                     (s_or, onesrow), (s_oc, onescol), (s_io8, iota8),
                     (s_tri, trimask), (s_cos, ropecos), (s_sin, ropesin),
                     (s_rpt, ropePT), (s_sgn, signcol),
                     (s_gbc, gamma_bc), (s_bbc, beta_bc),
                     (s_cb2, cb2r), (s_compb, compbr), (s_bigio, bigiota),
                     (s_ownrow, ownrow_u)]:
        evc.append(p.dma("sync", dst[:], src[:, :], slot="const"))
    evc.append(p.dma("sync", s_gTw[:], gT.rearrange("(c q) n -> q c n", q=128),
                     slot="const"))
    for dst, srcap in [(s_gcol, gammacol), (s_bcol, betacol), (s_cb1, cb1)]:
        evc.append(p.dma("sync", dst[:], srcap[:, :], slot="const"))
    for dst, srcap in [(s_cw2b, cw2T), (s_compwb, compwT)]:
        evc.append(p.dma("gpsimd", dst[:], srcap[:, :], slot="constg"))
    evc = [evc[-3], evc[-1]]

    def mkg9(v):
        for ch in range(8):
            v.tensor_scalar(s_g9[:, ch, 0:8], s_gTw[:, ch], s_gcol[:, ch:ch + 1],
                            None, ALU.mult)
        return v.memset(s_g9[:, :, 8:9], 1.0)

    ev_g9 = p.on("vector", mkg9, after=evc)
    ev_zt = p.on("vector", lambda v: v.memset(s_zero[:, :], 0.0))
    ev_eps = p.on("vector", lambda v: v.memset(s_eps[:, :], EPS))
    ev_iok = p.on("vector",
                  lambda v: v.tensor_scalar(s_iok[:, :], s_io8[:, :], 1000.0,
                                            None, ALU.add), after=evc)
    ev_m1000 = p.on("vector", lambda v: v.memset(s_m1000[:, :], -1000.0))
    ev_acc0 = p.on("vector", lambda v: v.memset(s_acc[:, :], 0.0))
    ev_acc20 = p.on("vector", lambda v: v.memset(s_acc2[:, :], 0.0))
    ev_cf0 = p.on("vector", lambda v: v.memset(s_confacc[:, :], 0.0))
    ev_cp0 = p.on("vector", lambda v: v.memset(s_compacc[:, :], 0.0))

    with SC() as c0:
        p_r8 = c0.enter_context(nc.psum_tensor("p_r8", [1, 8], F32))
        p_b8 = c0.enter_context(nc.psum_tensor("p_b8", [128, 8], F32))

        def mmC(t):
            for ch in range(8):
                mm = t.matmul(p_r8[:, :], s_oc[:, :], s_g9[:, ch, 0:8],
                              start=(ch == 0), stop=(ch == 7))
            return mm
        e1 = p.on("tensor", mmC, after=[ev_g9] + evc, inc=("tensor", 1))
        e2 = p.on("vector", lambda v: v.tensor_copy(s_Cbc[0:1, :], p_r8[:, :]),
                  after=[e1])
        e3 = p.on("tensor",
                  lambda t: t.matmul(p_b8[:, :], s_or[0:1, :], s_Cbc[0:1, :],
                                     start=True, stop=True),
                  after=[e2], inc=("tensor", 1))
        e4 = p.on("vector", lambda v: v.tensor_copy(s_Cbc[:, :], p_b8[:, :]),
                  after=[e3])

        def mmBg(t):
            for ch in range(8):
                mm = t.matmul(p_r8[:, :], s_bcol[:, ch:ch + 1], s_gTw[:, ch],
                              start=(ch == 0), stop=(ch == 7))
            return mm
        e5 = p.on("tensor", mmBg, after=[e4] + evc, inc=("tensor", 1))
        e6 = p.on("vector", lambda v: v.tensor_copy(s_Bgbc[0:1, :], p_r8[:, :]),
                  after=[e5])
        e7 = p.on("tensor",
                  lambda t: t.matmul(p_b8[:, :], s_or[0:1, :], s_Bgbc[0:1, :],
                                     start=True, stop=True),
                  after=[e6], inc=("tensor", 1))
        ev_gate = p.on("vector", lambda v: v.tensor_copy(s_Bgbc[:, :], p_b8[:, :]),
                       after=[e7])

    ev_dz = []
    for r0 in range(0, E * C_LOC, 128):
        ev_dz.append(p.dma("gpsimd", disp[r0:r0 + 128, :], s_zero[:, :],
                           after=[ev_zt], label="dispz", slot="dz"))
    ev_dz_last = ev_dz[-1]

    ev_pc_qkv = []
    for wi_ in range(3):
        ev_pc_qkv.append(p.dma("gpsimd", wqkv_bf[wi_], wqkvt[wi_],
                               label=f"pcq{wi_}", slot=f"pq{wi_}"))
    ev_pc_qkv.append(p.dma("gpsimd", wato_bf, watot,
                           label="pcq3", slot="pq3"))

    NQ = NTILE // 4
    with SC() as c1:
        sb1 = lambda n, sh, dt=F32: c1.enter_context(nc.sbuf_tensor(n, sh, dt))
        s_xq = sb1("sc_x", [128, 2, 4, D])
        s_xTq = sb1("sc_xT", [128, 2, 8, 512])
        s_jk = c1.enter_context(nc.sbuf_tensor("sc_jk", [128, 2, D], BF16))
        s_gTq = sb1("sc_gT", [9, 2, 512])
        s_Gq = sb1("sc_G", [128, 4, 40])
        s_m8 = sb1("sc_m8", [128, 4, 32])
        s_q = sb1("sc_q", [128, 4, 24])
        s_lgq = sb1("sc_lg", [128, 4, 32])
        s_ohq = sb1("sc_oh", [128, 4, 32])
        s_r8q = sb1("sc_r8", [128, 4, 32])
        s_mp = sb1("sc_mp", [128, 2, 4, 8])
        s_tot = sb1("sc_tot", [1, 2, 32])
        s_off = sb1("sc_off", [1, 2, 4, 8])
        s_acc2d = sb1("sc_acc2d", [1, 2, 8])
        p_gTq = c1.enter_context(nc.psum_tensor("p_gT", [9, 2, 512], F32))
        p_Gq = c1.enter_context(nc.psum_tensor("p_G", [128, 2, 40], F32))
        p_rank = c1.enter_context(nc.psum_tensor("p_rank", [128, 4, 8], F32))
        p_tot = c1.enter_context(nc.psum_tensor("p_tot", [1, 32], F32))

        ev_z1 = p.on("vector", lambda v: v.memset(s_m8[:, :, 24:32], 0.0))
        ev_acc0q = p.on("vector", lambda v: v.memset(s_acc2d[:, :], 0.0))
        ev_gpp = p.chain("vector", [
            lambda v: v.tensor_scalar(s_Cbc[:, :], s_Cbc[:, :], 1.0 / D, None,
                                      ALU.mult)] + [
            (lambda v, ch=ch: v.tensor_tensor(s_g9[:, ch, 0:8], s_g9[:, ch, 0:8],
                                              s_Cbc[:, :], ALU.subtract))
            for ch in range(8)],
            after=[ev_g9, ev_gate], label="gpp")

        ev_sq = {}
        ev_xld = {}
        ev_xTld = {}
        ev_gmm = {}
        ev_gevac = {}
        ev_gtr = {}
        ev_Gev = {}
        ev_snap = {-1: ev_acc0}
        ev_meta = {}
        ev_st1 = {}
        ev_r1 = {}
        ev_st2 = {}
        ev_r2 = {}
        ev_lgam = {}
        ev_e8 = {}
        ev_pack = {}

        def S0(q):
            b2 = q % 2
            ev_xld[q] = p.dma("sync", s_xq[:, b2],
                              xf[q * 512:(q + 1) * 512, :]
                              .rearrange("(a p) n -> p a n", p=128),
                              after=[ev_sq.get(q - 2)], label=f"xld{q}",
                              slot=f"x{b2}")
            ev_xTld[q] = p.dma(
                "sync", s_xTq[:, b2], xT[q],
                after=[ev_gmm.get(q - 2)],
                label=f"xTld{q}", slot=f"xT{b2}")

            def gmm(t, b2=b2):
                for ch in range(8):
                    mm = t.matmul(p_gTq[:, b2], s_g9[:, ch], s_xTq[:, b2, ch],
                                  start=(ch == 0), stop=(ch == 7))
                return mm
            ev_gmm[q] = p.on("tensor", gmm,
                             after=[ev_xTld[q], ev_gpp, ev_gevac.get(q - 2)],
                             inc=("tensor", 1), label=f"gmm{q}")
            ev_gevac[q] = p.on(
                "vector",
                lambda v, b2=b2: v.tensor_copy(s_gTq[:, b2], p_gTq[:, b2]),
                after=[ev_gmm[q], ev_gtr.get(q - 2)], label=f"gev{q}")

            def gtr(t, b2=b2):
                for k in range(4):
                    mm = t.transpose(p_Gq[:, b2, 10 * k:10 * k + 9],
                                     s_gTq[:, b2, 128 * k:128 * k + 128],
                                     s_id[0:9, 0:9])
                return mm
            ev_gtr[q] = p.on("tensor", gtr,
                             after=[ev_gevac[q], ev_Gev.get(q - 2)],
                             inc=("tensor", 1), label=f"gtr{q}")

            def _gev(v, q=q, b2=b2):
                for k in range(4):
                    cp = v.tensor_copy(s_Gq[:, q % 4, 10 * k:10 * k + 9],
                                       p_Gq[:, b2, 10 * k:10 * k + 9])
                return cp
            ev_Gev[q] = p.on("vector", _gev, after=[ev_gtr[q]], label=f"Gev{q}")

        def S1(q):
            b2 = q % 2
            ev = None
            for k in range(4):
                ev = p.on("scalar",
                          lambda s, q=q, b2=b2, k=k: s.activation(
                              out=s_jk[:, k % 2], in_=s_xq[:, b2, k],
                              func=AF.Square,
                              accum_out=s_q[:, q % 4, k:k + 1]),
                          after=[ev_xld[q], ev_meta.get(q - 4)],
                          label=f"sq{q}_{k}")
            ev_sq[q] = ev

        def S2(q):
            b4 = q % 4
            qq, m8 = s_q[:, b4], s_m8[:, b4]
            G4 = s_Gq[:, b4]
            Gs = G4.rearrange("p (a b) -> p a b", b=10)[:, :, 8:9]
            m13 = m8[:, 0:4].rearrange("p (a b) -> p a b", b=1)
            ev_st1[q] = p.chain("vector", [
                lambda v: v.tensor_scalar(m13, Gs, 1.0 / D, None, ALU.mult),
                lambda v: v.tensor_tensor(qq[:, 4:8], m8[:, 0:4], m8[:, 0:4],
                                          ALU.mult),
                lambda v: v.tensor_scalar(qq[:, 8:12], qq[:, 0:4], 1.0 / D,
                                          None, ALU.mult),
                lambda v: v.tensor_tensor(qq[:, 8:12], qq[:, 8:12], qq[:, 4:8],
                                          ALU.subtract),
            ], after=[ev_sq[q], ev_Gev[q], ev_meta.get(q - 4), ev_z1],
                label=f"st1_{q}")

        def S3(q):
            b4 = q % 4
            p.on("scalar",
                 lambda s, b4=b4: s.activation(
                     out=s_q[:, b4, 4:8], in_=s_q[:, b4, 8:12],
                     func=AF.Ln, bias=s_eps[:, 0:1]),
                 after=[ev_st1[q], ev_eps], label=f"ln1_{q}")
            ev_r1[q] = p.on("scalar",
                            lambda s, b4=b4: s.activation(
                                out=s_m8[:, b4, 4:8], in_=s_q[:, b4, 4:8],
                                func=AF.Exp, scale=-0.5),
                            after=[], label=f"r1_{q}")

        def S4(q):
            b4 = q % 4
            qq, m8 = s_q[:, b4], s_m8[:, b4]
            ev_st2[q] = p.chain("vector", [
                lambda v: v.tensor_tensor(qq[:, 12:16], m8[:, 4:8], m8[:, 4:8],
                                          ALU.mult),
                lambda v: v.tensor_tensor(qq[:, 12:16], qq[:, 8:12],
                                          qq[:, 12:16], ALU.mult),
            ], after=[ev_r1[q]], label=f"st2_{q}")

        def S5(q):
            b4 = q % 4
            p.on("scalar",
                 lambda s, b4=b4: s.activation(
                     out=s_q[:, b4, 16:20], in_=s_q[:, b4, 12:16],
                     func=AF.Ln, bias=s_eps[:, 0:1]),
                 after=[ev_st2[q]], label=f"ln2_{q}")
            ev_r2[q] = p.on("scalar",
                            lambda s, b4=b4: s.activation(
                                out=s_q[:, b4, 16:20], in_=s_q[:, b4, 16:20],
                                func=AF.Exp, scale=-0.5),
                            after=[], label=f"r2_{q}")

        def S6(q):
            b4 = q % 4
            qq, m8 = s_q[:, b4], s_m8[:, b4]
            lg, oh, r8 = s_lgq[:, b4], s_ohq[:, b4], s_r8q[:, b4]
            lg3 = s_lgq[:, b4].rearrange("p (a b) -> p a b", b=8)
            oh3 = s_ohq[:, b4].rearrange("p (a b) -> p a b", b=8)
            r83 = s_r8q[:, b4].rearrange("p (a b) -> p a b", b=8)
            G3 = s_Gq[:, b4].rearrange("p (a b) -> p a b", b=10)[:, :, 0:8]
            al_b = m8[:, 8:12].rearrange("p (a b) -> p a b", b=1)\
                .broadcast_to([128, 4, 8])
            mx_b = qq[:, 20:24].rearrange("p (a b) -> p a b", b=1)\
                .broadcast_to([128, 4, 8])
            e_b = m8[:, 16:20].rearrange("p (a b) -> p a b", b=1)\
                .broadcast_to([128, 4, 8])
            Bg_b = s_Bgbc[:, :].rearrange("p (a b) -> p a b", a=1)\
                .broadcast_to([128, 4, 8])
            iok_b = s_iok[:, :].rearrange("p (a b) -> p a b", a=1)\
                .broadcast_to([128, 4, 8])
            ev_lgam[q] = p.chain("vector", [
                lambda v: v.tensor_tensor(m8[:, 8:12], m8[:, 4:8], qq[:, 16:20],
                                          ALU.mult),
                lambda v: v.tensor_tensor(lg3, G3, al_b, ALU.mult),
                lambda v: v.tensor_tensor(lg3, lg3, Bg_b, ALU.add),
                lambda v: v.tensor_reduce(qq[:, 20:24], lg3, AX.X, ALU.max),
                lambda v: v.tensor_tensor(oh3, lg3, mx_b, ALU.is_equal),
                lambda v: v.scalar_tensor_tensor(
                    oh3, oh3, s_m1000[:, 0:1], iok_b, ALU.mult, ALU.add),
                lambda v: v.tensor_reduce(m8[:, 16:20], oh3, AX.X, ALU.min),
                lambda v: v.tensor_tensor(oh3, oh3, e_b, ALU.is_equal),
                lambda v: v.tensor_tensor(r83, lg3, mx_b, ALU.subtract),
            ], after=[ev_r2[q], ev_gpp], label=f"lgam{q}")

        def S7(q):
            b4 = q % 4
            ev_e8[q] = p.on("scalar",
                            lambda s, b4=b4: s.activation(
                                out=s_r8q[:, b4], in_=s_r8q[:, b4],
                                func=AF.Exp),
                            after=[ev_lgam[q]], label=f"exp8{q}")

        def S8(q):
            b4 = q % 4
            b2 = q % 2
            qa = q % 2
            qq, m8 = s_q[:, b4], s_m8[:, b4]
            r83 = s_r8q[:, b4].rearrange("p (a b) -> p a b", b=8)
            ev8 = p.chain("vector", [
                lambda v: v.tensor_reduce(qq[:, 0:4], r83, AX.X, ALU.add),
                lambda v: v.reciprocal(qq[:, 4:8], qq[:, 0:4]),
            ], after=[ev_e8[q]], label=f"wr{q}")

            def totmm(t, b4=b4):
                for k in range(4):
                    mm = t.matmul(p_tot[:, 8 * k:8 * k + 8], s_oc[:, :],
                                  s_ohq[:, b4, 8 * k:8 * k + 8],
                                  start=True, stop=True)
                return mm
            evtm = p.on("tensor", totmm, after=[ev_lgam[q], ev_snap[q - 1]],
                        inc=("tensor", 1), label=f"tot{q}")

            def rmm0(t, b4=b4, qa=qa):
                t.matmul(p_rank[:, 0], s_or[0:1, :], s_acc2d[0:1, qa],
                         start=True, stop=False)
                return t.matmul(p_rank[:, 0], s_L[:, :],
                                s_ohq[:, b4, 0:8],
                                start=False, stop=True)
            evr0 = p.on("tensor", rmm0,
                        after=[ev_lgam[q], ev_snap[q - 1], ev_pack.get(q - 1)],
                        inc=("tensor", 1), label=f"rmm0{q}")

            evoff = p.chain("vector", [
                lambda v, b2=b2: v.tensor_copy(s_tot[:, b2], p_tot[:, :]),
                lambda v, b2=b2, qa=qa: v.tensor_tensor(
                    s_off[:, b2, 0], s_acc2d[0:1, qa], s_tot[:, b2, 0:8],
                    ALU.add),
                lambda v, b2=b2: v.tensor_tensor(
                    s_off[:, b2, 1], s_off[:, b2, 0], s_tot[:, b2, 8:16],
                    ALU.add),
                lambda v, b2=b2: v.tensor_tensor(
                    s_off[:, b2, 2], s_off[:, b2, 1], s_tot[:, b2, 16:24],
                    ALU.add),
                lambda v, b2=b2, qa=qa: v.tensor_tensor(
                    s_acc2d[0:1, 1 - qa], s_off[0:1, b2, 2],
                    s_tot[:, b2, 24:32], ALU.add),
            ], after=[evtm, ev_acc0q], label=f"off{q}")
            ev_snap[q] = evoff

            def rmm(t, b4=b4, b2=b2):
                for k in range(1, 4):
                    t.matmul(p_rank[:, k], s_or[0:1, :],
                             s_off[0:1, b2, k - 1],
                             start=True, stop=False)
                    mm = t.matmul(p_rank[:, k], s_L[:, :],
                                  s_ohq[:, b4, 8 * k:8 * k + 8],
                                  start=False, stop=True)
                return mm
            evr = p.on("tensor", rmm, after=[evoff], inc=("tensor", 1),
                       label=f"rmm{q}")
            oh3 = s_ohq[:, b4].rearrange("p (a b) -> p a b", b=8)
            keep_b = qq[:, 8:12]
            evx = p.chain("vector", [
                lambda v: v.tensor_tensor(
                    s_r8q[:, b4].rearrange("p (a b) -> p a b", b=8),
                    p_rank[:, :, :], oh3, ALU.mult),
                lambda v: v.tensor_reduce(
                    m8[:, 20:24],
                    s_r8q[:, b4].rearrange("p (a b) -> p a b", b=8),
                    AX.X, ALU.add),
                lambda v: v.tensor_scalar(keep_b, m8[:, 20:24], float(CAP),
                                          None, ALU.is_lt),
                lambda v: v.tensor_tensor(m8[:, 12:16], qq[:, 4:8], keep_b,
                                          ALU.mult),
                lambda v, b2=b2: v.tensor_copy(
                    s_mp[:, b2],
                    s_m8[:, b4].rearrange("p (c k) -> p k c", k=4)),
            ], after=[evr, evr0, ev8, ev_meta.get(q - 2)], label=f"rx{q}")
            ev_pack[q] = evx
            ev_meta[q] = p.dma("gpsimd",
                               meta_d[q * 512:(q + 1) * 512, :]
                               .rearrange("(k p) b -> p k b", p=128),
                               s_mp[:, b2],
                               after=[evx], label=f"meta{q}",
                               slot=f"meta{q % 2}")

        stages = [S0, S1, S2, S3, S4, S5, S6, S7, S8]
        SKEW = [0, 1, 1, 2, 2, 3, 3, 4, 4]
        for step in range(NQ + 5):
            for st_fn, off in zip(stages, SKEW):
                qi = step - off
                if 0 <= qi < NQ:
                    st_fn(qi)

        ev_meta_all = [ev_meta[NQ - 2], ev_meta[NQ - 1]]

    with SC() as c2:
        s_cw1b = c2.enter_context(nc.sbuf_tensor("c_cw1b", [128, 8, 512], BF16))
        ev_cw1 = p.dma("gpsimd", s_cw1b[:], cw1T.rearrange("(c q) n -> q c n", q=128), slot="cw1")
        sb2 = lambda n, sh, dt=F32: c2.enter_context(nc.sbuf_tensor(n, sh, dt))
        s_ox = sb2("ow_x", [128, 2, D])
        s_om = sb2("ow_m", [128, 2, 8])
        s_u = sb2("ow_u", [128, D])
        s_t2b = c2.enter_context(nc.sbuf_tensor("ow_t2", [128, 2, D], BF16))
        s_t2T = c2.enter_context(nc.sbuf_tensor("ow_t2T", [128, 8, 128], BF16))
        s_hdnT = c2.enter_context(nc.sbuf_tensor("ow_hdn", [128, 4, 128], BF16))
        s_sc2 = sb2("ow_sc", [128, 16])
        s_oh2 = sb2("ow_oh", [128, 8])
        s_r82 = sb2("ow_r8", [128, 8])
        p_t2T = c2.enter_context(nc.psum_tensor("p_t2T", [128, 128], BF16))
        p_cf = c2.enter_context(nc.psum_tensor("p_cf", [128, 128], F32))
        p_c2 = c2.enter_context(nc.psum_tensor("p_c2", [128, 1], F32))
        p_cp = c2.enter_context(nc.psum_tensor("p_cp", [128, 1], F32))
        p_rank2 = c2.enter_context(nc.psum_tensor("p_rank2", [128, 8], F32))
        p_acc2p = c2.enter_context(nc.psum_tensor("p_acc2p", [1, 8], F32))

        ev_snap2 = {-1: ev_acc20}
        ev_xn_prev = {}
        fence2 = p.fence()
        ev_scat = {}
        ev_u_by_j = {}
        ev_si_by_j = {}
        ev_t2_by_j = {}
        prev_t2T_evac = None
        ev_conf_prev = None
        ev_comp_prev = None
        ev_sg_prev = None
        ev_sp_prev = None
        eva2_prev = None
        for j in range(OWN_TILES):
            b2 = j % 2
            evx_ = p.on("gpsimd",
                        lambda g, b2=b2, j=j: g.indirect_dma_start(
                            out=s_ox[:, b2], out_offset=None, in_=xf[:, :],
                            in_offset=IndirectOffsetOnAxis(s_ownrow[:, j:j + 1], 0)),
                        after=[ev_u_by_j.get(j - 2), ev_xn_prev.get(j - 2)]
                        + evc + fence2,
                        inc=("d_ox%d" % b2, 16), label=f"oxg{j}")
            evm_ = p.on("gpsimd",
                        lambda g, b2=b2, j=j: g.indirect_dma_start(
                            out=s_om[:, b2], out_offset=None, in_=meta_d[:, :],
                            in_offset=IndirectOffsetOnAxis(s_ownrow[:, j:j + 1], 0)),
                        after=ev_meta_all + [ev_si_by_j.get(j - 2)] + fence2,
                        inc=("d_om%d" % b2, 16), label=f"omg{j}")

            om_ = s_om[:, b2]
            evp = p.chain("vector", [
                lambda v, j=j, om=om_: v.tensor_scalar(
                    s_nm1r1[:, j:j + 1], om[:, 0:1], -1.0, None, ALU.mult),
                lambda v, j=j, om=om_: v.tensor_copy(
                    s_r1own[:, j:j + 1], om[:, 1:2]),
            ], after=[evm_, ev_u_by_j.get(j - 1)], label=f"prep{j}")

            ev_xn = p.on("vector",
                         lambda v, j=j, b2=b2: v.tensor_scalar(
                             s_xnown[:, j], s_ox[:, b2],
                             s_nm1r1[:, j:j + 1], s_om[:, b2, 1:2],
                             ALU.add, ALU.mult),
                         after=[evx_, evp], label=f"xn{j}")
            ev_xn_prev[j] = ev_xn
            ev_u = p.on("vector",
                        lambda v, j=j, b2=b2: v.tensor_scalar(
                            s_u[:, :], s_ox[:, b2],
                            s_nm1r1[:, j:j + 1], s_om[:, b2, 2:3],
                            ALU.add, ALU.mult),
                        after=[evx_, evp, ev_t2_by_j.get(j - 1)], label=f"u{j}")
            ev_u_by_j[j] = ev_u

            ev_t2 = p.chain("vector", [
                lambda v: v.tensor_tensor(s_u[:, :], s_u[:, :], s_gbc[:, :],
                                          ALU.mult),
                lambda v, b2=b2: v.tensor_tensor(s_t2b[:, b2], s_u[:, :],
                                                 s_bbc[:, :], ALU.add),
            ], after=[ev_u, ev_scat.get(j - 2)], label=f"t2{j}")
            ev_t2_by_j[j] = ev_t2

            tr_evs = []
            for ch in range(8):
                evtr = p.on("tensor",
                            lambda t, b2=b2, ch=ch: t.transpose(
                                p_t2T[:, :], s_t2b[:, b2, ch * 128:(ch + 1) * 128],
                                s_idb[:, :]),
                            after=[ev_t2, tr_evs[-1] if tr_evs else prev_t2T_evac],
                            inc=("tensor", 1), label=f"t2T{j}_{ch}")
                evev = p.on("vector",
                            lambda v, ch=ch: v.tensor_copy(s_t2T[:, ch],
                                                           p_t2T[:, :]),
                            after=[evtr], label=f"t2Te{j}_{ch}")
                tr_evs.append(evev)
            prev_t2T_evac = tr_evs[-1]
            t2T_done = tr_evs[-1]

            cf_evs = []
            for hc in range(4):
                def cfm(t, hc=hc):
                    for ch in range(8):
                        mm = t.matmul(p_cf[:, :],
                                      s_cw1b[:, ch, hc * 128:(hc + 1) * 128],
                                      s_t2T[:, ch], start=(ch == 0), stop=(ch == 7))
                    return mm
                evcf = p.on("tensor", cfm,
                            after=[t2T_done, ev_cw1,
                                   cf_evs[-1] if cf_evs else ev_conf_prev],
                            inc=("tensor", 1), label=f"cf{j}_{hc}")
                evrl = p.on("scalar",
                            lambda s, hc=hc: s.activation(
                                out=s_hdnT[:, hc], in_=p_cf[:, :], func=AF.Relu,
                                bias=s_cb1[:, hc:hc + 1]),
                            after=[evcf], label=f"rl{j}_{hc}")
                cf_evs.append(evrl)

            def cf2(t):
                for hc in range(4):
                    mm = t.matmul(p_c2[:, :], s_hdnT[:, hc], s_cw2b[:, hc:hc + 1],
                                  start=(hc == 0), stop=(hc == 3))
                return mm
            evc2 = p.on("tensor", cf2, after=cf_evs + [ev_sg_prev],
                        inc=("tensor", 1), label=f"cf2{j}")
            evsg = p.on("scalar",
                        lambda s: s.activation(out=s_sc2[:, 2:3], in_=p_c2[:, :],
                                               func=AF.Sigmoid,
                                               bias=s_cb2[:, 0:1]),
                        after=[evc2, ev_conf_prev], label=f"sg{j}")
            ev_sg_prev = evsg
            ev_conf_prev = p.on(
                "vector",
                lambda v: v.tensor_tensor(s_confacc[:, :], s_confacc[:, :],
                                          s_sc2[:, 2:3], ALU.add),
                after=[evsg, ev_cf0], label=f"cfa{j}")

            def cpm(t):
                for ch in range(8):
                    mm = t.matmul(p_cp[:, :], s_t2T[:, ch], s_compwb[:, ch:ch + 1],
                                  start=(ch == 0), stop=(ch == 7))
                return mm
            evcp = p.on("tensor", cpm, after=[t2T_done, ev_sp_prev],
                        inc=("tensor", 1), label=f"cp{j}")
            evsp = p.on("scalar",
                        lambda s: s.activation(out=s_sc2[:, 3:4], in_=p_cp[:, :],
                                               func=AF.Sigmoid,
                                               bias=s_compb[:, 0:1]),
                        after=[evcp, ev_comp_prev], label=f"sgp{j}")
            ev_sp_prev = evsp
            ev_comp_prev = p.on(
                "vector",
                lambda v: v.tensor_tensor(s_compacc[:, :], s_compacc[:, :],
                                          s_sc2[:, 3:4], ALU.add),
                after=[evsp, ev_cp0], label=f"cpa{j}")

            evrt = p.chain("vector", [
                lambda v, om=om_: v.tensor_scalar(
                    s_sc2[:, 4:5], om[:, 5:6], float(CAP), None, ALU.is_lt),
                lambda v, om=om_: v.tensor_scalar(
                    s_oh2[:, :], s_io8[:, :], om[:, 4:5], None, ALU.is_equal),
                lambda v: v.tensor_scalar(
                    s_oh2[:, :], s_oh2[:, :], s_sc2[:, 4:5], None, ALU.mult),
            ], after=[evm_, eva2_prev], label=f"rt{j}")

            def rmm2(t):
                t.matmul(p_rank2[:, :], s_or[0:1, :], s_acc2[0:1, :],
                         start=True, stop=False)
                return t.matmul(p_rank2[:, :], s_L[:, :], s_oh2[:, :],
                                start=False, stop=True)
            evr2 = p.on("tensor", rmm2, after=[evrt, ev_snap2[j - 1]],
                        inc=("tensor", 1), label=f"rmm2{j}")

            def accmm2(t):
                return t.matmul(p_acc2p[:, :], s_oc[:, :], s_oh2[:, :],
                                start=True, stop=True)
            eva2 = p.on("tensor", accmm2, after=[evr2], inc=("tensor", 1),
                        label=f"acc2{j}")
            eva2_prev = eva2

            evsi = p.chain("vector", [
                lambda v: v.tensor_tensor(s_r82[:, :], p_rank2[:, :],
                                          s_oh2[:, :], ALU.mult),
                lambda v: v.reduce_sum(s_sc2[:, 5:6], s_r82[:, :], AX.X),
                lambda v: v.tensor_scalar(s_sc2[:, 6:7], s_sc2[:, 5:6],
                                          float(C_LOC), None, ALU.is_lt),
                lambda v: v.tensor_tensor(s_sc2[:, 7:8], s_sc2[:, 4:5],
                                          s_sc2[:, 6:7], ALU.mult),
                lambda v, j=j, om=om_: v.tensor_tensor(
                    s_wown[:, j:j + 1], om[:, 3:4], s_sc2[:, 6:7], ALU.mult),
                lambda v, om=om_: v.tensor_scalar(
                    s_sc2[:, 8:9], om[:, 4:5], float(C_LOC), None, ALU.mult),
                lambda v: v.tensor_tensor(s_sc2[:, 8:9], s_sc2[:, 8:9],
                                          s_sc2[:, 5:6], ALU.add),
                lambda v: v.tensor_scalar(s_sc2[:, 9:10], s_sc2[:, 7:8],
                                          -1.0, -1.0, ALU.add, ALU.mult),
                lambda v, j=j: v.tensor_tensor(
                    s_sc2[:, 10:11], s_sc2[:, 9:10], s_bigio[:, j:j + 1],
                    ALU.mult),
                lambda v: v.tensor_tensor(s_sc2[:, 11:12], s_sc2[:, 8:9],
                                          s_sc2[:, 10:11], ALU.add),
                lambda v, j=j: v.tensor_copy(s_sidx[:, j:j + 1],
                                             s_sc2[:, 11:12]),
                lambda v: v.tensor_tensor(s_sc2[:, 12:13], s_sc2[:, 5:6],
                                          s_sc2[:, 7:8], ALU.mult),
                lambda v, om=om_: v.tensor_scalar(
                    s_sc2[:, 13:14], om[:, 4:5], float(C_LOC), None, ALU.mult),
                lambda v: v.tensor_tensor(s_sc2[:, 13:14], s_sc2[:, 13:14],
                                          s_sc2[:, 12:13], ALU.add),
                lambda v, j=j: v.tensor_copy(s_gidx[:, j:j + 1],
                                             s_sc2[:, 13:14]),
            ], after=[evr2, evrt], label=f"si{j}")
            ev_si_by_j[j] = evsi
            ev_snap2[j] = p.on(
                "vector", lambda v: v.tensor_tensor(s_acc2[:, :], s_acc2[:, :],
                                                    p_acc2p[:, :], ALU.add),
                after=[eva2], label=f"snap2{j}")

            ev_scat[j] = p.on(
                "gpsimd",
                lambda g, j=j, b2=b2: g.indirect_dma_start(
                    out=disp[:, :],
                    out_offset=IndirectOffsetOnAxis(s_sidx[:, j:j + 1], 0),
                    in_=s_t2b[:, b2], in_offset=None,
                    bounds_check=E * C_LOC - 1, oob_is_err=False),
                after=[evsi, ev_t2, ev_dz_last],
                inc=("d_scat%d" % b2, 16), label=f"scat{j}")

        ev_scat_all = [ev_scat[OWN_TILES - 2], ev_scat[OWN_TILES - 1]]
        ev_xn_all = p.now("vector")
        if DEBUG_OUTS:
            p.dma("gpsimd", dbg_xn2[0], s_xnown.rearrange("p a b -> p (a b)"),
                  after=[ev_xn_all], label="dbgxn0", slot="dbgx")
            p.dma("gpsimd", dbg_idx[:, 0:OWN_TILES], s_sidx[:, :],
                  after=[ev_xn_all] + ev_scat_all, label="dbgsi", slot="dbgx")
            p.dma("gpsimd", dbg_idx[:, OWN_TILES:], s_gidx[:, :],
                  after=[ev_xn_all] + ev_scat_all, label="dbggi", slot="dbgx")
            p.dma("gpsimd", dbg_w[:, :], s_wown[:, :],
                  after=[ev_xn_all] + ev_scat_all, label="dbgw", slot="dbgx")
    pre3.close()

    with SC() as c4:
        s_dT = c4.enter_context(nc.sbuf_tensor("ff_dT", [128, 8, C_LOC], BF16))
        s_dtok = c4.enter_context(nc.sbuf_tensor("ff_dtok", [128, 4, D], BF16))
        s_wib = c4.enter_context(nc.sbuf_tensor("ff_wib", [128, 2, 8, 512], BF16))
        s_hT = c4.enter_context(nc.sbuf_tensor("ff_hT", [128, 32, C_LOC], BF16))
        s_wob = c4.enter_context(nc.sbuf_tensor("ff_wob", [128, 2, 32, 128], BF16))
        s_eoT = c4.enter_context(nc.sbuf_tensor("ff_eoT", [128, 8, 512], F32))
        s_eot = c4.enter_context(nc.sbuf_tensor("ff_eot", [128, 2, D], F32))
        p_h = c4.enter_context(nc.psum_tensor("p_h", [128, 2, 512], F32))
        p_oT = c4.enter_context(nc.psum_tensor("p_oT", [128, 2, 512], F32))
        p_tr = c4.enter_context(nc.psum_tensor("p_tr", [128, 128], F32))
        p_trb = c4.enter_context(nc.psum_tensor("p_trb", [128, 128], BF16))

        fence4 = p.fence()
        eo_last = None
        dtok_guard = None
        dT_guard = None
        eoT_guard = None
        wi_guard = {}
        wo_guard = {}
        gel_ring = []
        oev_ring = []
        trb_evac = None
        tr_evac = None
        eot_dma = {}
        for e in range(E):
            dt_evs = []
            for st4 in range(4):
                evd = p.dma("sync", s_dtok[:, st4],
                            disp[e * C_LOC + st4 * 128:
                                 e * C_LOC + (st4 + 1) * 128, :],
                            after=ev_scat_all + [dtok_guard] + fence4,
                            label=f"dld{e}{st4}", slot=f"dld{st4}")
                for ch in range(8):
                    evt = p.on("tensor",
                               lambda t, st4=st4, ch=ch: t.transpose(
                                   p_trb[:, :],
                                   s_dtok[:, st4, ch * 128:(ch + 1) * 128],
                                   s_idb[:, :]),
                               after=[evd, trb_evac, dT_guard],
                               inc=("tensor", 1), label=f"dtr{e}{st4}{ch}")
                    eve = p.on("vector",
                               lambda v, st4=st4, ch=ch: v.tensor_copy(
                                   s_dT[:, ch, st4 * 128:(st4 + 1) * 128],
                                   p_trb[:, :]),
                               after=[evt], label=f"dtre{e}{st4}{ch}")
                    trb_evac = eve
                    dt_evs.append(eve)
                    dtok_guard = evt
            dT_done = dt_evs[-1]

            gel_evs = []
            for hg in range(8):
                bw = hg % 2
                evw = p.dma("gpsimd", s_wib[:, bw], wit[e, hg],
                            after=[wi_guard.get(bw)] + fence4,
                            label=f"wi{e}{hg}", slot=f"wi{bw}")
                for sub in range(4):
                    hc = hg * 4 + sub
                    bh = hc % 2

                    def hmm(t, bw=bw, sub=sub, bh=bh):
                        for ch in range(8):
                            mm = t.matmul(
                                p_h[:, bh],
                                s_wib[:, bw, ch, sub * 128:(sub + 1) * 128],
                                s_dT[:, ch, :],
                                start=(ch == 0), stop=(ch == 7))
                        return mm
                    evh = p.on("tensor", hmm,
                               after=[evw, dT_done,
                                      gel_ring[-2] if len(gel_ring) >= 2 else None],
                               inc=("tensor", 1), label=f"h{e}{hc}")
                    wi_guard[bw] = evh
                    dT_guard = evh
                    evg = p.on("scalar",
                               lambda s, hc=hc, bh=bh: s.activation(
                                   out=s_hT[:, hc], in_=p_h[:, bh],
                                   func=AF.Gelu),
                               after=[evh], label=f"g{e}{hc}")
                    gel_evs.append(evg)
                    gel_ring.append(evg)
            h_done = gel_evs[-1]

            oc_evs = []
            for oc in range(8):
                bw = oc % 2
                bh = oc % 2
                evw = p.dma("gpsimd", s_wob[:, bw], wot[e, oc],
                            after=[wo_guard.get(bw)] + fence4,
                            label=f"wo{e}{oc}", slot=f"wo{bw}")

                def omm(t, bw=bw, bh=bh):
                    for hc in range(32):
                        mm = t.matmul(p_oT[:, bh], s_wob[:, bw, hc],
                                      s_hT[:, hc, :],
                                      start=(hc == 0), stop=(hc == 31))
                    return mm
                evo = p.on("tensor", omm,
                           after=[evw, h_done,
                                  oev_ring[-2] if len(oev_ring) >= 2 else None],
                           inc=("tensor", 1), label=f"o{e}{oc}")
                wo_guard[bw] = evo
                evoe = p.on("vector",
                            lambda v, oc=oc, bh=bh: v.tensor_copy(
                                s_eoT[:, oc], p_oT[:, bh]),
                            after=[evo, eoT_guard], label=f"oev{e}{oc}")
                oc_evs.append(evoe)
                oev_ring.append(evoe)
            oT_done = oc_evs[-1]

            for st4 in range(4):
                b2 = st4 % 2
                tr_last = None
                for ch in range(8):
                    evt = p.on("tensor",
                               lambda t, st4=st4, ch=ch: t.transpose(
                                   p_tr[:, :],
                                   s_eoT[:, ch, st4 * 128:(st4 + 1) * 128],
                                   s_id[:, :]),
                               after=[oT_done, tr_evac],
                               inc=("tensor", 1), label=f"etr{e}{st4}{ch}")
                    eoT_guard = evt
                    eve = p.on("vector",
                               lambda v, b2=b2, ch=ch: v.tensor_copy(
                                   s_eot[:, b2, ch * 128:(ch + 1) * 128],
                                   p_tr[:, :]),
                               after=[evt, eot_dma.get(b2)],
                               label=f"etre{e}{st4}{ch}")
                    tr_evac = eve
                    tr_last = eve
                evdma = p.dma("gpsimd",
                              eo[e * C_LOC + st4 * 128:
                                 e * C_LOC + (st4 + 1) * 128, :],
                              s_eot[:, b2], after=[tr_last],
                              label=f"eody{e}{st4}", slot=f"eo{b2}")
                eot_dma[b2] = evdma
        ev_eo_all = [eot_dma[0], eot_dma[1]]

    st = dict(last_tb_evac=None, last_pj_evac=None, last_rot_evac=None,
              last_pair_pe=None, last_exp=None, prev_cb=None, ev_y_last=None,
              o_evac=None)
    s_wo2g = c23.enter_context(nc.sbuf_tensor("f_wo2g", [128, 8, D], BF16))
    ev_wo2g = p.dma("sync", s_wo2g[:], wato_bf,
                    after=[ev_pc_qkv[3]] + p.fence(), label="wo2g",
                    slot="awo2")
    prev_cbs = {}
    SHUF = []
    for _i in range(0, 32, 2):
        SHUF += [_i + 1, _i]
    prev_y = {}
    prev_xn2 = {}

    def emit_seq(sq):
        fenceA = p.fence()
        with SC() as cs:
            s_catT = cs.enter_context(nc.sbuf_tensor(f"catT{sq}", [128, 8, S],
                                                     BF16))
            with SC() as c3:
                s_xnT = c3.enter_context(nc.sbuf_tensor(f"at_xnT{sq}", [128, 8, S],
                                                        BF16))
                s_wqk = c3.enter_context(nc.sbuf_tensor(f"at_wqk{sq}",
                                                        [128, 2, 8, 128], BF16))
                s_wvp = c3.enter_context(nc.sbuf_tensor(f"at_wv{sq}", [128, 8, 128],
                                                        BF16))
                s_qT = c3.enter_context(nc.sbuf_tensor(f"at_qT{sq}", [128, S], BF16))
                s_kT = c3.enter_context(nc.sbuf_tensor(f"at_kT{sq}", [128, S], BF16))
                s_vh = c3.enter_context(nc.sbuf_tensor(f"at_vh{sq}", [128, 16, 130],
                                                       BF16))
                s_qe = c3.enter_context(nc.sbuf_tensor(f"at_qe{sq}", [128, 2, 512], BF16))
                s_t1 = c3.enter_context(nc.sbuf_tensor(f"at_t1{sq}", [128, 2, 512], BF16))
                s_PT = c3.enter_context(nc.sbuf_tensor(f"at_PT{sq}", [128, 16, 512],
                                                       BF16))
                p_pj = c3.enter_context(nc.psum_tensor(f"p_pj{sq}",
                                                        [128, 2, 512], F32))
                p_ST = c3.enter_context(nc.psum_tensor(f"p_ST{sq}",
                                                       [128, 2, 2, 512], F32))
                p_O = c3.enter_context(nc.psum_tensor(f"p_O{sq}",
                                                      [128, 4, 65], F32))
                p_tb = c3.enter_context(nc.psum_tensor(f"p_tb{sq}",
                                                       [128, 8, 128], BF16))
                s_Osl = c3.enter_context(nc.sbuf_tensor(f"at_Osl{sq}",
                                                        [128, 16, 65], F32))
                s_Ons = c3.enter_context(nc.sbuf_tensor(f"at_Ons{sq}",
                                                        [128, 16, 64], BF16))
                s_rcl16 = c3.enter_context(nc.sbuf_tensor(f"at_rcl{sq}",
                                                          [128, 16], F32))

                prev_evac = st['last_tb_evac']
                for t16 in range(16):
                    jj = sq * 16 + t16

                    def xtr(t, jj=jj):
                        for ch in range(8):
                            mm = t.transpose(p_tb[:, ch, :],
                                             s_xnown[:, jj, ch * 128:(ch + 1) * 128],
                                             s_idb[:, :])
                        return mm
                    evt = p.on("tensor", xtr,
                               after=[ev_xn_all, prev_evac] + fenceA,
                               inc=("tensor", 1), label=f"xnT{sq}_{t16}")
                    prev_evac = p.on(
                        "vector",
                        lambda v, t16=t16: v.tensor_copy(
                            s_xnT[:, 0:8, t16 * 128:(t16 + 1) * 128],
                            p_tb[:, :, :]),
                        after=[evt], label=f"xnTe{sq}_{t16}")
                xnT_ready = prev_evac
                st['last_tb_evac'] = prev_evac

                for pr in range(8):
                    evwq = p.dma("sync", s_wqk[:, 0], wqkv_bf[0, pr],
                                 after=[st['last_pair_pe'], ev_pc_qkv[0]]
                                 + fenceA,
                                 label=f"wq{sq}_{pr}", slot="awq")
                    evwk = p.dma("sync", s_wqk[:, 1], wqkv_bf[1, pr],
                                 after=[st['last_pair_pe'], ev_pc_qkv[1]]
                                 + fenceA,
                                 label=f"wk{sq}_{pr}", slot="awk")
                    evwv = p.dma("sync", s_wvp[:], wqkv_bf[2, pr],
                                 after=[st['last_pair_pe'], ev_pc_qkv[2]]
                                 + fenceA,
                                 label=f"wv{sq}_{pr}", slot="awv")

                    rope_done = None
                    unit_dve = {}
                    uidx = 0
                    for qk in range(2):
                        dstT = s_qT if qk == 0 else s_kT
                        for gq in range(4):
                            ub = uidx % 2

                            def pj(t, qk=qk, gq=gq, ub=ub):
                                for ch in range(8):
                                    mm = t.matmul(
                                        p_pj[:, ub], s_wqk[:, qk, ch],
                                        s_xnT[:, ch, gq * 512:(gq + 1) * 512],
                                        start=(ch == 0), stop=(ch == 7))
                                return mm
                            vr = st.get('v_ring')
                            evpj = p.on("tensor", pj,
                                        after=[xnT_ready, evwq, evwk,
                                               unit_dve.get(uidx - 2),
                                               vr[uidx] if (vr and uidx < 2)
                                               else None],
                                        inc=("tensor", 1),
                                        label=f"pj{sq}{pr}{qk}{gq}")
                            evac = p.on("vector",
                                        lambda v, ub=ub: v.tensor_copy(
                                            s_qe[:, ub], p_pj[:, ub]),
                                        after=[evpj],
                                        label=f"pjev{sq}{pr}{qk}{gq}")
                            evch = p.chain("vector", [
                                lambda v, ub=ub: v.stream_shuffle(
                                    s_t1[:, ub], s_qe[:, ub], SHUF),
                                lambda v, ub=ub, gq=gq: v.scalar_tensor_tensor(
                                    s_t1[:, ub], s_t1[:, ub], s_sgn[:, 0:1],
                                    s_sin[:, gq * 512:(gq + 1) * 512],
                                    ALU.mult, ALU.mult),
                                lambda v, ub=ub, gq=gq, dstT=dstT:
                                v.tensor_tensor(
                                    dstT[:, gq * 512:(gq + 1) * 512],
                                    s_qe[:, ub],
                                    s_cos[:, gq * 512:(gq + 1) * 512], ALU.mult),
                                lambda v, ub=ub, gq=gq, dstT=dstT:
                                v.tensor_tensor(
                                    dstT[:, gq * 512:(gq + 1) * 512],
                                    s_t1[:, ub],
                                    dstT[:, gq * 512:(gq + 1) * 512], ALU.add),
                            ], after=[st['last_pair_pe']],
                                label=f"rope{sq}{pr}{qk}{gq}")
                            unit_dve[uidx] = evac
                            rope_done = evch
                            uidx += 1
                    qk_done = rope_done

                    evvz = p.on("vector",
                                lambda v: v.memset(s_vh[:, :, 64:65], 1.0),
                                after=[st['last_pair_pe']], label="vones")
                    evvz2 = p.on("vector",
                                 lambda v: v.memset(s_vh[:, :, 129:130], 1.0),
                                 after=[st['last_pair_pe']], label="vones2")
                    v_last = None
                    v_prev = {}
                    for kc in range(16):
                        vb = kc % 2

                        def vpj(t, kc=kc, vb=vb):
                            for ch in range(8):
                                mm = t.matmul(
                                    p_pj[:, vb, 0:128],
                                    s_xnT[:, ch, kc * 128:(kc + 1) * 128],
                                    s_wvp[:, ch],
                                    start=(ch == 0), stop=(ch == 7))
                            return mm
                        evv = p.on("tensor", vpj,
                                   after=[xnT_ready, evwv, v_prev.get(kc - 2),
                                          unit_dve.get(6 + kc) if kc < 2
                                          else None],
                                   inc=("tensor", 1), label=f"v{sq}{pr}{kc}")
                        evve = p.chain("vector", [
                            lambda v, kc=kc, vb=vb: v.tensor_copy(
                                s_vh[:, kc, 0:64], p_pj[:, vb, 0:64]),
                            lambda v, kc=kc, vb=vb: v.tensor_copy(
                                s_vh[:, kc, 65:129], p_pj[:, vb, 64:128]),
                        ], after=[evv, evvz, evvz2], label="vev")
                        v_prev[kc] = evve
                        v_last = evve
                    v_done = v_last
                    st['v_ring'] = (v_prev[14], v_prev[15])

                    pair_pe_events = []
                    for hh in range(2):
                        pb = hh * 64
                        vsl = (0, 65) if hh == 0 else (65, 130)
                        for gq in range(4):
                            nch = 4 * gq + 4
                            exp_evs = {}
                            bidx = 0
                            for cb in range(0, nch, 2):
                                nb = min(2, nch - cb)
                                sb_ = bidx % 2

                                def stm(t, cb=cb, nb=nb, gq=gq, pb=pb, sb_=sb_):
                                    for kk in range(nb):
                                        kc = cb + kk
                                        mm = t.matmul(
                                            p_ST[:, sb_, kk],
                                            s_kT[pb:pb + 64,
                                                 kc * 128:(kc + 1) * 128],
                                            s_qT[pb:pb + 64,
                                                 gq * 512:(gq + 1) * 512],
                                            start=True, stop=True)
                                    return mm
                                evst = p.on("tensor", stm,
                                            after=[qk_done, v_done,
                                                   exp_evs.get(bidx - 2,
                                                               st['last_exp'])],
                                            inc=("tensor", 1),
                                            label=f"stm{sq}{pr}{hh}{gq}{cb}")
                                evex = p.on("scalar",
                                            lambda s, cb=cb, nb=nb, sb_=sb_:
                                            s.activation(
                                                out=s_PT[:, cb:cb + nb]
                                                .rearrange("p a b -> p (a b)"),
                                                in_=p_ST[:, sb_, 0:nb]
                                                .rearrange("p a b -> p (a b)"),
                                                func=AF.Exp, scale=0.125),
                                            after=[evst], label=f"ex{cb}")
                                exp_evs[bidx] = evex
                                bidx += 1
                            st['last_exp'] = exp_evs[bidx - 1]
                            mask_evs = []
                            for jd in range(4):
                                kc = 4 * gq + jd
                                evmk = p.on("vector",
                                            lambda v, kc=kc, jd=jd:
                                            v.tensor_tensor(
                                                s_PT[:, kc,
                                                     jd * 128:(jd + 1) * 128],
                                                s_PT[:, kc,
                                                     jd * 128:(jd + 1) * 128],
                                                s_tri[:, :], ALU.mult),
                                            after=[exp_evs[bidx - 1]],
                                            label=f"mk{kc}")
                                mask_evs.append(evmk)
                            for qt in range(4):
                                nk = 4 * gq + qt + 1

                                def pv(t, nk=nk, qt=qt, vsl=vsl):
                                    for kc in range(nk):
                                        mm = t.matmul(
                                            p_O[:, qt],
                                            s_PT[:, kc,
                                                 qt * 128:(qt + 1) * 128],
                                            s_vh[:, kc, vsl[0]:vsl[1]],
                                            start=(kc == 0),
                                            stop=(kc == nk - 1))
                                    return mm
                                evpv = p.on("tensor", pv,
                                            after=mask_evs + [st['o_evac']],
                                            inc=("tensor", 1),
                                            label=f"pv{sq}{pr}{hh}{gq}{qt}")
                                pair_pe_events.append(evpv)
                            st['o_evac'] = p.on(
                                "vector",
                                lambda v, gq=gq: v.tensor_copy(
                                    s_Osl[:, gq * 4:(gq + 1) * 4, :],
                                    p_O[:, :, :]),
                                after=[evpv], label=f"oev{sq}{pr}{hh}{gq}")
                        evn = p.chain("vector", [
                            lambda v: v.reciprocal(s_rcl16[:, :],
                                                   s_Osl[:, :, 64:65]
                                                   .rearrange("p a b -> p (a b)")),
                            lambda v: v.tensor_tensor(
                                s_Ons[:, :, :], s_Osl[:, :, 0:64],
                                s_rcl16[:, :].rearrange("p (a b) -> p a b", b=1)
                                .broadcast_to([128, 16, 64]), ALU.mult),
                        ], after=[st['o_evac']], label=f"onrm{sq}{pr}{hh}")
                        for grp in range(2):
                            def otr(t, grp=grp):
                                for k8 in range(8):
                                    mm = t.transpose(
                                        p_tb[0:64, k8, :],
                                        s_Ons[:, grp * 8 + k8, :],
                                        s_idb[:, :])
                                return mm
                            evot = p.on("tensor", otr,
                                        after=[evn, st['last_tb_evac']],
                                        inc=("tensor", 1),
                                        label=f"otr{sq}{pr}{hh}{grp}")
                            st['last_tb_evac'] = p.on(
                                "vector",
                                lambda v, pb=pb, pr=pr, grp=grp: v.tensor_copy(
                                    s_catT[pb:pb + 64, pr,
                                           grp * 1024:(grp + 1) * 1024],
                                    p_tb[0:64, :, :]
                                    .rearrange("p a b -> p (a b)")),
                                after=[evot], label=f"oevac{sq}{pr}{hh}{grp}")
                    st['last_pair_pe'] = pair_pe_events[-1]
                ev_attn_done = st['last_tb_evac']
                if DEBUG_OUTS:
                    p.dma("gpsimd", dbg_cat[sq],
                          s_catT.rearrange("p a b -> p (a b)"),
                          after=[ev_attn_done], label=f"dbgcat{sq}",
                          slot="dbgcat")

            fence5 = p.fence()
            with SC() as c5:
                s_fx = c5.enter_context(nc.sbuf_tensor(f"f_x{sq}", [128, 2, D], F32))
                s_eog = c5.enter_context(nc.sbuf_tensor(f"f_eog{sq}", [128, 2, D],
                                                        F32))
                s_xn2 = c5.enter_context(nc.sbuf_tensor(f"f_xn{sq}", [128, D], F32))
                s_yt = c5.enter_context(nc.sbuf_tensor(f"f_y{sq}", [128, 2, D], F32))
                p_w1 = c5.enter_context(nc.psum_tensor(f"p_w1{sq}",
                                                       [128, 2, 2, 512], F32))
                for t16 in range(16):
                    j = sq * 16 + t16
                    b2 = j % 2
                    evfx = p.on("gpsimd",
                                lambda g, j=j, b2=b2: g.indirect_dma_start(
                                    out=s_fx[:, b2], out_offset=None,
                                    in_=xf[:, :],
                                    in_offset=IndirectOffsetOnAxis(
                                        s_ownrow[:, j:j + 1], 0)),
                                after=[prev_y.get(j - 2),
                                       prev_xn2.get(j - 1)] + fence5,
                                inc=("d_fx%d" % b2, 16), label=f"fx{j}")
                    evge = p.on("gpsimd",
                                lambda g, j=j, b2=b2: g.indirect_dma_start(
                                    out=s_eog[:, b2], out_offset=None,
                                    in_=eo[:, :],
                                    in_offset=IndirectOffsetOnAxis(
                                        s_gidx[:, j:j + 1], 0)),
                                after=ev_eo_all + [prev_y.get(j - 2)] + fence5,
                                inc=("d_ge%d" % b2, 16), label=f"ge{j}")
                    wo_evs = []
                    for half in range(2):
                        def wmm(t, t16=t16, half=half, b2=b2):
                            for ch in range(8):
                                mm = t.matmul(
                                    p_w1[:, b2, half],
                                    s_catT[:, ch, t16 * 128:(t16 + 1) * 128],
                                    s_wo2g[:, ch, half * 512:(half + 1) * 512],
                                    start=(ch == 0), stop=(ch == 7))
                            return mm
                        evw = p.on("tensor", wmm,
                                   after=[ev_attn_done, ev_wo2g,
                                          prev_cbs.get(j - 2)],
                                   inc=("tensor", 1), label=f"wom{j}_{half}")
                        wo_evs.append(evw)
                    evxn = p.on("vector",
                                lambda v, j=j, b2=b2: v.tensor_scalar(
                                    s_xn2[:, :], s_fx[:, b2],
                                    s_nm1r1[:, j:j + 1],
                                    s_r1own[:, j:j + 1],
                                    ALU.add, ALU.mult),
                                after=[evfx, st['prev_cb']], label=f"fxn{j}")
                    prev_xn2[j] = evxn

                    evcb = p.chain("vector", [
                        lambda v, j=j, b2=b2: v.tensor_scalar(
                            s_eog[:, b2], s_eog[:, b2], s_wown[:, j:j + 1],
                            None, ALU.mult),
                        lambda v, b2=b2: v.tensor_tensor(
                            s_yt[:, b2], s_fx[:, b2], s_xn2[:, :], ALU.add),
                        lambda v, b2=b2: v.tensor_tensor(
                            s_yt[:, b2], s_yt[:, b2], s_eog[:, b2], ALU.add),
                        lambda v, b2=b2: v.tensor_tensor(
                            s_yt[:, b2, 0:512], s_yt[:, b2, 0:512],
                            p_w1[:, b2, 0], ALU.add),
                        lambda v, b2=b2: v.tensor_tensor(
                            s_yt[:, b2, 512:1024], s_yt[:, b2, 512:1024],
                            p_w1[:, b2, 1], ALU.add),
                    ], after=[evxn, evge] + wo_evs, label=f"cb{j}")
                    st['prev_cb'] = evcb
                    prev_cbs[j] = evcb
                    evy = p.dma("gpsimd", y[j * 128:(j + 1) * 128, :],
                                s_yt[:, b2], after=[evcb], label=f"y{j}",
                                slot=f"y{b2}")
                    prev_y[j] = evy
                    st['ev_y_last'] = evy

    for sq in ([1] if SEQ1_ONLY else range(2)):
        emit_seq(sq)
    c23.close()

    p.on("gpsimd", lambda g: g.engine_nop(),
         after=[prev_y[OWN_TILES - 2], prev_y[OWN_TILES - 1]], inc=None,
         label="fin")

    p.emit(nc, es)
    return nc


_NC_CACHE = {}


def _get_nc():
    if "nc" not in _NC_CACHE:
        _NC_CACHE["nc"] = build_nc()
    return _NC_CACHE["nc"]


def kernel(L0_attn_x, L0_attn_Wq, L0_attn_Wk, L0_attn_Wv, L0_attn_Wo,
           L0_ffn_gate_weight, L0_ffn_ln_gamma, L0_ffn_ln_beta,
           L0_ffn_wi, L0_ffn_wo, L0_ffn_conf_w1, L0_ffn_conf_b1,
           L0_ffn_conf_w2, L0_ffn_conf_b2, L0_ffn_comp_w, L0_ffn_comp_b):
    x = np.ascontiguousarray(np.asarray(L0_attn_x, dtype=np.float32))
    xf = x.reshape(NT, D)
    xT = np.ascontiguousarray(
        xf.T.reshape(8, 128, 64, 512).transpose(2, 1, 0, 3))

    c = {}
    c["ident"] = np.eye(128, dtype=np.float32)
    c["identb"] = np.eye(128, dtype=np.float32).astype(ml_dtypes.bfloat16)
    c["lstrictT"] = np.triu(np.ones((128, 128), np.float32), 1)
    c["onesrow"] = np.ones((1, 128), np.float32)
    c["onescol"] = np.ones((128, 1), np.float32)
    c["iota8"] = np.tile(np.arange(8, dtype=np.float32), (128, 1))
    c["trimask"] = np.triu(np.ones((128, 128), np.float32), 0).astype(
        ml_dtypes.bfloat16)
    inv = 1.0 / ROPE_THETA ** (np.arange(0, DK, 2, dtype=np.float64) / DK)
    ang = np.outer(np.arange(S, dtype=np.float64), inv)
    cos = np.repeat(np.cos(ang).astype(np.float32), 2, axis=1).T
    sin = np.repeat(np.sin(ang).astype(np.float32), 2, axis=1).T
    c["ropecos"] = np.ascontiguousarray(np.vstack([cos, cos]), dtype=np.float32)
    c["ropesin"] = np.ascontiguousarray(np.vstack([sin, sin]), dtype=np.float32)
    R = np.zeros((64, 64), np.float32)
    for i in range(32):
        R[2 * i, 2 * i + 1] = -1.0
        R[2 * i + 1, 2 * i] = 1.0
    blk = np.zeros((128, 128), np.float32)
    blk[0:64, 0:64] = R.T
    blk[64:128, 64:128] = R.T
    c["ropePT"] = blk.astype(ml_dtypes.bfloat16)
    sg = np.empty((128, 1), np.float32)
    sg[0::2, 0] = -1.0
    sg[1::2, 0] = 1.0
    c["signcol"] = sg
    big = np.empty((128, OWN_TILES), np.float32)
    for j in range(OWN_TILES):
        big[:, j] = 2.0e8 + j * 128 + np.arange(128)
    c["bigiota"] = big

    def til(w):
        return np.asarray(w, np.float32).reshape(8, 128, 8, 128)\
            .transpose(2, 1, 0, 3)
    base = {
        "wqkvt": np.ascontiguousarray(np.stack(
            [til(L0_attn_Wq), til(L0_attn_Wk), til(L0_attn_Wv)])),
        "watot": np.ascontiguousarray(
            np.asarray(L0_attn_Wo, np.float32).reshape(8, 128, D)
            .transpose(1, 0, 2)),
        "gT": np.ascontiguousarray(np.asarray(L0_ffn_gate_weight, np.float32).T),
        "gamma_bc": np.ascontiguousarray(
            np.tile(np.asarray(L0_ffn_ln_gamma, np.float32)[None, :], (128, 1))),
        "beta_bc": np.ascontiguousarray(
            np.tile(np.asarray(L0_ffn_ln_beta, np.float32)[None, :], (128, 1))),
        "gammacol": np.ascontiguousarray(
            np.asarray(L0_ffn_ln_gamma, np.float32).reshape(8, 128).T),
        "betacol": np.ascontiguousarray(
            np.asarray(L0_ffn_ln_beta, np.float32).reshape(8, 128).T),
        "wit": np.ascontiguousarray(
            np.asarray(L0_ffn_wi, np.float32).reshape(E, 8, 128, 8, 512)
            .transpose(0, 3, 2, 1, 4)),
        "wot": np.ascontiguousarray(
            np.asarray(L0_ffn_wo, np.float32).reshape(E, 32, 128, 8, 128)
            .transpose(0, 3, 2, 1, 4)),
        "cw1T": np.ascontiguousarray(np.asarray(L0_ffn_conf_w1, np.float32).T),
        "cb1": np.ascontiguousarray(
            np.asarray(L0_ffn_conf_b1, np.float32).reshape(4, 128).T),
        "cw2T": np.ascontiguousarray(
            np.asarray(L0_ffn_conf_w2, np.float32).reshape(4, 128).T),
        "cb2r": np.full((128, 1),
                        float(np.asarray(L0_ffn_conf_b2).reshape(-1)[0]),
                        np.float32),
        "compwT": np.ascontiguousarray(
            np.asarray(L0_ffn_comp_w, np.float32).reshape(8, 128).T),
        "compbr": np.full((128, 1),
                          float(np.asarray(L0_ffn_comp_b).reshape(-1)[0]),
                          np.float32),
        "xf": xf, "xT": xT,
    }
    base.update(c)

    in_maps = []
    for core in range(NCORES):
        m = dict(base)
        rows = np.empty((128, OWN_TILES), np.uint32)
        for j in range(OWN_TILES):
            sq, t16 = divmod(j, 16)
            bb = core + 8 * sq
            rows[:, j] = bb * S + t16 * 128 + np.arange(128, dtype=np.uint32)
        m["ownrow_u"] = rows
        in_maps.append(m)

    nc = _get_nc()
    res = bass_utils.run_bass_kernel_spmd(nc, in_maps,
                                          core_ids=list(range(NCORES)))
    out = np.empty((B, S, D), np.float32)
    for core in range(NCORES):
        ycore = res.results[core]["y"]
        out[core] = ycore[0:S]
        out[core + 8] = ycore[S:2 * S]
    return out


# version 16 (final)
# speedup vs baseline: 1.0869x; 1.0109x over previous
import contextlib

import numpy as np
import ml_dtypes
import concourse.bass as bass
import concourse.mybir as mybir
import concourse.bass_utils as bass_utils
from concourse.bass_core_rust import IndirectOffsetOnAxis

F32 = mybir.dt.float32
BF16 = mybir.dt.bfloat16
U32 = mybir.dt.uint32
AF = mybir.ActivationFunctionType
ALU = mybir.AluOpType
AX = mybir.AxisListType

B, S, D = 16, 2048, 1024
NCORES = 8
NT = B * S
NTILE = NT // 128
OWN_TILES = 2 * S // 128
H, DK = 16, 64
E, CAP = 8, 2048
C_LOC = 512
DI = 4096
EPS = 1e-5
ROPE_THETA = 10000.0
DEBUG_OUTS = False
SEQ1_ONLY = False
SERIAL_ATTN = False

ENGINES = ("sync", "tensor", "vector", "scalar", "gpsimd")


class Prog:
    def __init__(self):
        self.items = {e: [] for e in ENGINES}
        self.cnt = {}

    def now(self, sem):
        return (sem, self.cnt.get(sem, 0))

    def _bump(self, sem, k):
        self.cnt[sem] = self.cnt.get(sem, 0) + k
        return (sem, self.cnt[sem])

    def on(self, eng, fn, after=(), inc="auto", label=""):
        waits = [w for w in after if w is not None]
        if eng in ("vector", "scalar", "gpsimd"):
            waits.append(self.now(eng))
            if inc == "auto":
                inc = (eng, 1)
        if inc == "auto":
            inc = None
        ev, incs = None, []
        if inc is not None:
            ev = self._bump(*inc)
            incs = [inc]
        self.items[eng].append((waits, fn, incs, label))
        return ev

    def fence(self):
        evs = [self.now(e) for e in ("tensor", "vector", "scalar", "gpsimd")]
        evs += [self.now(s) for s in list(self.cnt) if s.startswith("d_")]
        return evs

    def chain(self, eng, fns, after=(), label=""):
        ev = None
        for ix, fn in enumerate(fns):
            ev = self.on(eng, fn, after=after if ix == 0 else (),
                         label=f"{label}#{ix}")
        return ev

    def dma(self, eng, out, in_, after=(), label="", slot=None, **kw):
        assert slot is not None, label
        def fn(g):
            return g.dma_start(out=out, in_=in_, **kw)
        return self.on(eng, fn, after=after, inc=("d_" + slot, 16), label=label)

    def emit(self, nc, es):
        sems = {}
        for name in sorted(self.cnt.keys()):
            assert self.cnt[name] < 2 ** 31
            sems[name] = es.enter_context(nc.semaphore(name))
        lastwait = {e: {} for e in ENGINES}

        def run(engobj, eng):
            lw = lastwait[eng]
            for waits, fn, incs, label in self.items[eng]:
                for sem, val in waits:
                    if val <= 0 or lw.get(sem, -1) >= val:
                        continue
                    engobj.wait_ge(sems[sem], val)
                    lw[sem] = val
                inst = fn(engobj)
                for sem, k in incs:
                    assert inst is not None, label
                    inst.then_inc(sems[sem], k)

        with nc.Block() as block:
            @block.sync
            def _(e):
                run(e, "sync")

            @block.tensor
            def _(e):
                run(e, "tensor")

            @block.vector
            def _(e):
                run(e, "vector")

            @block.scalar
            def _(e):
                run(e, "scalar")

            @block.gpsimd
            def _(e):
                run(e, "gpsimd")


def build_nc():
    nc = bass.Bass("TRN2", target_bir_lowering=False, debug=False)
    es = contextlib.ExitStack()
    nc._keepalive_es = es
    p = Prog()
    SC = contextlib.ExitStack

    def din(name, shape, dt=F32):
        return nc.dram_tensor(name, shape, dt, kind="ExternalInput").ap()

    xf = din("xf", [NT, D])
    xT = din("xT", [64, 128, 8, 512])
    wqkvt = din("wqkvt", [3, 8, 128, 8, 128])
    watot = din("watot", [128, 8, D])
    gT = din("gT", [D, E])
    gamma_bc = din("gamma_bc", [128, D]); beta_bc = din("beta_bc", [128, D])
    gammacol = din("gammacol", [128, 8]); betacol = din("betacol", [128, 8])
    wit = din("wit", [E, 8, 128, 8, 512])
    wot = din("wot", [E, 8, 128, 32, 128])
    cw1T = din("cw1T", [D, 512]); cb1 = din("cb1", [128, 4])
    cw2T = din("cw2T", [128, 4]); cb2r = din("cb2r", [128, 1])
    compwT = din("compwT", [128, 8]); compbr = din("compbr", [128, 1])
    ident = din("ident", [128, 128]); identb = din("identb", [128, 128], BF16)
    lstrictT = din("lstrictT", [128, 128])
    onesrow = din("onesrow", [1, 128]); onescol = din("onescol", [128, 1])
    iota8 = din("iota8", [128, 8])
    trimask = din("trimask", [128, 128], BF16)
    ropecos = din("ropecos", [128, S]); ropesin = din("ropesin", [128, S])
    ropePT = din("ropePT", [128, 128], BF16)
    signcol = din("signcol", [128, 1])
    bigiota = din("bigiota", [128, OWN_TILES])
    ownrow_u = nc.dram_tensor("ownrow_u", [128, OWN_TILES], U32,
                              kind="ExternalInput").ap()

    y = nc.dram_tensor("y", [2 * S, D], F32, kind="ExternalOutput").ap()
    _dbg = "ExternalOutput" if DEBUG_OUTS else "Internal"
    dbg_cat = (nc.dram_tensor("dbg_cat", [2, 128, 8 * S], BF16, kind=_dbg).ap()
               if DEBUG_OUTS else None)
    dbg_xnT = (nc.dram_tensor("dbg_xnT", [2, 128, 8 * S], BF16, kind=_dbg).ap()
               if DEBUG_OUTS else None)
    dbg_qk = (nc.dram_tensor("dbg_qk", [2, 128, S], BF16, kind=_dbg).ap()
              if DEBUG_OUTS else None)
    dbg_PT = (nc.dram_tensor("dbg_PT", [128, 16 * 512], BF16, kind=_dbg).ap()
              if DEBUG_OUTS else None)
    dbg_xn2 = (nc.dram_tensor("dbg_xn2", [2, 128, OWN_TILES * D], BF16,
                              kind=_dbg).ap() if DEBUG_OUTS else None)
    dbg_idx = (nc.dram_tensor("dbg_idx", [128, 2 * OWN_TILES], U32,
                              kind=_dbg).ap() if DEBUG_OUTS else None)
    dbg_w = (nc.dram_tensor("dbg_w", [128, OWN_TILES], F32, kind=_dbg).ap()
             if DEBUG_OUTS else None)
    disp = nc.dram_tensor("disp", [E * C_LOC, D], BF16, kind=_dbg).ap()
    eo = nc.dram_tensor("eo", [E * C_LOC, D], F32, kind=_dbg).ap()
    meta_d = nc.dram_tensor("meta_d", [NT, 8], F32, kind=_dbg).ap()
    wqkv_bf = nc.dram_tensor("wqkv_bf", [3, 8, 128, 8, 128], BF16,
                             kind="Internal").ap()
    wato_bf = nc.dram_tensor("wato_bf", [128, 8, D], BF16,
                             kind="Internal").ap()

    sb = lambda n, sh, dt=F32: es.enter_context(nc.sbuf_tensor(n, sh, dt))

    c23 = SC()
    pre3 = SC()
    _sp_reqs = []

    def sp(n, sh, dt=F32):
        _sp_reqs.append((n, sh, dt))
        return ("SPREQ", len(_sp_reqs) - 1)
    s_id = sb("c_id", [128, 128]); s_idb = sb("c_idb", [128, 128], BF16)
    s_L = sp("c_L", [128, 128])
    s_or = sb("c_or", [1, 128]); s_oc = sb("c_oc", [128, 1])
    s_io8 = sp("c_io8", [128, 8]); s_tri = sb("c_tri", [128, 128], BF16)
    s_cos = sb("c_cos", [128, S]); s_sin = sb("c_sin", [128, S])
    s_rpt = sb("c_rpt", [128, 128], BF16)
    s_sgn = sb("c_sgn", [128, 1])
    s_gbc = sp("c_gbc", [128, D]); s_bbc = sp("c_bbc", [128, D])
    s_gTw = sp("c_gTw", [128, 8, 8])
    s_gcol = sp("c_gcol", [128, 8]); s_bcol = sp("c_bcol", [128, 8])
    s_g9 = sp("c_g9", [128, 8, 9])
    s_Cbc = sp("c_Cbc", [128, 8]); s_Bgbc = sp("c_Bgbc", [128, 8])
    s_cb2 = sp("c_cb2", [128, 1]); s_compb = sp("c_compb", [128, 1])
    s_cw2b = sp("c_cw2b", [128, 4], BF16)
    s_compwb = sp("c_compwb", [128, 8], BF16)
    s_cb1 = sp("c_cb1", [128, 4])
    s_bigio = sp("c_bigio", [128, OWN_TILES])
    s_ownrow = es.enter_context(nc.sbuf_tensor("ownrow", [128, OWN_TILES], U32))
    s_gidx = es.enter_context(nc.sbuf_tensor("gidx", [128, OWN_TILES], U32))
    s_sidx = es.enter_context(nc.sbuf_tensor("sidx", [128, OWN_TILES], U32))
    s_wown = sb("wown", [128, OWN_TILES])
    s_r1own = sb("r1own", [128, OWN_TILES])
    s_nm1r1 = sb("nm1r1own", [128, OWN_TILES])
    s_acc = sp("accs", [1, 8])
    s_iok = sp("c_iok", [128, 8])
    s_m1000 = sp("c_m1000", [128, 1])
    s_acc2 = sp("accs2", [1, 8])
    s_confacc = sp("confacc", [128, 1]); s_compacc = sp("compacc", [128, 1])
    s_zero = sp("zerot", [128, D], BF16)
    s_eps = sp("epst", [128, 1])

    s_xnown = c23.enter_context(nc.sbuf_tensor("xnown", [128, OWN_TILES, D], BF16))
    _sp_t = [pre3.enter_context(nc.sbuf_tensor(n_, sh_, dt_))
             for (n_, sh_, dt_) in _sp_reqs]

    def _resolve(h):
        return _sp_t[h[1]] if isinstance(h, tuple) and len(h) == 2 and h[0] == "SPREQ" else h
    s_L = _resolve(s_L); s_io8 = _resolve(s_io8)
    s_gbc = _resolve(s_gbc); s_bbc = _resolve(s_bbc)
    s_gTw = _resolve(s_gTw); s_gcol = _resolve(s_gcol); s_bcol = _resolve(s_bcol)
    s_g9 = _resolve(s_g9); s_Cbc = _resolve(s_Cbc); s_Bgbc = _resolve(s_Bgbc)
    s_cb2 = _resolve(s_cb2); s_compb = _resolve(s_compb)
    s_cw2b = _resolve(s_cw2b); s_compwb = _resolve(s_compwb)
    s_cb1 = _resolve(s_cb1); s_bigio = _resolve(s_bigio)
    s_acc = _resolve(s_acc); s_acc2 = _resolve(s_acc2)
    s_iok = _resolve(s_iok); s_m1000 = _resolve(s_m1000)
    s_confacc = _resolve(s_confacc); s_compacc = _resolve(s_compacc)
    s_zero = _resolve(s_zero)
    s_eps = _resolve(s_eps)
    evc = []
    for dst, src in [(s_id, ident), (s_idb, identb), (s_L, lstrictT),
                     (s_or, onesrow), (s_oc, onescol), (s_io8, iota8),
                     (s_tri, trimask), (s_cos, ropecos), (s_sin, ropesin),
                     (s_rpt, ropePT), (s_sgn, signcol),
                     (s_gbc, gamma_bc), (s_bbc, beta_bc),
                     (s_cb2, cb2r), (s_compb, compbr), (s_bigio, bigiota),
                     (s_ownrow, ownrow_u)]:
        evc.append(p.dma("sync", dst[:], src[:, :], slot="const"))
    evc.append(p.dma("sync", s_gTw[:], gT.rearrange("(c q) n -> q c n", q=128),
                     slot="const"))
    for dst, srcap in [(s_gcol, gammacol), (s_bcol, betacol), (s_cb1, cb1)]:
        evc.append(p.dma("sync", dst[:], srcap[:, :], slot="const"))
    for dst, srcap in [(s_cw2b, cw2T), (s_compwb, compwT)]:
        evc.append(p.dma("gpsimd", dst[:], srcap[:, :], slot="constg"))
    evc = [evc[-3], evc[-1]]

    def mkg9(v):
        for ch in range(8):
            v.tensor_scalar(s_g9[:, ch, 0:8], s_gTw[:, ch], s_gcol[:, ch:ch + 1],
                            None, ALU.mult)
        return v.memset(s_g9[:, :, 8:9], 1.0)

    ev_g9 = p.on("vector", mkg9, after=evc)
    ev_zt = p.on("vector", lambda v: v.memset(s_zero[:, :], 0.0))
    ev_eps = p.on("vector", lambda v: v.memset(s_eps[:, :], EPS))
    ev_iok = p.on("vector",
                  lambda v: v.tensor_scalar(s_iok[:, :], s_io8[:, :], 1000.0,
                                            None, ALU.add), after=evc)
    ev_m1000 = p.on("vector", lambda v: v.memset(s_m1000[:, :], -1000.0))
    ev_acc0 = p.on("vector", lambda v: v.memset(s_acc[:, :], 0.0))
    ev_acc20 = p.on("vector", lambda v: v.memset(s_acc2[:, :], 0.0))
    ev_cf0 = p.on("vector", lambda v: v.memset(s_confacc[:, :], 0.0))
    ev_cp0 = p.on("vector", lambda v: v.memset(s_compacc[:, :], 0.0))

    with SC() as c0:
        p_r8 = c0.enter_context(nc.psum_tensor("p_r8", [1, 8], F32))
        p_b8 = c0.enter_context(nc.psum_tensor("p_b8", [128, 8], F32))

        def mmC(t):
            for ch in range(8):
                mm = t.matmul(p_r8[:, :], s_oc[:, :], s_g9[:, ch, 0:8],
                              start=(ch == 0), stop=(ch == 7))
            return mm
        e1 = p.on("tensor", mmC, after=[ev_g9] + evc, inc=("tensor", 1))
        e2 = p.on("vector", lambda v: v.tensor_copy(s_Cbc[0:1, :], p_r8[:, :]),
                  after=[e1])
        e3 = p.on("tensor",
                  lambda t: t.matmul(p_b8[:, :], s_or[0:1, :], s_Cbc[0:1, :],
                                     start=True, stop=True),
                  after=[e2], inc=("tensor", 1))
        e4 = p.on("vector", lambda v: v.tensor_copy(s_Cbc[:, :], p_b8[:, :]),
                  after=[e3])

        def mmBg(t):
            for ch in range(8):
                mm = t.matmul(p_r8[:, :], s_bcol[:, ch:ch + 1], s_gTw[:, ch],
                              start=(ch == 0), stop=(ch == 7))
            return mm
        e5 = p.on("tensor", mmBg, after=[e4] + evc, inc=("tensor", 1))
        e6 = p.on("vector", lambda v: v.tensor_copy(s_Bgbc[0:1, :], p_r8[:, :]),
                  after=[e5])
        e7 = p.on("tensor",
                  lambda t: t.matmul(p_b8[:, :], s_or[0:1, :], s_Bgbc[0:1, :],
                                     start=True, stop=True),
                  after=[e6], inc=("tensor", 1))
        ev_gate = p.on("vector", lambda v: v.tensor_copy(s_Bgbc[:, :], p_b8[:, :]),
                       after=[e7])

    ev_dz = []
    for r0 in range(0, E * C_LOC, 128):
        ev_dz.append(p.dma("gpsimd", disp[r0:r0 + 128, :], s_zero[:, :],
                           after=[ev_zt], label="dispz", slot="dz"))
    ev_dz_last = ev_dz[-1]

    ev_pc_qkv = []
    for wi_ in range(3):
        ev_pc_qkv.append(p.dma("gpsimd", wqkv_bf[wi_], wqkvt[wi_],
                               label=f"pcq{wi_}", slot=f"pq{wi_}"))
    ev_pc_qkv.append(p.dma("gpsimd", wato_bf, watot,
                           label="pcq3", slot="pq3"))

    NQ = NTILE // 4
    with SC() as c1:
        sb1 = lambda n, sh, dt=F32: c1.enter_context(nc.sbuf_tensor(n, sh, dt))
        s_xq = sb1("sc_x", [128, 2, 4, D])
        s_xTq = sb1("sc_xT", [128, 2, 8, 512])
        s_jk = c1.enter_context(nc.sbuf_tensor("sc_jk", [128, 2, D], BF16))
        s_gTq = sb1("sc_gT", [9, 2, 512])
        s_Gq = sb1("sc_G", [128, 4, 40])
        s_m8 = sb1("sc_m8", [128, 4, 32])
        s_q = sb1("sc_q", [128, 4, 24])
        s_lgq = sb1("sc_lg", [128, 4, 32])
        s_ohq = sb1("sc_oh", [128, 4, 32])
        s_r8q = sb1("sc_r8", [128, 4, 32])
        s_mp = sb1("sc_mp", [128, 2, 4, 8])
        s_tot = sb1("sc_tot", [1, 2, 32])
        s_off = sb1("sc_off", [1, 2, 4, 8])
        s_acc2d = sb1("sc_acc2d", [1, 2, 8])
        p_gTq = c1.enter_context(nc.psum_tensor("p_gT", [9, 2, 512], F32))
        p_Gq = c1.enter_context(nc.psum_tensor("p_G", [128, 2, 40], F32))
        p_rank = c1.enter_context(nc.psum_tensor("p_rank", [128, 4, 8], F32))
        p_tot = c1.enter_context(nc.psum_tensor("p_tot", [1, 32], F32))

        ev_z1 = p.on("vector", lambda v: v.memset(s_m8[:, :, 24:32], 0.0))
        ev_acc0q = p.on("vector", lambda v: v.memset(s_acc2d[:, :], 0.0))
        ev_gpp = p.chain("vector", [
            lambda v: v.tensor_scalar(s_Cbc[:, :], s_Cbc[:, :], 1.0 / D, None,
                                      ALU.mult)] + [
            (lambda v, ch=ch: v.tensor_tensor(s_g9[:, ch, 0:8], s_g9[:, ch, 0:8],
                                              s_Cbc[:, :], ALU.subtract))
            for ch in range(8)],
            after=[ev_g9, ev_gate], label="gpp")

        ev_sq = {}
        ev_xld = {}
        ev_xTld = {}
        ev_gmm = {}
        ev_gevac = {}
        ev_gtr = {}
        ev_Gev = {}
        ev_snap = {-1: ev_acc0}
        ev_meta = {}
        ev_st1 = {}
        ev_r1 = {}
        ev_st2 = {}
        ev_r2 = {}
        ev_lgam = {}
        ev_e8 = {}
        ev_pack = {}

        def S0(q):
            b2 = q % 2
            ev_xld[q] = p.dma("sync", s_xq[:, b2],
                              xf[q * 512:(q + 1) * 512, :]
                              .rearrange("(a p) n -> p a n", p=128),
                              after=[ev_sq.get(q - 2)], label=f"xld{q}",
                              slot=f"x{b2}")
            ev_xTld[q] = p.dma(
                "sync", s_xTq[:, b2], xT[q],
                after=[ev_gmm.get(q - 2)],
                label=f"xTld{q}", slot=f"xT{b2}")

            def gmm(t, b2=b2):
                for ch in range(8):
                    mm = t.matmul(p_gTq[:, b2], s_g9[:, ch], s_xTq[:, b2, ch],
                                  start=(ch == 0), stop=(ch == 7))
                return mm
            ev_gmm[q] = p.on("tensor", gmm,
                             after=[ev_xTld[q], ev_gpp, ev_gevac.get(q - 2)],
                             inc=("tensor", 1), label=f"gmm{q}")
            ev_gevac[q] = p.on(
                "vector",
                lambda v, b2=b2: v.tensor_copy(s_gTq[:, b2], p_gTq[:, b2]),
                after=[ev_gmm[q], ev_gtr.get(q - 2)], label=f"gev{q}")

            def gtr(t, b2=b2):
                for k in range(4):
                    mm = t.transpose(p_Gq[:, b2, 10 * k:10 * k + 9],
                                     s_gTq[:, b2, 128 * k:128 * k + 128],
                                     s_id[0:9, 0:9])
                return mm
            ev_gtr[q] = p.on("tensor", gtr,
                             after=[ev_gevac[q], ev_Gev.get(q - 2)],
                             inc=("tensor", 1), label=f"gtr{q}")

            def _gev(v, q=q, b2=b2):
                for k in range(4):
                    cp = v.tensor_copy(s_Gq[:, q % 4, 10 * k:10 * k + 9],
                                       p_Gq[:, b2, 10 * k:10 * k + 9])
                return cp
            ev_Gev[q] = p.on("vector", _gev, after=[ev_gtr[q]], label=f"Gev{q}")

        def S1(q):
            b2 = q % 2
            ev = None
            for k in range(4):
                ev = p.on("scalar",
                          lambda s, q=q, b2=b2, k=k: s.activation(
                              out=s_jk[:, k % 2], in_=s_xq[:, b2, k],
                              func=AF.Square,
                              accum_out=s_q[:, q % 4, k:k + 1]),
                          after=[ev_xld[q], ev_meta.get(q - 4)],
                          label=f"sq{q}_{k}")
            ev_sq[q] = ev

        def S2(q):
            b4 = q % 4
            qq, m8 = s_q[:, b4], s_m8[:, b4]
            G4 = s_Gq[:, b4]
            Gs = G4.rearrange("p (a b) -> p a b", b=10)[:, :, 8:9]
            m13 = m8[:, 0:4].rearrange("p (a b) -> p a b", b=1)
            ev_st1[q] = p.chain("vector", [
                lambda v: v.tensor_scalar(m13, Gs, 1.0 / D, None, ALU.mult),
                lambda v: v.tensor_tensor(qq[:, 4:8], m8[:, 0:4], m8[:, 0:4],
                                          ALU.mult),
                lambda v: v.tensor_scalar(qq[:, 8:12], qq[:, 0:4], 1.0 / D,
                                          None, ALU.mult),
                lambda v: v.tensor_tensor(qq[:, 8:12], qq[:, 8:12], qq[:, 4:8],
                                          ALU.subtract),
            ], after=[ev_sq[q], ev_Gev[q], ev_meta.get(q - 4), ev_z1],
                label=f"st1_{q}")

        def S3(q):
            b4 = q % 4
            p.on("scalar",
                 lambda s, b4=b4: s.activation(
                     out=s_q[:, b4, 4:8], in_=s_q[:, b4, 8:12],
                     func=AF.Ln, bias=s_eps[:, 0:1]),
                 after=[ev_st1[q], ev_eps], label=f"ln1_{q}")
            ev_r1[q] = p.on("scalar",
                            lambda s, b4=b4: s.activation(
                                out=s_m8[:, b4, 4:8], in_=s_q[:, b4, 4:8],
                                func=AF.Exp, scale=-0.5),
                            after=[], label=f"r1_{q}")

        def S4(q):
            b4 = q % 4
            qq, m8 = s_q[:, b4], s_m8[:, b4]
            ev_st2[q] = p.chain("vector", [
                lambda v: v.tensor_tensor(qq[:, 12:16], m8[:, 4:8], m8[:, 4:8],
                                          ALU.mult),
                lambda v: v.tensor_tensor(qq[:, 12:16], qq[:, 8:12],
                                          qq[:, 12:16], ALU.mult),
            ], after=[ev_r1[q]], label=f"st2_{q}")

        def S5(q):
            b4 = q % 4
            p.on("scalar",
                 lambda s, b4=b4: s.activation(
                     out=s_q[:, b4, 16:20], in_=s_q[:, b4, 12:16],
                     func=AF.Ln, bias=s_eps[:, 0:1]),
                 after=[ev_st2[q]], label=f"ln2_{q}")
            ev_r2[q] = p.on("scalar",
                            lambda s, b4=b4: s.activation(
                                out=s_q[:, b4, 16:20], in_=s_q[:, b4, 16:20],
                                func=AF.Exp, scale=-0.5),
                            after=[], label=f"r2_{q}")

        def S6(q):
            b4 = q % 4
            qq, m8 = s_q[:, b4], s_m8[:, b4]
            lg, oh, r8 = s_lgq[:, b4], s_ohq[:, b4], s_r8q[:, b4]
            lg3 = s_lgq[:, b4].rearrange("p (a b) -> p a b", b=8)
            oh3 = s_ohq[:, b4].rearrange("p (a b) -> p a b", b=8)
            r83 = s_r8q[:, b4].rearrange("p (a b) -> p a b", b=8)
            G3 = s_Gq[:, b4].rearrange("p (a b) -> p a b", b=10)[:, :, 0:8]
            al_b = m8[:, 8:12].rearrange("p (a b) -> p a b", b=1)\
                .broadcast_to([128, 4, 8])
            mx_b = qq[:, 20:24].rearrange("p (a b) -> p a b", b=1)\
                .broadcast_to([128, 4, 8])
            e_b = m8[:, 16:20].rearrange("p (a b) -> p a b", b=1)\
                .broadcast_to([128, 4, 8])
            Bg_b = s_Bgbc[:, :].rearrange("p (a b) -> p a b", a=1)\
                .broadcast_to([128, 4, 8])
            iok_b = s_iok[:, :].rearrange("p (a b) -> p a b", a=1)\
                .broadcast_to([128, 4, 8])
            ev_lgam[q] = p.chain("vector", [
                lambda v: v.tensor_tensor(m8[:, 8:12], m8[:, 4:8], qq[:, 16:20],
                                          ALU.mult),
                lambda v: v.tensor_tensor(lg3, G3, al_b, ALU.mult),
                lambda v: v.tensor_tensor(lg3, lg3, Bg_b, ALU.add),
                lambda v: v.tensor_reduce(qq[:, 20:24], lg3, AX.X, ALU.max),
                lambda v: v.tensor_tensor(oh3, lg3, mx_b, ALU.is_equal),
                lambda v: v.scalar_tensor_tensor(
                    oh3, oh3, s_m1000[:, 0:1], iok_b, ALU.mult, ALU.add),
                lambda v: v.tensor_reduce(m8[:, 16:20], oh3, AX.X, ALU.min),
                lambda v: v.tensor_tensor(oh3, oh3, e_b, ALU.is_equal),
                lambda v: v.tensor_tensor(r83, lg3, mx_b, ALU.subtract),
            ], after=[ev_r2[q], ev_gpp], label=f"lgam{q}")

        def S7(q):
            b4 = q % 4
            ev_e8[q] = p.on("scalar",
                            lambda s, b4=b4: s.activation(
                                out=s_r8q[:, b4], in_=s_r8q[:, b4],
                                func=AF.Exp),
                            after=[ev_lgam[q]], label=f"exp8{q}")

        def S8(q):
            b4 = q % 4
            b2 = q % 2
            qa = q % 2
            qq, m8 = s_q[:, b4], s_m8[:, b4]
            r83 = s_r8q[:, b4].rearrange("p (a b) -> p a b", b=8)
            ev8 = p.chain("vector", [
                lambda v: v.tensor_reduce(qq[:, 0:4], r83, AX.X, ALU.add),
                lambda v: v.reciprocal(qq[:, 4:8], qq[:, 0:4]),
            ], after=[ev_e8[q]], label=f"wr{q}")

            def totmm(t, b4=b4):
                for k in range(4):
                    mm = t.matmul(p_tot[:, 8 * k:8 * k + 8], s_oc[:, :],
                                  s_ohq[:, b4, 8 * k:8 * k + 8],
                                  start=True, stop=True)
                return mm
            evtm = p.on("tensor", totmm, after=[ev_lgam[q], ev_snap[q - 1]],
                        inc=("tensor", 1), label=f"tot{q}")

            def rmm0(t, b4=b4, qa=qa):
                t.matmul(p_rank[:, 0], s_or[0:1, :], s_acc2d[0:1, qa],
                         start=True, stop=False)
                return t.matmul(p_rank[:, 0], s_L[:, :],
                                s_ohq[:, b4, 0:8],
                                start=False, stop=True)
            evr0 = p.on("tensor", rmm0,
                        after=[ev_lgam[q], ev_snap[q - 1], ev_pack.get(q - 1)],
                        inc=("tensor", 1), label=f"rmm0{q}")

            evoff = p.chain("vector", [
                lambda v, b2=b2: v.tensor_copy(s_tot[:, b2], p_tot[:, :]),
                lambda v, b2=b2, qa=qa: v.tensor_tensor(
                    s_off[:, b2, 0], s_acc2d[0:1, qa], s_tot[:, b2, 0:8],
                    ALU.add),
                lambda v, b2=b2: v.tensor_tensor(
                    s_off[:, b2, 1], s_off[:, b2, 0], s_tot[:, b2, 8:16],
                    ALU.add),
                lambda v, b2=b2: v.tensor_tensor(
                    s_off[:, b2, 2], s_off[:, b2, 1], s_tot[:, b2, 16:24],
                    ALU.add),
                lambda v, b2=b2, qa=qa: v.tensor_tensor(
                    s_acc2d[0:1, 1 - qa], s_off[0:1, b2, 2],
                    s_tot[:, b2, 24:32], ALU.add),
            ], after=[evtm, ev_acc0q], label=f"off{q}")
            ev_snap[q] = evoff

            def rmm(t, b4=b4, b2=b2):
                for k in range(1, 4):
                    t.matmul(p_rank[:, k], s_or[0:1, :],
                             s_off[0:1, b2, k - 1],
                             start=True, stop=False)
                    mm = t.matmul(p_rank[:, k], s_L[:, :],
                                  s_ohq[:, b4, 8 * k:8 * k + 8],
                                  start=False, stop=True)
                return mm
            evr = p.on("tensor", rmm, after=[evoff], inc=("tensor", 1),
                       label=f"rmm{q}")
            oh3 = s_ohq[:, b4].rearrange("p (a b) -> p a b", b=8)
            keep_b = qq[:, 8:12]
            evx = p.chain("vector", [
                lambda v: v.tensor_tensor(
                    s_r8q[:, b4].rearrange("p (a b) -> p a b", b=8),
                    p_rank[:, :, :], oh3, ALU.mult),
                lambda v: v.tensor_reduce(
                    m8[:, 20:24],
                    s_r8q[:, b4].rearrange("p (a b) -> p a b", b=8),
                    AX.X, ALU.add),
                lambda v: v.tensor_scalar(keep_b, m8[:, 20:24], float(CAP),
                                          None, ALU.is_lt),
                lambda v: v.tensor_tensor(m8[:, 12:16], qq[:, 4:8], keep_b,
                                          ALU.mult),
                lambda v, b2=b2: v.tensor_copy(
                    s_mp[:, b2],
                    s_m8[:, b4].rearrange("p (c k) -> p k c", k=4)),
            ], after=[evr, evr0, ev8, ev_meta.get(q - 2)], label=f"rx{q}")
            ev_pack[q] = evx
            ev_meta[q] = p.dma("gpsimd",
                               meta_d[q * 512:(q + 1) * 512, :]
                               .rearrange("(k p) b -> p k b", p=128),
                               s_mp[:, b2],
                               after=[evx], label=f"meta{q}",
                               slot=f"meta{q % 2}")

        stages = [S0, S1, S2, S3, S4, S5, S6, S7, S8]
        SKEW = [0, 1, 1, 2, 2, 3, 3, 4, 4]
        for step in range(NQ + 5):
            for st_fn, off in zip(stages, SKEW):
                qi = step - off
                if 0 <= qi < NQ:
                    st_fn(qi)

        ev_meta_all = [ev_meta[NQ - 2], ev_meta[NQ - 1]]

    with SC() as c2:
        s_cw1b = c2.enter_context(nc.sbuf_tensor("c_cw1b", [128, 8, 512], BF16))
        ev_cw1 = p.dma("gpsimd", s_cw1b[:], cw1T.rearrange("(c q) n -> q c n", q=128), slot="cw1")
        sb2 = lambda n, sh, dt=F32: c2.enter_context(nc.sbuf_tensor(n, sh, dt))
        s_ox = sb2("ow_x", [128, 2, D])
        s_om = sb2("ow_m", [128, 2, 8])
        s_u = sb2("ow_u", [128, D])
        s_t2b = c2.enter_context(nc.sbuf_tensor("ow_t2", [128, 2, D], BF16))
        s_t2T = c2.enter_context(nc.sbuf_tensor("ow_t2T", [128, 8, 128], BF16))
        s_hdnT = c2.enter_context(nc.sbuf_tensor("ow_hdn", [128, 4, 128], BF16))
        s_sc2 = sb2("ow_sc", [128, 16])
        s_oh2 = sb2("ow_oh", [128, 8])
        s_r82 = sb2("ow_r8", [128, 8])
        p_t2T = c2.enter_context(nc.psum_tensor("p_t2T", [128, 128], BF16))
        p_cf = c2.enter_context(nc.psum_tensor("p_cf", [128, 128], F32))
        p_c2 = c2.enter_context(nc.psum_tensor("p_c2", [128, 1], F32))
        p_cp = c2.enter_context(nc.psum_tensor("p_cp", [128, 1], F32))
        p_rank2 = c2.enter_context(nc.psum_tensor("p_rank2", [128, 8], F32))
        p_acc2p = c2.enter_context(nc.psum_tensor("p_acc2p", [1, 8], F32))

        ev_snap2 = {-1: ev_acc20}
        ev_xn_prev = {}
        fence2 = p.fence()
        ev_scat = {}
        ev_u_by_j = {}
        ev_si_by_j = {}
        ev_t2_by_j = {}
        prev_t2T_evac = None
        ev_conf_prev = None
        ev_comp_prev = None
        ev_sg_prev = None
        ev_sp_prev = None
        eva2_prev = None
        for j in range(OWN_TILES):
            b2 = j % 2
            evx_ = p.on("gpsimd",
                        lambda g, b2=b2, j=j: g.indirect_dma_start(
                            out=s_ox[:, b2], out_offset=None, in_=xf[:, :],
                            in_offset=IndirectOffsetOnAxis(s_ownrow[:, j:j + 1], 0)),
                        after=[ev_u_by_j.get(j - 2), ev_xn_prev.get(j - 2)]
                        + evc + fence2,
                        inc=("d_ox%d" % b2, 16), label=f"oxg{j}")
            evm_ = p.on("gpsimd",
                        lambda g, b2=b2, j=j: g.indirect_dma_start(
                            out=s_om[:, b2], out_offset=None, in_=meta_d[:, :],
                            in_offset=IndirectOffsetOnAxis(s_ownrow[:, j:j + 1], 0)),
                        after=ev_meta_all + [ev_si_by_j.get(j - 2)] + fence2,
                        inc=("d_om%d" % b2, 16), label=f"omg{j}")

            om_ = s_om[:, b2]
            evp = p.chain("vector", [
                lambda v, j=j, om=om_: v.tensor_scalar(
                    s_nm1r1[:, j:j + 1], om[:, 0:1], -1.0, None, ALU.mult),
                lambda v, j=j, om=om_: v.tensor_copy(
                    s_r1own[:, j:j + 1], om[:, 1:2]),
            ], after=[evm_, ev_u_by_j.get(j - 1)], label=f"prep{j}")

            ev_xn = p.on("vector",
                         lambda v, j=j, b2=b2: v.tensor_scalar(
                             s_xnown[:, j], s_ox[:, b2],
                             s_nm1r1[:, j:j + 1], s_om[:, b2, 1:2],
                             ALU.add, ALU.mult),
                         after=[evx_, evp], label=f"xn{j}")
            ev_xn_prev[j] = ev_xn
            ev_u = p.on("vector",
                        lambda v, j=j, b2=b2: v.tensor_scalar(
                            s_u[:, :], s_ox[:, b2],
                            s_nm1r1[:, j:j + 1], s_om[:, b2, 2:3],
                            ALU.add, ALU.mult),
                        after=[evx_, evp, ev_t2_by_j.get(j - 1)], label=f"u{j}")
            ev_u_by_j[j] = ev_u

            ev_t2 = p.chain("vector", [
                lambda v: v.tensor_tensor(s_u[:, :], s_u[:, :], s_gbc[:, :],
                                          ALU.mult),
                lambda v, b2=b2: v.tensor_tensor(s_t2b[:, b2], s_u[:, :],
                                                 s_bbc[:, :], ALU.add),
            ], after=[ev_u, ev_scat.get(j - 2)], label=f"t2{j}")
            ev_t2_by_j[j] = ev_t2

            tr_evs = []
            for ch in range(8):
                evtr = p.on("tensor",
                            lambda t, b2=b2, ch=ch: t.transpose(
                                p_t2T[:, :], s_t2b[:, b2, ch * 128:(ch + 1) * 128],
                                s_idb[:, :]),
                            after=[ev_t2, tr_evs[-1] if tr_evs else prev_t2T_evac],
                            inc=("tensor", 1), label=f"t2T{j}_{ch}")
                evev = p.on("vector",
                            lambda v, ch=ch: v.tensor_copy(s_t2T[:, ch],
                                                           p_t2T[:, :]),
                            after=[evtr], label=f"t2Te{j}_{ch}")
                tr_evs.append(evev)
            prev_t2T_evac = tr_evs[-1]
            t2T_done = tr_evs[-1]

            cf_evs = []
            for hc in range(4):
                def cfm(t, hc=hc):
                    for ch in range(8):
                        mm = t.matmul(p_cf[:, :],
                                      s_cw1b[:, ch, hc * 128:(hc + 1) * 128],
                                      s_t2T[:, ch], start=(ch == 0), stop=(ch == 7))
                    return mm
                evcf = p.on("tensor", cfm,
                            after=[t2T_done, ev_cw1,
                                   cf_evs[-1] if cf_evs else ev_conf_prev],
                            inc=("tensor", 1), label=f"cf{j}_{hc}")
                evrl = p.on("scalar",
                            lambda s, hc=hc: s.activation(
                                out=s_hdnT[:, hc], in_=p_cf[:, :], func=AF.Relu,
                                bias=s_cb1[:, hc:hc + 1]),
                            after=[evcf], label=f"rl{j}_{hc}")
                cf_evs.append(evrl)

            def cf2(t):
                for hc in range(4):
                    mm = t.matmul(p_c2[:, :], s_hdnT[:, hc], s_cw2b[:, hc:hc + 1],
                                  start=(hc == 0), stop=(hc == 3))
                return mm
            evc2 = p.on("tensor", cf2, after=cf_evs + [ev_sg_prev],
                        inc=("tensor", 1), label=f"cf2{j}")
            evsg = p.on("scalar",
                        lambda s: s.activation(out=s_sc2[:, 2:3], in_=p_c2[:, :],
                                               func=AF.Sigmoid,
                                               bias=s_cb2[:, 0:1]),
                        after=[evc2, ev_conf_prev], label=f"sg{j}")
            ev_sg_prev = evsg
            ev_conf_prev = p.on(
                "vector",
                lambda v: v.tensor_tensor(s_confacc[:, :], s_confacc[:, :],
                                          s_sc2[:, 2:3], ALU.add),
                after=[evsg, ev_cf0], label=f"cfa{j}")

            def cpm(t):
                for ch in range(8):
                    mm = t.matmul(p_cp[:, :], s_t2T[:, ch], s_compwb[:, ch:ch + 1],
                                  start=(ch == 0), stop=(ch == 7))
                return mm
            evcp = p.on("tensor", cpm, after=[t2T_done, ev_sp_prev],
                        inc=("tensor", 1), label=f"cp{j}")
            evsp = p.on("scalar",
                        lambda s: s.activation(out=s_sc2[:, 3:4], in_=p_cp[:, :],
                                               func=AF.Sigmoid,
                                               bias=s_compb[:, 0:1]),
                        after=[evcp, ev_comp_prev], label=f"sgp{j}")
            ev_sp_prev = evsp
            ev_comp_prev = p.on(
                "vector",
                lambda v: v.tensor_tensor(s_compacc[:, :], s_compacc[:, :],
                                          s_sc2[:, 3:4], ALU.add),
                after=[evsp, ev_cp0], label=f"cpa{j}")

            evrt = p.chain("vector", [
                lambda v, om=om_: v.tensor_scalar(
                    s_sc2[:, 4:5], om[:, 5:6], float(CAP), None, ALU.is_lt),
                lambda v, om=om_: v.tensor_scalar(
                    s_oh2[:, :], s_io8[:, :], om[:, 4:5], None, ALU.is_equal),
                lambda v: v.tensor_scalar(
                    s_oh2[:, :], s_oh2[:, :], s_sc2[:, 4:5], None, ALU.mult),
            ], after=[evm_, eva2_prev], label=f"rt{j}")

            def rmm2(t):
                t.matmul(p_rank2[:, :], s_or[0:1, :], s_acc2[0:1, :],
                         start=True, stop=False)
                return t.matmul(p_rank2[:, :], s_L[:, :], s_oh2[:, :],
                                start=False, stop=True)
            evr2 = p.on("tensor", rmm2, after=[evrt, ev_snap2[j - 1]],
                        inc=("tensor", 1), label=f"rmm2{j}")

            def accmm2(t):
                return t.matmul(p_acc2p[:, :], s_oc[:, :], s_oh2[:, :],
                                start=True, stop=True)
            eva2 = p.on("tensor", accmm2, after=[evr2], inc=("tensor", 1),
                        label=f"acc2{j}")
            eva2_prev = eva2

            evsi = p.chain("vector", [
                lambda v: v.tensor_tensor(s_r82[:, :], p_rank2[:, :],
                                          s_oh2[:, :], ALU.mult),
                lambda v: v.reduce_sum(s_sc2[:, 5:6], s_r82[:, :], AX.X),
                lambda v: v.tensor_scalar(s_sc2[:, 6:7], s_sc2[:, 5:6],
                                          float(C_LOC), None, ALU.is_lt),
                lambda v: v.tensor_tensor(s_sc2[:, 7:8], s_sc2[:, 4:5],
                                          s_sc2[:, 6:7], ALU.mult),
                lambda v, j=j, om=om_: v.tensor_tensor(
                    s_wown[:, j:j + 1], om[:, 3:4], s_sc2[:, 6:7], ALU.mult),
                lambda v, om=om_: v.tensor_scalar(
                    s_sc2[:, 8:9], om[:, 4:5], float(C_LOC), None, ALU.mult),
                lambda v: v.tensor_tensor(s_sc2[:, 8:9], s_sc2[:, 8:9],
                                          s_sc2[:, 5:6], ALU.add),
                lambda v: v.tensor_scalar(s_sc2[:, 9:10], s_sc2[:, 7:8],
                                          -1.0, -1.0, ALU.add, ALU.mult),
                lambda v, j=j: v.tensor_tensor(
                    s_sc2[:, 10:11], s_sc2[:, 9:10], s_bigio[:, j:j + 1],
                    ALU.mult),
                lambda v: v.tensor_tensor(s_sc2[:, 11:12], s_sc2[:, 8:9],
                                          s_sc2[:, 10:11], ALU.add),
                lambda v, j=j: v.tensor_copy(s_sidx[:, j:j + 1],
                                             s_sc2[:, 11:12]),
                lambda v: v.tensor_tensor(s_sc2[:, 12:13], s_sc2[:, 5:6],
                                          s_sc2[:, 7:8], ALU.mult),
                lambda v, om=om_: v.tensor_scalar(
                    s_sc2[:, 13:14], om[:, 4:5], float(C_LOC), None, ALU.mult),
                lambda v: v.tensor_tensor(s_sc2[:, 13:14], s_sc2[:, 13:14],
                                          s_sc2[:, 12:13], ALU.add),
                lambda v, j=j: v.tensor_copy(s_gidx[:, j:j + 1],
                                             s_sc2[:, 13:14]),
            ], after=[evr2, evrt], label=f"si{j}")
            ev_si_by_j[j] = evsi
            ev_snap2[j] = p.on(
                "vector", lambda v: v.tensor_tensor(s_acc2[:, :], s_acc2[:, :],
                                                    p_acc2p[:, :], ALU.add),
                after=[eva2], label=f"snap2{j}")

            ev_scat[j] = p.on(
                "gpsimd",
                lambda g, j=j, b2=b2: g.indirect_dma_start(
                    out=disp[:, :],
                    out_offset=IndirectOffsetOnAxis(s_sidx[:, j:j + 1], 0),
                    in_=s_t2b[:, b2], in_offset=None,
                    bounds_check=E * C_LOC - 1, oob_is_err=False),
                after=[evsi, ev_t2, ev_dz_last],
                inc=("d_scat%d" % b2, 16), label=f"scat{j}")

        ev_scat_all = [ev_scat[OWN_TILES - 2], ev_scat[OWN_TILES - 1]]
        ev_xn_all = p.now("vector")
        if DEBUG_OUTS:
            p.dma("gpsimd", dbg_xn2[0], s_xnown.rearrange("p a b -> p (a b)"),
                  after=[ev_xn_all], label="dbgxn0", slot="dbgx")
            p.dma("gpsimd", dbg_idx[:, 0:OWN_TILES], s_sidx[:, :],
                  after=[ev_xn_all] + ev_scat_all, label="dbgsi", slot="dbgx")
            p.dma("gpsimd", dbg_idx[:, OWN_TILES:], s_gidx[:, :],
                  after=[ev_xn_all] + ev_scat_all, label="dbggi", slot="dbgx")
            p.dma("gpsimd", dbg_w[:, :], s_wown[:, :],
                  after=[ev_xn_all] + ev_scat_all, label="dbgw", slot="dbgx")
    pre3.close()

    with SC() as c4:
        s_dT = c4.enter_context(nc.sbuf_tensor("ff_dT", [128, 8, C_LOC], BF16))
        s_dtok = c4.enter_context(nc.sbuf_tensor("ff_dtok", [128, 4, D], BF16))
        s_wib = c4.enter_context(nc.sbuf_tensor("ff_wib", [128, 2, 8, 512], BF16))
        s_hT = c4.enter_context(nc.sbuf_tensor("ff_hT", [128, 32, C_LOC], BF16))
        s_wob = c4.enter_context(nc.sbuf_tensor("ff_wob", [128, 2, 32, 128], BF16))
        s_eoT = c4.enter_context(nc.sbuf_tensor("ff_eoT", [128, 8, 512], F32))
        s_eot = c4.enter_context(nc.sbuf_tensor("ff_eot", [128, 2, D], F32))
        p_h = c4.enter_context(nc.psum_tensor("p_h", [128, 2, 512], F32))
        p_oT = c4.enter_context(nc.psum_tensor("p_oT", [128, 2, 512], F32))
        p_tr = c4.enter_context(nc.psum_tensor("p_tr", [128, 128], F32))
        p_trb = c4.enter_context(nc.psum_tensor("p_trb", [128, 128], BF16))

        fence4 = p.fence()
        eo_last = None
        dtok_guard = None
        dT_guard = None
        eoT_guard = None
        wi_guard = {}
        wo_guard = {}
        gel_ring = []
        oev_ring = []
        trb_evac = None
        tr_evac = None
        eot_dma = {}
        for e in range(E):
            dt_evs = []
            for st4 in range(4):
                evd = p.dma("sync", s_dtok[:, st4],
                            disp[e * C_LOC + st4 * 128:
                                 e * C_LOC + (st4 + 1) * 128, :],
                            after=ev_scat_all + [dtok_guard] + fence4,
                            label=f"dld{e}{st4}", slot=f"dld{st4}")
                for ch in range(8):
                    evt = p.on("tensor",
                               lambda t, st4=st4, ch=ch: t.transpose(
                                   p_trb[:, :],
                                   s_dtok[:, st4, ch * 128:(ch + 1) * 128],
                                   s_idb[:, :]),
                               after=[evd, trb_evac, dT_guard],
                               inc=("tensor", 1), label=f"dtr{e}{st4}{ch}")
                    eve = p.on("vector",
                               lambda v, st4=st4, ch=ch: v.tensor_copy(
                                   s_dT[:, ch, st4 * 128:(st4 + 1) * 128],
                                   p_trb[:, :]),
                               after=[evt], label=f"dtre{e}{st4}{ch}")
                    trb_evac = eve
                    dt_evs.append(eve)
                    dtok_guard = evt
            dT_done = dt_evs[-1]

            gel_evs = []
            for hg in range(8):
                bw = hg % 2
                evw = p.dma("gpsimd", s_wib[:, bw], wit[e, hg],
                            after=[wi_guard.get(bw)] + fence4,
                            label=f"wi{e}{hg}", slot=f"wi{bw}")
                for sub in range(4):
                    hc = hg * 4 + sub
                    bh = hc % 2

                    def hmm(t, bw=bw, sub=sub, bh=bh):
                        for ch in range(8):
                            mm = t.matmul(
                                p_h[:, bh],
                                s_wib[:, bw, ch, sub * 128:(sub + 1) * 128],
                                s_dT[:, ch, :],
                                start=(ch == 0), stop=(ch == 7))
                        return mm
                    evh = p.on("tensor", hmm,
                               after=[evw, dT_done,
                                      gel_ring[-2] if len(gel_ring) >= 2 else None],
                               inc=("tensor", 1), label=f"h{e}{hc}")
                    wi_guard[bw] = evh
                    dT_guard = evh
                    evg = p.on("scalar",
                               lambda s, hc=hc, bh=bh: s.activation(
                                   out=s_hT[:, hc], in_=p_h[:, bh],
                                   func=AF.Gelu),
                               after=[evh], label=f"g{e}{hc}")
                    gel_evs.append(evg)
                    gel_ring.append(evg)
            h_done = gel_evs[-1]

            oc_evs = []
            for oc in range(8):
                bw = oc % 2
                bh = oc % 2
                evw = p.dma("gpsimd", s_wob[:, bw], wot[e, oc],
                            after=[wo_guard.get(bw)] + fence4,
                            label=f"wo{e}{oc}", slot=f"wo{bw}")

                def omm(t, bw=bw, bh=bh):
                    for hc in range(32):
                        mm = t.matmul(p_oT[:, bh], s_wob[:, bw, hc],
                                      s_hT[:, hc, :],
                                      start=(hc == 0), stop=(hc == 31))
                    return mm
                evo = p.on("tensor", omm,
                           after=[evw, h_done,
                                  oev_ring[-2] if len(oev_ring) >= 2 else None],
                           inc=("tensor", 1), label=f"o{e}{oc}")
                wo_guard[bw] = evo
                evoe = p.on("vector",
                            lambda v, oc=oc, bh=bh: v.tensor_copy(
                                s_eoT[:, oc], p_oT[:, bh]),
                            after=[evo, eoT_guard], label=f"oev{e}{oc}")
                oc_evs.append(evoe)
                oev_ring.append(evoe)
            oT_done = oc_evs[-1]

            for st4 in range(4):
                b2 = st4 % 2
                tr_last = None
                for ch in range(8):
                    evt = p.on("tensor",
                               lambda t, st4=st4, ch=ch: t.transpose(
                                   p_tr[:, :],
                                   s_eoT[:, ch, st4 * 128:(st4 + 1) * 128],
                                   s_id[:, :]),
                               after=[oT_done, tr_evac],
                               inc=("tensor", 1), label=f"etr{e}{st4}{ch}")
                    eoT_guard = evt
                    eve = p.on("vector",
                               lambda v, b2=b2, ch=ch: v.tensor_copy(
                                   s_eot[:, b2, ch * 128:(ch + 1) * 128],
                                   p_tr[:, :]),
                               after=[evt, eot_dma.get(b2)],
                               label=f"etre{e}{st4}{ch}")
                    tr_evac = eve
                    tr_last = eve
                evdma = p.dma("gpsimd",
                              eo[e * C_LOC + st4 * 128:
                                 e * C_LOC + (st4 + 1) * 128, :],
                              s_eot[:, b2], after=[tr_last],
                              label=f"eody{e}{st4}", slot=f"eo{b2}")
                eot_dma[b2] = evdma
        ev_eo_all = [eot_dma[0], eot_dma[1]]

    st = dict(last_tb_evac=None, last_pj_evac=None, last_rot_evac=None,
              last_pair_pe=None, last_exp=None, prev_cb=None, ev_y_last=None,
              o_evac=None)
    s_wo2g = c23.enter_context(nc.sbuf_tensor("f_wo2g", [128, 8, D], BF16))
    ev_wo2g = p.dma("sync", s_wo2g[:], wato_bf,
                    after=[ev_pc_qkv[3]] + p.fence(), label="wo2g",
                    slot="awo2")
    prev_cbs = {}
    SHUF = []
    for _i in range(0, 32, 2):
        SHUF += [_i + 1, _i]
    prev_y = {}
    prev_xn2 = {}

    def emit_seq(sq):
        fenceA = p.fence()
        with SC() as cs:
            s_catT = cs.enter_context(nc.sbuf_tensor(f"catT{sq}", [128, 8, S],
                                                     BF16))
            with SC() as c3:
                s_xnT = c3.enter_context(nc.sbuf_tensor(f"at_xnT{sq}", [128, 8, S],
                                                        BF16))
                s_wqk = c3.enter_context(nc.sbuf_tensor(f"at_wqk{sq}",
                                                        [128, 2, 8, 128], BF16))
                s_wvp = c3.enter_context(nc.sbuf_tensor(f"at_wv{sq}", [128, 8, 128],
                                                        BF16))
                s_qT = c3.enter_context(nc.sbuf_tensor(f"at_qT{sq}", [128, S], BF16))
                s_kT = c3.enter_context(nc.sbuf_tensor(f"at_kT{sq}", [128, S], BF16))
                s_vh = c3.enter_context(nc.sbuf_tensor(f"at_vh{sq}", [128, 16, 130],
                                                       BF16))
                s_qe = c3.enter_context(nc.sbuf_tensor(f"at_qe{sq}", [128, 2, 512], BF16))
                s_t1 = c3.enter_context(nc.sbuf_tensor(f"at_t1{sq}", [128, 2, 512], BF16))
                s_PT = c3.enter_context(nc.sbuf_tensor(f"at_PT{sq}", [128, 16, 512],
                                                       BF16))
                p_pj = c3.enter_context(nc.psum_tensor(f"p_pj{sq}",
                                                        [128, 2, 512], F32))
                p_ST = c3.enter_context(nc.psum_tensor(f"p_ST{sq}",
                                                       [128, 2, 2, 512], F32))
                p_O = c3.enter_context(nc.psum_tensor(f"p_O{sq}",
                                                      [128, 4, 65], F32))
                p_tb = c3.enter_context(nc.psum_tensor(f"p_tb{sq}",
                                                       [128, 8, 128], BF16))
                s_Osl = c3.enter_context(nc.sbuf_tensor(f"at_Osl{sq}",
                                                        [128, 16, 65], F32))
                s_Ons = c3.enter_context(nc.sbuf_tensor(f"at_Ons{sq}",
                                                        [128, 16, 64], BF16))
                s_rcl16 = c3.enter_context(nc.sbuf_tensor(f"at_rcl{sq}",
                                                          [128, 16], F32))

                prev_evac = st['last_tb_evac']
                for t16 in range(16):
                    jj = sq * 16 + t16

                    def xtr(t, jj=jj):
                        for ch in range(8):
                            mm = t.transpose(p_tb[:, ch, :],
                                             s_xnown[:, jj, ch * 128:(ch + 1) * 128],
                                             s_idb[:, :])
                        return mm
                    evt = p.on("tensor", xtr,
                               after=[ev_xn_all, prev_evac] + fenceA,
                               inc=("tensor", 1), label=f"xnT{sq}_{t16}")
                    prev_evac = p.on(
                        "vector",
                        lambda v, t16=t16: v.tensor_copy(
                            s_xnT[:, 0:8, t16 * 128:(t16 + 1) * 128],
                            p_tb[:, :, :]),
                        after=[evt], label=f"xnTe{sq}_{t16}")
                xnT_ready = prev_evac
                st['last_tb_evac'] = prev_evac

                for pr in range(8):
                    evwq = p.dma("sync", s_wqk[:, 0], wqkv_bf[0, pr],
                                 after=[st['last_pair_pe'], ev_pc_qkv[0]]
                                 + fenceA,
                                 label=f"wq{sq}_{pr}", slot="awq")
                    evwk = p.dma("sync", s_wqk[:, 1], wqkv_bf[1, pr],
                                 after=[st['last_pair_pe'], ev_pc_qkv[1]]
                                 + fenceA,
                                 label=f"wk{sq}_{pr}", slot="awk")
                    evwv = p.dma("sync", s_wvp[:], wqkv_bf[2, pr],
                                 after=[st['last_pair_pe'], ev_pc_qkv[2]]
                                 + fenceA,
                                 label=f"wv{sq}_{pr}", slot="awv")

                    rope_done = None
                    unit_dve = {}
                    uidx = 0
                    for qk in range(2):
                        dstT = s_qT if qk == 0 else s_kT
                        for gq in range(4):
                            ub = uidx % 2

                            def pj(t, qk=qk, gq=gq, ub=ub):
                                for ch in range(8):
                                    mm = t.matmul(
                                        p_pj[:, ub], s_wqk[:, qk, ch],
                                        s_xnT[:, ch, gq * 512:(gq + 1) * 512],
                                        start=(ch == 0), stop=(ch == 7))
                                return mm
                            vr = st.get('v_ring')
                            evpj = p.on("tensor", pj,
                                        after=[xnT_ready, evwq, evwk,
                                               unit_dve.get(uidx - 2),
                                               vr[uidx] if (vr and uidx < 2)
                                               else None],
                                        inc=("tensor", 1),
                                        label=f"pj{sq}{pr}{qk}{gq}")
                            evac = p.on("vector",
                                        lambda v, ub=ub: v.tensor_copy(
                                            s_qe[:, ub], p_pj[:, ub]),
                                        after=[evpj],
                                        label=f"pjev{sq}{pr}{qk}{gq}")
                            evch = p.chain("vector", [
                                lambda v, ub=ub: v.stream_shuffle(
                                    s_t1[:, ub], s_qe[:, ub], SHUF),
                                lambda v, ub=ub, gq=gq: v.scalar_tensor_tensor(
                                    s_t1[:, ub], s_t1[:, ub], s_sgn[:, 0:1],
                                    s_sin[:, gq * 512:(gq + 1) * 512],
                                    ALU.mult, ALU.mult),
                                lambda v, ub=ub, gq=gq, dstT=dstT:
                                v.tensor_tensor(
                                    dstT[:, gq * 512:(gq + 1) * 512],
                                    s_qe[:, ub],
                                    s_cos[:, gq * 512:(gq + 1) * 512], ALU.mult),
                                lambda v, ub=ub, gq=gq, dstT=dstT:
                                v.tensor_tensor(
                                    dstT[:, gq * 512:(gq + 1) * 512],
                                    s_t1[:, ub],
                                    dstT[:, gq * 512:(gq + 1) * 512], ALU.add),
                            ], after=[st['last_pair_pe']],
                                label=f"rope{sq}{pr}{qk}{gq}")
                            unit_dve[uidx] = evac
                            rope_done = evch
                            uidx += 1
                    qk_done = rope_done

                    evvz = p.on("vector",
                                lambda v: v.memset(s_vh[:, :, 64:65], 1.0),
                                after=[st['last_pair_pe']], label="vones")
                    evvz2 = p.on("vector",
                                 lambda v: v.memset(s_vh[:, :, 129:130], 1.0),
                                 after=[st['last_pair_pe']], label="vones2")
                    v_last = None
                    v_prev = {}
                    for kc in range(16):
                        vb = kc % 2

                        def vpj(t, kc=kc, vb=vb):
                            for ch in range(8):
                                mm = t.matmul(
                                    p_pj[:, vb, 0:128],
                                    s_xnT[:, ch, kc * 128:(kc + 1) * 128],
                                    s_wvp[:, ch],
                                    start=(ch == 0), stop=(ch == 7))
                            return mm
                        evv = p.on("tensor", vpj,
                                   after=[xnT_ready, evwv, v_prev.get(kc - 2),
                                          unit_dve.get(6 + kc) if kc < 2
                                          else None],
                                   inc=("tensor", 1), label=f"v{sq}{pr}{kc}")
                        evve = p.chain("vector", [
                            lambda v, kc=kc, vb=vb: v.tensor_copy(
                                s_vh[:, kc, 0:64], p_pj[:, vb, 0:64]),
                            lambda v, kc=kc, vb=vb: v.tensor_copy(
                                s_vh[:, kc, 65:129], p_pj[:, vb, 64:128]),
                        ], after=[evv, evvz, evvz2], label="vev")
                        v_prev[kc] = evve
                        v_last = evve
                    v_done = v_last
                    st['v_ring'] = (v_prev[14], v_prev[15])

                    pair_pe_events = []
                    gring = []
                    gb = 0
                    pend = None
                    for hh in range(2):
                        pb = hh * 64
                        vsl = (0, 65) if hh == 0 else (65, 130)
                        for gq in range(4):
                            nch = 4 * gq + 4
                            bstart = 0
                            if pend is not None:
                                evst0, sb0, pvg = pend
                                evex0 = p.on(
                                    "scalar",
                                    lambda s, sb0=sb0: s.activation(
                                        out=s_PT[:, 0:2]
                                        .rearrange("p a b -> p (a b)"),
                                        in_=p_ST[:, sb0, 0:2]
                                        .rearrange("p a b -> p (a b)"),
                                        func=AF.Exp, scale=0.125),
                                    after=[evst0, pvg],
                                    label=f"exP{sq}{pr}{hh}{gq}")
                                gring.append(evex0)
                                pend = None
                                bstart = 2
                            for cb in range(bstart, nch, 2):
                                nb = min(2, nch - cb)
                                sb_ = gb % 2

                                def stm(t, cb=cb, nb=nb, gq=gq, pb=pb, sb_=sb_):
                                    for kk in range(nb):
                                        kc = cb + kk
                                        mm = t.matmul(
                                            p_ST[:, sb_, kk],
                                            s_kT[pb:pb + 64,
                                                 kc * 128:(kc + 1) * 128],
                                            s_qT[pb:pb + 64,
                                                 gq * 512:(gq + 1) * 512],
                                            start=True, stop=True)
                                    return mm
                                evst = p.on("tensor", stm,
                                            after=[qk_done, v_done,
                                                   gring[gb - 2] if gb >= 2
                                                   else st['last_exp']],
                                            inc=("tensor", 1),
                                            label=f"stm{sq}{pr}{hh}{gq}{cb}")
                                gb += 1
                                evex = p.on("scalar",
                                            lambda s, cb=cb, nb=nb, sb_=sb_:
                                            s.activation(
                                                out=s_PT[:, cb:cb + nb]
                                                .rearrange("p a b -> p (a b)"),
                                                in_=p_ST[:, sb_, 0:nb]
                                                .rearrange("p a b -> p (a b)"),
                                                func=AF.Exp, scale=0.125),
                                            after=[evst], label=f"ex{cb}")
                                gring.append(evex)
                            mask_evs = []
                            for jd in range(4):
                                kc = 4 * gq + jd
                                evmk = p.on("vector",
                                            lambda v, kc=kc, jd=jd:
                                            v.tensor_tensor(
                                                s_PT[:, kc,
                                                     jd * 128:(jd + 1) * 128],
                                                s_PT[:, kc,
                                                     jd * 128:(jd + 1) * 128],
                                                s_tri[:, :], ALU.mult),
                                            after=[gring[-1]],
                                            label=f"mk{kc}")
                                mask_evs.append(evmk)
                            for qt in range(4):
                                nk = 4 * gq + qt + 1

                                def pv(t, nk=nk, qt=qt, vsl=vsl):
                                    for kc in range(nk):
                                        mm = t.matmul(
                                            p_O[:, qt],
                                            s_PT[:, kc,
                                                 qt * 128:(qt + 1) * 128],
                                            s_vh[:, kc, vsl[0]:vsl[1]],
                                            start=(kc == 0),
                                            stop=(kc == nk - 1))
                                    return mm
                                evpv = p.on("tensor", pv,
                                            after=mask_evs + [st['o_evac']],
                                            inc=("tensor", 1),
                                            label=f"pv{sq}{pr}{hh}{gq}{qt}")
                                pair_pe_events.append(evpv)
                            st['o_evac'] = p.on(
                                "vector",
                                lambda v, gq=gq: v.tensor_copy(
                                    s_Osl[:, gq * 4:(gq + 1) * 4, :],
                                    p_O[:, :, :]),
                                after=[evpv], label=f"oev{sq}{pr}{hh}{gq}")
                            if not (hh == 1 and gq == 3):
                                hh2, gq2 = (hh, gq + 1) if gq < 3 \
                                    else (hh + 1, 0)
                                pb2 = hh2 * 64
                                sb0 = gb % 2

                                def stm0(t, gq2=gq2, pb2=pb2, sb0=sb0):
                                    for kk in range(2):
                                        mm = t.matmul(
                                            p_ST[:, sb0, kk],
                                            s_kT[pb2:pb2 + 64,
                                                 kk * 128:(kk + 1) * 128],
                                            s_qT[pb2:pb2 + 64,
                                                 gq2 * 512:(gq2 + 1) * 512],
                                            start=True, stop=True)
                                    return mm
                                evst0 = p.on("tensor", stm0,
                                             after=[qk_done, v_done,
                                                    gring[gb - 2] if gb >= 2
                                                    else st['last_exp']],
                                             inc=("tensor", 1),
                                             label=f"stmP{sq}{pr}{hh}{gq}")
                                gb += 1
                                pend = (evst0, sb0, evpv)
                        st['last_exp'] = gring[-1]
                        evn = p.chain("vector", [
                            lambda v: v.reciprocal(s_rcl16[:, :],
                                                   s_Osl[:, :, 64:65]
                                                   .rearrange("p a b -> p (a b)")),
                            lambda v: v.tensor_tensor(
                                s_Ons[:, :, :], s_Osl[:, :, 0:64],
                                s_rcl16[:, :].rearrange("p (a b) -> p a b", b=1)
                                .broadcast_to([128, 16, 64]), ALU.mult),
                        ], after=[st['o_evac']], label=f"onrm{sq}{pr}{hh}")
                        for grp in range(2):
                            def otr(t, grp=grp):
                                for k8 in range(8):
                                    mm = t.transpose(
                                        p_tb[0:64, k8, :],
                                        s_Ons[:, grp * 8 + k8, :],
                                        s_idb[:, :])
                                return mm
                            evot = p.on("tensor", otr,
                                        after=[evn, st['last_tb_evac']],
                                        inc=("tensor", 1),
                                        label=f"otr{sq}{pr}{hh}{grp}")
                            st['last_tb_evac'] = p.on(
                                "vector",
                                lambda v, pb=pb, pr=pr, grp=grp: v.tensor_copy(
                                    s_catT[pb:pb + 64, pr,
                                           grp * 1024:(grp + 1) * 1024],
                                    p_tb[0:64, :, :]
                                    .rearrange("p a b -> p (a b)")),
                                after=[evot], label=f"oevac{sq}{pr}{hh}{grp}")
                    st['last_pair_pe'] = pair_pe_events[-1]
                ev_attn_done = st['last_tb_evac']
                if DEBUG_OUTS:
                    p.dma("gpsimd", dbg_cat[sq],
                          s_catT.rearrange("p a b -> p (a b)"),
                          after=[ev_attn_done], label=f"dbgcat{sq}",
                          slot="dbgcat")

            fence5 = p.fence()
            with SC() as c5:
                s_fx = c5.enter_context(nc.sbuf_tensor(f"f_x{sq}", [128, 2, D], F32))
                s_eog = c5.enter_context(nc.sbuf_tensor(f"f_eog{sq}", [128, 2, D],
                                                        F32))
                s_xn2 = c5.enter_context(nc.sbuf_tensor(f"f_xn{sq}", [128, D], F32))
                s_yt = c5.enter_context(nc.sbuf_tensor(f"f_y{sq}", [128, 2, D], F32))
                p_w1 = c5.enter_context(nc.psum_tensor(f"p_w1{sq}",
                                                       [128, 2, 2, 512], F32))
                for t16 in range(16):
                    j = sq * 16 + t16
                    b2 = j % 2
                    evfx = p.on("gpsimd",
                                lambda g, j=j, b2=b2: g.indirect_dma_start(
                                    out=s_fx[:, b2], out_offset=None,
                                    in_=xf[:, :],
                                    in_offset=IndirectOffsetOnAxis(
                                        s_ownrow[:, j:j + 1], 0)),
                                after=[prev_y.get(j - 2),
                                       prev_xn2.get(j - 1)] + fence5,
                                inc=("d_fx%d" % b2, 16), label=f"fx{j}")
                    evge = p.on("gpsimd",
                                lambda g, j=j, b2=b2: g.indirect_dma_start(
                                    out=s_eog[:, b2], out_offset=None,
                                    in_=eo[:, :],
                                    in_offset=IndirectOffsetOnAxis(
                                        s_gidx[:, j:j + 1], 0)),
                                after=ev_eo_all + [prev_y.get(j - 2)] + fence5,
                                inc=("d_ge%d" % b2, 16), label=f"ge{j}")
                    wo_evs = []
                    for half in range(2):
                        def wmm(t, t16=t16, half=half, b2=b2):
                            for ch in range(8):
                                mm = t.matmul(
                                    p_w1[:, b2, half],
                                    s_catT[:, ch, t16 * 128:(t16 + 1) * 128],
                                    s_wo2g[:, ch, half * 512:(half + 1) * 512],
                                    start=(ch == 0), stop=(ch == 7))
                            return mm
                        evw = p.on("tensor", wmm,
                                   after=[ev_attn_done, ev_wo2g,
                                          prev_cbs.get(j - 2)],
                                   inc=("tensor", 1), label=f"wom{j}_{half}")
                        wo_evs.append(evw)
                    evxn = p.on("vector",
                                lambda v, j=j, b2=b2: v.tensor_scalar(
                                    s_xn2[:, :], s_fx[:, b2],
                                    s_nm1r1[:, j:j + 1],
                                    s_r1own[:, j:j + 1],
                                    ALU.add, ALU.mult),
                                after=[evfx, st['prev_cb']], label=f"fxn{j}")
                    prev_xn2[j] = evxn

                    evcb = p.chain("vector", [
                        lambda v, j=j, b2=b2: v.tensor_scalar(
                            s_eog[:, b2], s_eog[:, b2], s_wown[:, j:j + 1],
                            None, ALU.mult),
                        lambda v, b2=b2: v.tensor_tensor(
                            s_yt[:, b2], s_fx[:, b2], s_xn2[:, :], ALU.add),
                        lambda v, b2=b2: v.tensor_tensor(
                            s_yt[:, b2], s_yt[:, b2], s_eog[:, b2], ALU.add),
                        lambda v, b2=b2: v.tensor_tensor(
                            s_yt[:, b2, 0:512], s_yt[:, b2, 0:512],
                            p_w1[:, b2, 0], ALU.add),
                        lambda v, b2=b2: v.tensor_tensor(
                            s_yt[:, b2, 512:1024], s_yt[:, b2, 512:1024],
                            p_w1[:, b2, 1], ALU.add),
                    ], after=[evxn, evge] + wo_evs, label=f"cb{j}")
                    st['prev_cb'] = evcb
                    prev_cbs[j] = evcb
                    evy = p.dma("gpsimd", y[j * 128:(j + 1) * 128, :],
                                s_yt[:, b2], after=[evcb], label=f"y{j}",
                                slot=f"y{b2}")
                    prev_y[j] = evy
                    st['ev_y_last'] = evy

    for sq in ([1] if SEQ1_ONLY else range(2)):
        emit_seq(sq)
    c23.close()

    p.on("gpsimd", lambda g: g.engine_nop(),
         after=[prev_y[OWN_TILES - 2], prev_y[OWN_TILES - 1]], inc=None,
         label="fin")

    p.emit(nc, es)
    return nc


_NC_CACHE = {}


def _get_nc():
    if "nc" not in _NC_CACHE:
        _NC_CACHE["nc"] = build_nc()
    return _NC_CACHE["nc"]


def kernel(L0_attn_x, L0_attn_Wq, L0_attn_Wk, L0_attn_Wv, L0_attn_Wo,
           L0_ffn_gate_weight, L0_ffn_ln_gamma, L0_ffn_ln_beta,
           L0_ffn_wi, L0_ffn_wo, L0_ffn_conf_w1, L0_ffn_conf_b1,
           L0_ffn_conf_w2, L0_ffn_conf_b2, L0_ffn_comp_w, L0_ffn_comp_b):
    x = np.ascontiguousarray(np.asarray(L0_attn_x, dtype=np.float32))
    xf = x.reshape(NT, D)
    xT = np.ascontiguousarray(
        xf.T.reshape(8, 128, 64, 512).transpose(2, 1, 0, 3))

    c = {}
    c["ident"] = np.eye(128, dtype=np.float32)
    c["identb"] = np.eye(128, dtype=np.float32).astype(ml_dtypes.bfloat16)
    c["lstrictT"] = np.triu(np.ones((128, 128), np.float32), 1)
    c["onesrow"] = np.ones((1, 128), np.float32)
    c["onescol"] = np.ones((128, 1), np.float32)
    c["iota8"] = np.tile(np.arange(8, dtype=np.float32), (128, 1))
    c["trimask"] = np.triu(np.ones((128, 128), np.float32), 0).astype(
        ml_dtypes.bfloat16)
    inv = 1.0 / ROPE_THETA ** (np.arange(0, DK, 2, dtype=np.float64) / DK)
    ang = np.outer(np.arange(S, dtype=np.float64), inv)
    cos = np.repeat(np.cos(ang).astype(np.float32), 2, axis=1).T
    sin = np.repeat(np.sin(ang).astype(np.float32), 2, axis=1).T
    c["ropecos"] = np.ascontiguousarray(np.vstack([cos, cos]), dtype=np.float32)
    c["ropesin"] = np.ascontiguousarray(np.vstack([sin, sin]), dtype=np.float32)
    R = np.zeros((64, 64), np.float32)
    for i in range(32):
        R[2 * i, 2 * i + 1] = -1.0
        R[2 * i + 1, 2 * i] = 1.0
    blk = np.zeros((128, 128), np.float32)
    blk[0:64, 0:64] = R.T
    blk[64:128, 64:128] = R.T
    c["ropePT"] = blk.astype(ml_dtypes.bfloat16)
    sg = np.empty((128, 1), np.float32)
    sg[0::2, 0] = -1.0
    sg[1::2, 0] = 1.0
    c["signcol"] = sg
    big = np.empty((128, OWN_TILES), np.float32)
    for j in range(OWN_TILES):
        big[:, j] = 2.0e8 + j * 128 + np.arange(128)
    c["bigiota"] = big

    def til(w):
        return np.asarray(w, np.float32).reshape(8, 128, 8, 128)\
            .transpose(2, 1, 0, 3)
    base = {
        "wqkvt": np.ascontiguousarray(np.stack(
            [til(L0_attn_Wq), til(L0_attn_Wk), til(L0_attn_Wv)])),
        "watot": np.ascontiguousarray(
            np.asarray(L0_attn_Wo, np.float32).reshape(8, 128, D)
            .transpose(1, 0, 2)),
        "gT": np.ascontiguousarray(np.asarray(L0_ffn_gate_weight, np.float32).T),
        "gamma_bc": np.ascontiguousarray(
            np.tile(np.asarray(L0_ffn_ln_gamma, np.float32)[None, :], (128, 1))),
        "beta_bc": np.ascontiguousarray(
            np.tile(np.asarray(L0_ffn_ln_beta, np.float32)[None, :], (128, 1))),
        "gammacol": np.ascontiguousarray(
            np.asarray(L0_ffn_ln_gamma, np.float32).reshape(8, 128).T),
        "betacol": np.ascontiguousarray(
            np.asarray(L0_ffn_ln_beta, np.float32).reshape(8, 128).T),
        "wit": np.ascontiguousarray(
            np.asarray(L0_ffn_wi, np.float32).reshape(E, 8, 128, 8, 512)
            .transpose(0, 3, 2, 1, 4)),
        "wot": np.ascontiguousarray(
            np.asarray(L0_ffn_wo, np.float32).reshape(E, 32, 128, 8, 128)
            .transpose(0, 3, 2, 1, 4)),
        "cw1T": np.ascontiguousarray(np.asarray(L0_ffn_conf_w1, np.float32).T),
        "cb1": np.ascontiguousarray(
            np.asarray(L0_ffn_conf_b1, np.float32).reshape(4, 128).T),
        "cw2T": np.ascontiguousarray(
            np.asarray(L0_ffn_conf_w2, np.float32).reshape(4, 128).T),
        "cb2r": np.full((128, 1),
                        float(np.asarray(L0_ffn_conf_b2).reshape(-1)[0]),
                        np.float32),
        "compwT": np.ascontiguousarray(
            np.asarray(L0_ffn_comp_w, np.float32).reshape(8, 128).T),
        "compbr": np.full((128, 1),
                          float(np.asarray(L0_ffn_comp_b).reshape(-1)[0]),
                          np.float32),
        "xf": xf, "xT": xT,
    }
    base.update(c)

    in_maps = []
    for core in range(NCORES):
        m = dict(base)
        rows = np.empty((128, OWN_TILES), np.uint32)
        for j in range(OWN_TILES):
            sq, t16 = divmod(j, 16)
            bb = core + 8 * sq
            rows[:, j] = bb * S + t16 * 128 + np.arange(128, dtype=np.uint32)
        m["ownrow_u"] = rows
        in_maps.append(m)

    nc = _get_nc()
    res = bass_utils.run_bass_kernel_spmd(nc, in_maps,
                                          core_ids=list(range(NCORES)))
    out = np.empty((B, S, D), np.float32)
    for core in range(NCORES):
        ycore = res.results[core]["y"]
        out[core] = ycore[0:S]
        out[core + 8] = ycore[S:2 * S]
    return out
